# version 19
# baseline (speedup 1.0000x reference)
import numpy as np
import ml_dtypes
from contextlib import ExitStack

import concourse.bass as bass
import concourse.mybir as mybir
from concourse.bass_utils import run_bass_kernel_spmd

F32 = mybir.dt.float32
BF16 = mybir.dt.bfloat16
AX = mybir.AxisListType.X
ALU = mybir.AluOpType
ACT = mybir.ActivationFunctionType

B, T, D = 4, 4096, 1024
NTOK = 2048
NSLOT = 4
EXT = [2, 4, 6, 8]
OWN_TGS = {0: [0, 3, 4, 7], 1: [1, 2, 5, 6]}
SHIFT = 20.0
NEXP, IEXP, ISH = 8, 512, 2048
EPS = 1e-5


def ts(i, n):
    return slice(i * n, (i + 1) * n)


class Sem:
    def __init__(self, stk, nc, name):
        self.h = stk.enter_context(nc.semaphore(name))
        self.n = 0


class Ctx:
    ENGINES = ("sync", "gpsimd", "tensor", "vector", "scalar")

    def __init__(self, nc):
        self.nc = nc
        self.ops = {e: [] for e in self.ENGINES}

    def emit(self, eng, fn):
        self.ops[eng].append(fn)

    def wait(self, eng, dep):
        if dep is None:
            return
        sem, val = dep
        if val <= 0:
            return
        assert val <= sem.n, f"wait {val} > emitted {sem.n} on {eng}"
        self.ops[eng].append(lambda e, s=sem.h, v=val: e.wait_ge(s, v))

    def flush(self):
        with self.nc.Block() as blk:
            for name in self.ENGINES:
                lst = self.ops[name]
                if lst:
                    getattr(blk, name)(
                        lambda e, _l=tuple(lst): [f(e) for f in _l])
        self.ops = {e: [] for e in self.ENGINES}


class Slot:
    def __init__(self, res, sem=None):
        self.res = res
        self.sem = sem
        self.free = None
        self.ready = None


class Pool:
    def __init__(self, resources):
        self.slots = [Slot(r) for r in resources]
        self.i = 0

    def next(self):
        s = self.slots[self.i % len(self.slots)]
        self.i += 1
        return s


def build_program():
    nc = bass.Bass("TRN2", target_bir_lowering=False, debug=False,
                   detect_race_conditions=False)
    dri = lambda name, shape, dt=F32: nc.dram_tensor(
        name, shape, dt, kind="ExternalInput").ap()
    x_full = dri("x_full", [T, D])
    x_own = dri("x_own", [NTOK, D])
    wqkv_t = dri("wqkv_t", [D, 3 * D])
    wout_t = dri("wout_t", [D, D])
    wmod = dri("wmod", [D, 6 * D])
    c_fm_in = dri("c_fm_in", [128, 8])
    rb = dri("rb", [1, 1])
    rw = dri("rw", [128, 64])
    gwt = dri("gwt", [NEXP, D, IEXP])
    uwt = dri("uwt", [NEXP, D, IEXP])
    dwt = dri("dwt", [NEXP, IEXP, D])
    sgw = dri("sgw", [D, ISH])
    suw = dri("suw", [D, ISH])
    sdw = dri("sdw", [ISH, D])
    masks = dri("masks", [NSLOT, 128, 4096], BF16)
    y = nc.dram_tensor("y", [NTOK, D], F32, kind="ExternalOutput").ap()
    mod_dram = nc.dram_tensor("mod_scratch", [1, 4 * D], F32, kind="Internal").ap()

    root = ExitStack()
    rsb = lambda n, s, d=F32: root.enter_context(nc.sbuf_tensor(n, s, d))

    K_bf = [rsb(f"K{p}", [128, T], BF16) for p in range(4)]
    V_bf = rsb("Vb", [128, 32 * 520], BF16)
    attnout = [rsb(f"ao{d}", [128, NTOK], BF16) for d in range(8)]
    ident = rsb("ident", [128, 128], BF16)
    ones_bf = rsb("ones", [128, 512], BF16)
    brow = rsb("brow", [1, 3 * D], BF16)
    cbp1_tm = rsb("cbp1", [128, D])
    sbm_tm = rsb("sbm", [128, D], BF16)
    cap1_fm = rsb("cap1", [128, 8])
    ga_fm = rsb("ga", [128, 8])
    gb_fm = rsb("gb", [128, 8])
    neg20 = rsb("neg20", [128, 1])
    epsc = rsb("epsc", [128, 1])
    rbcol = rsb("rbcol", [128, 1])
    wsel_bf = rsb("wsel", [128, 128], BF16)
    sa_bf = rsb("sa_bf", [128, 8], BF16)
    x_res = [K_bf[d][:].bitcast(F32) for d in range(4)] + \
            [V_bf[:].bitcast(F32)[:, ts(i, NTOK)] for i in range(4)]
    h2_fm = attnout

    ctx = Ctx(nc)
    sG = Sem(root, nc, "sG")
    sT = Sem(root, nc, "sT")
    sV = Sem(root, nc, "sV")
    sS = Sem(root, nc, "sS")
    s_gp0 = Sem(root, nc, "s_gp0")
    s_wqkv = Sem(root, nc, "s_wqkv")
    s_wsl = Sem(root, nc, "s_wsl")
    s_sgu = Sem(root, nc, "s_sgu")
    wm_sems = [Sem(root, nc, f"s_wm{i}") for i in range(4)]
    xa_sems = [Sem(root, nc, f"s_xa{i}") for i in range(4)]
    s_msk = Sem(root, nc, "s_msk")
    gu_sems = [Sem(root, nc, f"s_gu{i}") for i in range(2)]
    dw_sems = [Sem(root, nc, f"s_dwe{i}") for i in range(2)]
    dc_sems = [Sem(root, nc, f"s_dc{i}") for i in range(2)]
    sd_sems = [Sem(root, nc, f"s_sd{i}") for i in range(2)]
    ys_sems = [Sem(root, nc, f"s_ys{i}") for i in range(2)]

    def dma(eng, sem, out, in_, slow=False):
        def f(e, o=out, i=in_, s=sem.h, sl=slow):
            kw = {"allow_slow_non_contiguous": True} if sl else {}
            e.dma_start(o, i, **kw).then_inc(s, 16)
        ctx.emit(eng, f)
        sem.n += 16
        return (sem, sem.n)

    def op(eng, sem, fn):
        def f(e, fn=fn, s=sem.h):
            fn(e).then_inc(s, 1)
        ctx.emit(eng, f)
        sem.n += 1
        return (sem, sem.n)

    def opq(eng, fn):
        ctx.emit(eng, fn)

    W = ctx.wait

    def _recl(e, out_ap, in_ap):
        with nc.allow_low_precision(reason="softmax denom recip to bf16"):
            return e.reciprocal(out_ap, in_ap)

    def barrier():
        snap = [(sT, sT.n), (sV, sV.n), (sS, sS.n)]
        for eng in Ctx.ENGINES:
            for dep in snap:
                W(eng, dep)

    b0 = ExitStack()
    bsb = lambda n, s, d=F32: b0.enter_context(nc.sbuf_tensor(n, s, d))
    bps = lambda n, s: b0.enter_context(nc.psum_tensor(n, s, F32))
    io_col = bsb("io_col", [128, 128])
    io_par = bsb("io_par", [128, 1])
    c_fm = bsb("c_fm", [128, 8])
    sil = bsb("sil", [128, 8])
    sil_hi = bsb("sil_hi", [128, 8], BF16)
    sil_lo = bsb("sil_lo", [128, 8], BF16)
    rep_hi = [bsb(f"rph{d}", [128, 128], BF16) for d in range(8)]
    rep_lo = [bsb(f"rpl{d}", [128, 128], BF16) for d in range(8)]
    wm_st = [bsb(f"wm{i}", [128, 512]) for i in range(4)]
    wm_hl = [bsb(f"wmh{i}", [128, 1024], BF16) for i in range(2)]
    modrow = bsb("modrow", [1, 4 * D])
    sga = bsb("sga", [128, 8])
    sgb = bsb("sgb", [128, 8])
    aba = bsb("aba", [128, 8])
    abb = bsb("abb", [128, 8])
    rb_sb = bsb("rb_sb", [1, 1], BF16)
    m32 = bsb("m32", [32, 128])
    m32h = bsb("m32h", [32, 128], BF16)
    m32l = bsb("m32l", [32, 128], BF16)
    mfm = bsb("mfm", [128, 32])
    mfm_ps = b0.enter_context(nc.psum_tensor("pmfm", [128, 32], F32))
    mod_ps = Pool([bps(f"p0_{i}", [128, 512]) for i in range(2)])
    rbc_ps = bps("prbc", [128, 512])

    d_iota = op("gpsimd", sG, lambda e: e.iota(
        io_col[:], [[1, 128]], channel_multiplier=0,
        allow_small_or_imprecise_dtypes=True))
    d_iota = op("gpsimd", sG, lambda e: e.iota(
        io_par[:], [[0, 1]], channel_multiplier=1,
        allow_small_or_imprecise_dtypes=True))
    W("vector", d_iota)
    r_ident = op("vector", sV, lambda e: e.tensor_scalar(
        ident[:], io_col[:], io_par[:, 0:1], None, ALU.is_equal))
    op("vector", sV, lambda e: e.memset(ones_bf[:], 1.0))
    op("vector", sV, lambda e: e.memset(epsc[:], EPS))
    r_ones = op("vector", sV, lambda e: e.memset(neg20[:], -SHIFT))

    dma("gpsimd", s_gp0, c_fm[:, 0:8], c_fm_in[:, :])
    d_crb = dma("gpsimd", s_gp0, rb_sb[0:1, 0:1], rb[0:1, 0:1])
    W("scalar", d_crb)
    r_sil = op("scalar", sS, lambda e: e.activation(sil[:], c_fm[:], ACT.Silu))
    W("vector", r_sil)
    a = op("vector", sV, lambda e: e.tensor_copy(sil_hi[:], sil[:]))
    W("vector", a)
    for d in range(8):
        r_rep = op("vector", sV, lambda e, d=d: e.tensor_copy(
            rep_hi[d][:], sil_hi[:, d:d + 1].to_broadcast([128, 128])))

    st_slots = Pool(wm_st)
    for i, sl in enumerate(st_slots.slots):
        sl.sem = wm_sems[i]
        sl.q = "sync" if i < 2 else "gpsimd"
    hl_slots = Pool(wm_hl)
    r_mod = {}
    for g in range(12):
        mp = mod_ps.next()
        W("tensor", mp.free)
        for d in range(8):
            k = g * 8 + d
            st = st_slots.next()
            W(st.q, st.free)
            st.ready = dma(st.q, st.sem, st.res[:], wmod[ts(d, 128), ts(g, 512)])
            hl = hl_slots.next()
            W("vector", st.ready)
            W("vector", hl.free)
            hl.ready = op("vector", sV, lambda e, a=hl.res, b=st.res:
                          e.tensor_copy(a[:, 0:512], b[:]))
            st.free = hl.ready
            W("tensor", hl.ready)
            if k == 0:
                W("tensor", r_rep)
            hl.free = op("tensor", sT, lambda e, d=d, a=hl.res, mp=mp.res,
                         st_=(d == 0), sp=(d == 7):
                         e.matmul(mp[:], rep_hi[d][:], a[:, 0:512],
                                  start=st_, stop=sp))
        mp.ready = hl.free
        m, hh = g // 2, g % 2
        W("vector", mp.ready)
        if m == 0:
            ev = op("vector", sV, lambda e, p=mp.res, hh=hh:
                    e.tensor_copy(modrow[0:1, ts(hh, 512)], p[0:1, :]))
        elif m == 1:
            ev = op("vector", sV, lambda e, p=mp.res, hh=hh: e.tensor_copy(
                modrow[0:1, 1024 + hh * 512:1024 + hh * 512 + 512], p[0:1, :]))
        elif m == 2:
            ev = op("vector", sV, lambda e, p=mp.res, hh=hh: e.tensor_copy(
                modrow[0:1, 2048 + hh * 512:2048 + hh * 512 + 512], p[0:1, :]))
        elif m == 3:
            ev = op("vector", sV, lambda e, p=mp.res, hh=hh:
                    e.tensor_copy(sbm_tm[:, ts(hh, 512)], p[:]))
        elif m == 4:
            ev = op("vector", sV, lambda e, p=mp.res, hh=hh: e.tensor_scalar(
                cbp1_tm[:, ts(hh, 512)], p[:], 1.0, None, ALU.add))
        else:
            ev = op("vector", sV, lambda e, p=mp.res, hh=hh: e.tensor_copy(
                modrow[0:1, 3072 + hh * 512:3072 + hh * 512 + 512], p[0:1, :]))
        mp.free = ev
        r_mod[m * 2 + hh] = ev

    W("gpsimd", r_mod[11])
    d_bounce = dma("gpsimd", s_gp0, mod_dram[0:1, :], modrow[0:1, :])
    W("gpsimd", d_bounce)
    d_m32 = dma("gpsimd", s_gp0, m32[:, :],
                bass.AP(mod_dram.tensor, 0, [[128, 32], [1, 128]]))
    W("vector", d_m32)
    r_m32 = op("vector", sV, lambda e: e.tensor_copy(m32h[:], m32[:]))
    W("tensor", r_m32)
    W("tensor", r_ident)
    t_mfm = op("tensor", sT, lambda e: e.matmul(mfm_ps[:], m32h[:],
                                                ident[0:32, 0:32],
                                                start=True, stop=True))
    W("vector", t_mfm)
    r_mfm = op("vector", sV, lambda e: e.tensor_copy(mfm[:], mfm_ps[0:128, 0:32]))
    W("vector", r_mfm)
    r_cap1 = op("vector", sV, lambda e: e.tensor_scalar(
        cap1_fm[:], mfm[:, 8:16], 1.0, None, ALU.add))
    r_sabf = op("vector", sV, lambda e: e.tensor_copy(sa_bf[:], mfm[:, 0:8]))
    op("vector", sV, lambda e: e.tensor_copy(ga_fm[:], mfm[:, 16:24]))
    r_gcp = op("vector", sV, lambda e: e.tensor_copy(gb_fm[:], mfm[:, 24:32]))
    W("scalar", r_gcp)
    op("scalar", sS, lambda e: e.activation(sga[:], ga_fm[:], ACT.Sign))
    op("scalar", sS, lambda e: e.activation(sgb[:], gb_fm[:], ACT.Sign))
    op("scalar", sS, lambda e: e.activation(aba[:], ga_fm[:], ACT.Abs))
    a_gab = op("scalar", sS, lambda e: e.activation(abb[:], gb_fm[:], ACT.Abs))
    W("vector", a_gab)
    op("vector", sV, lambda e: e.tensor_scalar(
        aba[:], aba[:], 0.25, None, ALU.max))
    a = op("vector", sV, lambda e: e.tensor_scalar(
        abb[:], abb[:], 0.25, None, ALU.max))
    W("vector", a)
    op("vector", sV, lambda e: e.tensor_mul(ga_fm[:], sga[:], aba[:]))
    r_gab = op("vector", sV, lambda e: e.tensor_mul(gb_fm[:], sgb[:], abb[:]))

    W("tensor", d_crb)
    W("tensor", r_ones)
    t_rbc = op("tensor", sT, lambda e: e.matmul(
        rbc_ps[0:128, 0:1], ones_bf[0:1, 0:128], rb_sb[0:1, 0:1],
        start=True, stop=True))
    W("vector", t_rbc)
    op("vector", sV, lambda e: e.tensor_copy(rbcol[:], rbc_ps[0:128, 0:1]))

    ctx.flush()
    b0.close()
    barrier()

    sa_stk = ExitStack()
    asb = lambda n, s, d=F32: sa_stk.enter_context(nc.sbuf_tensor(n, s, d))
    Q_bf = [asb(f"Q{p}", [128, NTOK], BF16) for p in range(4)]
    wsl = [asb(f"wsl{d}", [128, 1536], BF16) for d in range(8)]
    xa = [asb(f"xa{i}", [128, D]) for i in range(4)]
    xnb = [asb(f"xn{i}", [128, D], BF16) for i in range(2)]
    xfm = [[asb(f"xf{s}_{d}", [128, 512], BF16) for d in range(8)]
           for s in range(2)]
    st8 = asb("st8", [128, 32])
    P_sb = [asb(f"P{i}", [128, 1024], BF16) for i in range(4)]
    msk_sb = asb("msk", [128, 4096], BF16)
    avb_st = [asb(f"avst{i}", [64, 512], BF16) for i in range(2)]
    den_all = asb("dst", [97, 512])
    rec2b = asb("rec2b", [65, 512], BF16)

    xofm = [Q_bf[p][:].bitcast(F32) for p in range(4)]

    def ln_group(tiles):
        n = len(tiles)
        for k, (xt, xnt, st, dep, nfree) in enumerate(tiles):
            W("vector", dep)
            op("vector", sV, lambda e, xt=xt, st=st:
               e.reduce_sum(st[:, 0:1], xt[:], AX))
        a = None
        for k, (xt, xnt, st, dep, nfree) in enumerate(tiles):
            W("scalar", dep)
            W("scalar", nfree)
            a = op("scalar", sS, lambda e, xt=xt, xnt=xnt, st=st: e.activation(
                xnt[:], xt[:], ACT.Square, accum_out=st[:, 1:2]))
        W("vector", a)
        for k, (xt, xnt, st, dep, nfree) in enumerate(tiles):
            op("vector", sV, lambda e, st=st: e.tensor_scalar(
                st[:, 5:6], st[:, 1:2], 1.0 / D, EPS, ALU.mult, ALU.add))
        a = None
        for k, (xt, xnt, st, dep, nfree) in enumerate(tiles):
            a = op("vector", sV, lambda e, st=st: e.tensor_scalar(
                st[:, 2:3], st[:, 0:1], 1.0 / D, None, ALU.mult))
        W("scalar", a)
        for k, (xt, xnt, st, dep, nfree) in enumerate(tiles):
            a = op("scalar", sS, lambda e, st=st: e.activation(
                st[:, 3:4], st[:, 2:3], ACT.Square))
        W("vector", a)
        for k, (xt, xnt, st, dep, nfree) in enumerate(tiles):
            a = op("vector", sV, lambda e, st=st: e.scalar_tensor_tensor(
                st[:, 4:5], st[:, 3:4], -1.0, st[:, 5:6], ALU.mult, ALU.add))
        W("scalar", a)
        for k, (xt, xnt, st, dep, nfree) in enumerate(tiles):
            a = op("scalar", sS, lambda e, st=st: e.activation(
                st[:, 6:7], st[:, 4:5], ACT.Sqrt))
        W("vector", a)
        for k, (xt, xnt, st, dep, nfree) in enumerate(tiles):
            a = op("vector", sV, lambda e, st=st: e.reciprocal(
                st[:, 7:8], st[:, 6:7]))
        if n < 4:
            W("vector", a)
        fin = [None] * n
        for k, (xt, xnt, st, dep, nfree) in enumerate(tiles):
            W("vector", nfree)
            fin[k] = op("vector", sV, lambda e, xt=xt, xnt=xnt, st=st:
                        e.scalar_tensor_tensor(
                xnt[:], xt[:], st[:, 2:3],
                st[:, 7:8].to_broadcast([128, D]), ALU.subtract, ALU.mult))
        return fin

    def ln_tile(xt, xnt, stats, dep_x):
        return ln_group([(xt, xnt, stats, dep_x, None)])[0]

    xa_p = Pool(xa)
    for i, sl in enumerate(xa_p.slots):
        sl.sem = xa_sems[i]
    xn_p = Pool(xnb + [msk_sb[:, ts(i, 1024)] for i in range(4)])
    for hg in range(2):
        barrier()
        pa = ExitStack()
        tr_ps = Pool([pa.enter_context(nc.psum_tensor(f"tr{hg}_{i}", [128, 128], F32))
                      for i in range(2)])
        kv_ps = Pool([pa.enter_context(nc.psum_tensor(f"kv{hg}_{i}", [128, 512], F32))
                      for i in range(4)])

        d_w = None
        for d in range(8):
            dma("gpsimd", s_wsl, wsl[d][:, 0:512],
                wqkv_t[ts(d, 128), hg * 512:hg * 512 + 512])
            dma("gpsimd", s_wsl, wsl[d][:, 512:1024],
                wqkv_t[ts(d, 128), 1024 + hg * 512:1024 + hg * 512 + 512])
            d_w = dma("gpsimd", s_wsl, wsl[d][:, 1024:1536],
                      wqkv_t[ts(d, 128), 2048 + hg * 512:2048 + hg * 512 + 512])
        br_ps = Pool([pa.enter_context(nc.psum_tensor(f"br{hg}_{i}", [1, 512],
                                                      F32)) for i in range(2)])
        W("tensor", d_w)
        W("tensor", r_sabf)
        r_brow = None
        t_brow = None
        for part in range(3):
            bp = br_ps.next()
            W("tensor", bp.free)
            for d in range(8):
                t_brow = op("tensor", sT,
                            lambda e, d=d, part=part, bp=bp.res:
                            e.matmul(bp[:], sa_bf[:, d:d + 1],
                                     wsl[d][:, ts(part, 512)],
                                     start=(d == 0), stop=(d == 7)))
            bp.ready = t_brow
            W("vector", bp.ready)
            dstb = [hg * 512, 1024 + hg * 512, 2048 + hg * 512][part]
            if part == 0:
                ev = op("vector", sV, lambda e, bp=bp.res, dstb=dstb:
                        e.tensor_scalar(brow[0:1, dstb:dstb + 512],
                                        bp[0:1, :], 0.125, None, ALU.mult))
            else:
                ev = op("vector", sV, lambda e, bp=bp.res, dstb=dstb:
                        e.tensor_copy(brow[0:1, dstb:dstb + 512], bp[0:1, :]))
            bp.free = ev
            r_brow = ev
        W("vector", t_brow)
        W("vector", d_w)
        W("vector", r_cap1)
        for d in range(8):
            op("vector", sV, lambda e, d=d: e.tensor_mul(
                wsl[d][:], wsl[d][:],
                cap1_fm[:, d:d + 1].to_broadcast([128, 1536])))
            r_wsl = op("vector", sV, lambda e, d=d: e.tensor_scalar(
                wsl[d][:, 0:512], wsl[d][:, 0:512], 0.125, None, ALU.mult))
        for kc in range(32):
            r_vones = op("vector", sV, lambda e, kc=kc: e.memset(
                bass.AP(V_bf, 520 * kc + 64, [[32 * 520, 128], [65, 8], [1, 1]]),
                1.0))

        xfm_ready = [None, None]
        xfm_free = [None, None]
        for grp in range(12):
            qpass = grp >= 8
            s2 = grp % 2
            group, xss, nss = [], [], []
            for i in range(4):
                it = grp * 4 + i
                src = (x_own[ts(it - 32, 128), :] if qpass
                       else x_full[ts(it, 128), :])
                xs = xa_p.next()
                W("sync", xs.free)
                xs.ready = dma("sync", xs.sem, xs.res[:], src)
                ns = xn_p.next()
                xss.append(xs)
                nss.append(ns)
                group.append((xs.res, ns.res, st8[:, 8 * i:8 * i + 8],
                              xs.ready, ns.free))
            fins = ln_group(group)
            for i in range(4):
                xss[i].free = fins[i]
            if grp == 0:
                W("tensor", r_wsl)
                W("tensor", r_vones)
                W("tensor", r_brow)
                W("tensor", r_ident)
            W("tensor", xfm_free[s2])
            for i in range(4):
                ns = nss[i]
                W("tensor", fins[i])
                for d in range(8):
                    tp = tr_ps.next()
                    W("tensor", tp.free)
                    tp.ready = op("tensor", sT,
                                  lambda e, tp=tp.res, x=ns.res, d=d:
                                  e.matmul(tp[:], x[:, ts(d, 128)], ident[:],
                                           start=True, stop=True))
                    W("scalar", tp.ready)
                    tp.free = op("scalar", sS,
                                 lambda e, tp=tp.res, d=d, s2=s2, i=i:
                                 e.copy(xfm[s2][d][:, ts(i, 128)], tp[:]))
                    r_evac = tp.free
                ns.free = r_evac
            xfm_ready[s2] = r_evac
            W("tensor", xfm_ready[s2])
            r_cons = None
            if not qpass:
                tg = grp
                for p in range(4):
                    kp = kv_ps.next()
                    W("tensor", kp.free)
                    for d in range(8):
                        opq("tensor", lambda e, kp=kp.res, d=d, p=p, s2=s2:
                            e.matmul(kp[:], wsl[d][:, 512 + p * 128:512 + p * 128 + 128],
                                     xfm[s2][d][:], start=(d == 0), stop=False))
                    kp.ready = op("tensor", sT, lambda e, kp=kp.res, p=p, hg=hg:
                                  e.matmul(kp[:],
                                           brow[0:1, 1024 + hg * 512 + p * 128:
                                                1024 + hg * 512 + p * 128 + 128],
                                           ones_bf[0:1, :], start=False, stop=True))
                    W("vector", kp.ready)
                    kp.free = op("vector", sV, lambda e, kp=kp.res, p=p, tg=tg:
                                 e.tensor_copy(K_bf[p][:, ts(tg, 512)], kp[:]))
                    r_cons = kp.free
                for kk in range(4):
                    kc = tg * 4 + kk
                    vp = kv_ps.next()
                    W("tensor", vp.free)
                    for d in range(8):
                        opq("tensor", lambda e, vp=vp.res, d=d, kk=kk, s2=s2:
                            e.matmul(vp[:], xfm[s2][d][:, ts(kk, 128)],
                                     wsl[d][:, 1024:1536], start=(d == 0), stop=False))
                    vp.ready = op("tensor", sT, lambda e, vp=vp.res, hg=hg:
                                  e.matmul(vp[:], ones_bf[0:1, 0:128],
                                           brow[0:1, 2048 + hg * 512:
                                                2048 + hg * 512 + 512],
                                           start=False, stop=True))
                    W("vector", vp.ready)
                    vp.free = op("vector", sV, lambda e, vp=vp.res, kc=kc:
                                 e.tensor_copy(
                        bass.AP(V_bf, 520 * kc, [[32 * 520, 128], [65, 8], [1, 64]]),
                        vp[:]))
                    r_cons = vp.free
            else:
                jo = grp - 8
                for p in range(4):
                    qp = kv_ps.next()
                    W("tensor", qp.free)
                    for d in range(8):
                        opq("tensor", lambda e, qp=qp.res, d=d, p=p, s2=s2:
                            e.matmul(qp[:], wsl[d][:, p * 128:p * 128 + 128],
                                     xfm[s2][d][:], start=(d == 0), stop=False))
                    qp.ready = op("tensor", sT, lambda e, qp=qp.res, p=p, hg=hg:
                                  e.matmul(qp[:],
                                           brow[0:1, hg * 512 + p * 128:
                                                hg * 512 + p * 128 + 128],
                                           ones_bf[0:1, :], start=False, stop=True))
                    W("vector", qp.ready)
                    qp.free = op("vector", sV, lambda e, qp=qp.res, p=p, jo=jo:
                                 e.tensor_copy(Q_bf[p][:, ts(jo, 512)], qp[:]))
                    r_cons = qp.free
            xfm_free[s2] = r_cons
        ctx.flush()
        pa.close()
        barrier()

        pa2 = ExitStack()
        sc_ps = Pool([pa2.enter_context(nc.psum_tensor(f"sc{hg}_{i}", [128, 1024], F32))
                      for i in range(2)])
        av_ps = Pool([pa2.enter_context(nc.psum_tensor(f"av{hg}_{i}", [65, 512], F32))
                      for i in range(2)])
        bc_ps = Slot(pa2.enter_context(nc.psum_tensor(f"bcp{hg}", [64, 512], F32)))
        P_p = Pool(P_sb)
        msk_slot = Slot(msk_sb)

        for j in range(NSLOT):
            W("sync", msk_slot.free)
            msk_slot.ready = dma("sync", s_msk, msk_sb[:], masks[j, :, :])
            r_mask_last = None
            npair = 2 * EXT[j]
            for hp in range(4):
                hls = (2 * hp, 2 * hp + 1)
                avs, scs, pbs = {}, {}, {}
                for hl in hls:
                    avs[hl] = av_ps.next()
                for pc in range(npair):
                    kc0, kc1 = 2 * pc, 2 * pc + 1
                    for hl in hls:
                        p, par = hl // 2, hl % 2
                        pr = slice(64 * par, 64 * par + 64)
                        sc = sc_ps.next()
                        scs[hl] = sc
                        W("tensor", sc.free)
                        opq("tensor",
                            lambda e, sc=sc.res, p=p, pr=pr, kc0=kc0, j=j:
                            e.matmul(sc[:, 0:512], K_bf[p][pr, ts(kc0, 128)],
                                     Q_bf[p][pr, ts(j, 512)],
                                     start=True, stop=True))
                        sc.ready = op("tensor", sT,
                                      lambda e, sc=sc.res, p=p, pr=pr,
                                      kc1=kc1, j=j:
                                      e.matmul(sc[:, 512:1024],
                                               K_bf[p][pr, ts(kc1, 128)],
                                               Q_bf[p][pr, ts(j, 512)],
                                               start=True, stop=True))
                    for hl in hls:
                        sc = scs[hl]
                        pb = P_p.next()
                        pbs[hl] = pb
                        W("scalar", sc.ready)
                        W("scalar", pb.free)
                        pb.ready = op("scalar", sS,
                                      lambda e, sc=sc.res, pb=pb.res:
                                      e.activation(pb[:], sc[:], ACT.Exp,
                                                   bias=neg20[:, 0:1]))
                        sc.free = pb.ready
                    masked = pc >= npair - 4
                    if masked:
                        mc = pc - (npair - 4)
                        for hl in hls:
                            pb = pbs[hl]
                            W("vector", pb.ready)
                            if hp == 0 and hl == hls[0] and mc == 0:
                                W("vector", msk_slot.ready)
                            pb.ready = op("vector", sV,
                                          lambda e, pb=pb.res, mc=mc:
                                          e.tensor_mul(pb[:], pb[:],
                                                       msk_sb[:, ts(mc, 1024)]))
                            r_mask_last = pb.ready
                    for hl in hls:
                        av, pb = avs[hl], pbs[hl]
                        if pc == 0:
                            W("tensor", av.free)
                        W("tensor", pb.ready)
                        opq("tensor",
                            lambda e, av=av.res, hl=hl, kc0=kc0, pb=pb.res,
                            st_=(pc == 0):
                            e.matmul(av[:], bass.AP(V_bf, 520 * kc0 + 65 * hl,
                                                    [[32 * 520, 128], [1, 65]]),
                                     pb[:, 0:512], start=st_, stop=False))
                        pb.free = op("tensor", sT,
                                     lambda e, av=av.res, hl=hl, kc1=kc1,
                                     pb=pb.res, sp=(pc == npair - 1):
                                     e.matmul(av[:],
                                              bass.AP(V_bf, 520 * kc1 + 65 * hl,
                                                      [[32 * 520, 128], [1, 65]]),
                                              pb[:, 512:1024],
                                              start=False, stop=sp))
                        av.ready = pb.free
                avb = [avb_st[0], avb_st[1]]
                dbase = [32 * (hp % 2), 32 * (hp % 2) + 64]
                for q, hl in enumerate(hls):
                    av = avs[hl]
                    W("vector", av.ready)
                    op("vector", sV, lambda e, av=av.res, b=avb[q]:
                       e.tensor_copy(b[:], av[0:64, :]))
                    r_st = op("vector", sV,
                              lambda e, av=av.res, b0=dbase[q]:
                              e.tensor_copy(den_all[b0:b0 + 1, :],
                                            av[64:65, :]))
                    av.free = r_st
                op("vector", sV, lambda e, b0=dbase[0]: _recl(
                    e, rec2b[0:1, :], den_all[b0:b0 + 1, :]))
                r_rec = op("vector", sV, lambda e, b0=dbase[1]: _recl(
                    e, rec2b[64:65, :], den_all[b0:b0 + 1, :]))
                for q, hl in enumerate(hls):
                    W("tensor", r_rec)
                    W("tensor", bc_ps.free)
                    bc_ps.ready = op("tensor", sT, lambda e, q=q: e.matmul(
                        bc_ps.res[:], ones_bf[64 * q:64 * q + 1, 0:64],
                        rec2b[64 * q:64 * q + 1, :],
                        start=True, stop=True))
                    W("vector", bc_ps.ready)
                    hd = (hg * 8 + hl) // 2
                    dst = slice(64 * ((hg * 8 + hl) % 2),
                                64 * ((hg * 8 + hl) % 2) + 64)
                    bc_ps.free = op("vector", sV,
                                    lambda e, b=avb[q], hd=hd, dst=dst, j=j:
                                    e.tensor_mul(attnout[hd][dst, ts(j, 512)],
                                                 b[:], bc_ps.res[:]))
            msk_slot.free = r_mask_last
        ctx.flush()
        pa2.close()

    pb5 = ExitStack()
    tr5 = Pool([pb5.enter_context(nc.psum_tensor(f"t5{i}", [128, 128], F32))
                for i in range(2)])
    pr5 = Pool([pb5.enter_context(nc.psum_tensor(f"p5{i}", [128, 512], F32))
                for i in range(2)])
    xhl5_p = Pool([msk_sb[:, 0:2 * D], msk_sb[:, 2 * D:4 * D]])
    wout_bf = [wsl[d][:, 0:1024] for d in range(8)]
    barrier()
    d_wo = None
    for d in range(8):
        d_wo = dma("gpsimd", s_wsl, wout_bf[d], wout_t[ts(d, 128), :])

    for jo in range(NSLOT):
        r_xo = None
        for i in range(4):
            it = jo * 4 + i
            xs = xa_p.next()
            W("sync", xs.free)
            xs.ready = dma("sync", xs.sem, xs.res[:], x_own[ts(it, 128), :])
            W("vector", xs.ready)
            hb = xhl5_p.next()
            W("vector", hb.free)
            op("vector", sV, lambda e, x=xs.res, h=hb.res:
               e.tensor_copy(h[:, 0:D], x[:]))
            r_hl = op("vector", sV, lambda e, x=xs.res, h=hb.res:
                      e.tensor_sub(h[:, D:2 * D], x[:], h[:, 0:D]))
            xs.free = r_hl
            W("tensor", r_hl)
            if it == 0:
                W("tensor", d_wo)
            for d in range(8):
                tp = tr5.next()
                W("tensor", tp.free)
                opq("tensor", lambda e, tp=tp.res, d=d, h=hb.res: e.matmul(
                    tp[:], h[:, ts(d, 128)], ident[:], start=True, stop=False))
                tp.ready = op("tensor", sT, lambda e, tp=tp.res, d=d, h=hb.res:
                              e.matmul(tp[:], h[:, D + d * 128:D + d * 128 + 128],
                                       ident[:], start=False, stop=True))
                hb.free = tp.ready
                W("scalar", tp.ready)
                tp.free = op("scalar", sS, lambda e, tp=tp.res, d=d, i=i: e.copy(
                    xofm[d // 2][:, (d % 2) * 512 + i * 128:
                                 (d % 2) * 512 + i * 128 + 128], tp[:]))
                r_xo = tp.free
        W("tensor", r_xo)
        for c in range(8):
            pp = pr5.next()
            W("tensor", pp.free)
            for d in range(8):
                pp.ready = op("tensor", sT, lambda e, pp=pp.res, c=c, d=d, jo=jo:
                              e.matmul(pp[:], wout_bf[d][:, ts(c, 128)],
                                       attnout[d][:, ts(jo, 512)],
                                       start=(d == 0), stop=(d == 7)))
            W("vector", pp.ready)
            pp.free = op("vector", sV, lambda e, pp=pp.res, c=c, jo=jo:
                         e.scalar_tensor_tensor(
                x_res[c][:, ts(jo, 512)], pp[:], ga_fm[:, c:c + 1],
                xofm[c // 2][:, (c % 2) * 512:(c % 2) * 512 + 512],
                ALU.mult, ALU.add))
    ctx.flush()
    pb5.close()
    sa_stk.close()

    b6 = ExitStack()
    b6sb = lambda n, s, d=F32: b6.enter_context(nc.sbuf_tensor(n, s, d))
    x2tm = [b6sb(f"x2_{i}", [128, D]) for i in range(2)]
    h2tm = [b6sb(f"h2_{i}", [128, D], BF16) for i in range(2)]
    hi6 = [b6sb(f"hi6_{p}", [128, 128], BF16) for p in range(2)]
    lo6 = [b6sb(f"lo6_{p}", [128, 128], BF16) for p in range(2)]
    st6 = b6sb("st6", [128, 16])
    rw_bf = b6sb("rw_bf", [128, 64], BF16)
    sm8 = b6sb("sm8", [128, 192])
    tr6 = Pool([b6.enter_context(nc.psum_tensor(f"t6{i}", [128, 128], F32))
                for i in range(2)])
    lg6 = Pool([b6.enter_context(nc.psum_tensor(f"l6{i}", [128, 8], F32))
                for i in range(4)])

    barrier()
    d_rw = dma("gpsimd", s_gp0, rw_bf[:, 0:64], rw[:, :])

    x2_p = Pool(x2tm)
    h2_p = Pool(h2tm)
    r_h2last = None
    for it in range(16):
        jo, i = divmod(it, 4)
        x2 = x2_p.next()
        h2s = h2_p.next()
        r_tx2 = [None, None]
        for dd in (0, 2, 4, 6):
            r_hl = [None, None]
            for p in (0, 1):
                d = dd + p
                W("vector", r_tx2[p])
                op("vector", sV, lambda e, d=d, jo=jo, i=i, p=p: e.tensor_copy(
                    hi6[p][:],
                    x_res[d][:, jo * 512 + i * 128:jo * 512 + i * 128 + 128]))
            for p in (0, 1):
                d = dd + p
                r_hl[p] = op("vector", sV, lambda e, d=d, jo=jo, i=i, p=p:
                             e.tensor_sub(
                    lo6[p][:],
                    x_res[d][:, jo * 512 + i * 128:jo * 512 + i * 128 + 128],
                    hi6[p][:]))
            for p in (0, 1):
                d = dd + p
                tp = tr6.next()
                W("tensor", r_hl[p])
                W("tensor", tp.free)
                W("tensor", x2.free)
                opq("tensor", lambda e, tp=tp.res, p=p: e.matmul(
                    tp[:], hi6[p][:], ident[:], start=True, stop=False))
                tp.ready = op("tensor", sT, lambda e, tp=tp.res, p=p: e.matmul(
                    tp[:], lo6[p][:], ident[:], start=False, stop=True))
                r_tx2[p] = tp.ready
                W("scalar", tp.ready)
                tp.free = op("scalar", sS, lambda e, tp=tp.res, d=d, x2=x2.res:
                             e.copy(x2[:, ts(d, 128)], tp[:]))
                r_x2 = tp.free
        x2.ready = r_x2
        W("vector", h2s.free)
        r_ln = ln_tile(x2.res, h2s.res, st6, x2.ready)
        op("vector", sV, lambda e, h=h2s.res: e.tensor_mul(h[:], h[:], cbp1_tm[:]))
        r_h2 = op("vector", sV, lambda e, h=h2s.res:
                  e.tensor_add(h[:], h[:], sbm_tm[:]))
        x2.free = r_h2
        W("tensor", r_h2)
        for d in range(8):
            tp = tr6.next()
            W("tensor", tp.free)
            tp.ready = op("tensor", sT, lambda e, tp=tp.res, d=d, h=h2s.res:
                          e.matmul(tp[:], h[:, ts(d, 128)], ident[:],
                                   start=True, stop=True))
            W("scalar", tp.ready)
            tp.free = op("scalar", sS, lambda e, tp=tp.res, d=d, it=it:
                         e.copy(h2_fm[d][:, ts(it, 128)], tp[:]))
            r_h2last = tp.free
        h2s.free = r_h2last

    W("tensor", r_h2last)
    W("tensor", d_rw)
    for tkp in range(4):
        tks = tuple(4 * tkp + q for q in range(4))
        sm = [sm8[:, ts(q, 48)] for q in range(4)]
        lgs = []
        for tk in tks:
            lg = lg6.next()
            W("tensor", lg.free)
            for d in range(8):
                lg.ready = op("tensor", sT, lambda e, lg=lg.res, d=d, tk=tk:
                              e.matmul(lg[:], h2_fm[d][:, ts(tk, 128)],
                                       rw_bf[:, ts(d, 8)],
                                       start=(d == 0), stop=(d == 7)))
            lgs.append(lg)
        for p, lg in enumerate(lgs):
            W("vector", lg.ready)
            a = op("vector", sV, lambda e, lg=lg.res, s=sm[p]:
                   e.reduce_max(s[:, 0:1], lg[:], AX, negate=True))
            W("scalar", a)
            a = op("scalar", sS, lambda e, lg=lg.res, s=sm[p]: e.activation(
                s[:, 8:16], lg[:], ACT.Exp, bias=s[:, 0:1]))
            lg.free = a
            W("vector", a)

        def pair(fn):
            r = None
            for p in range(4):
                r = op("vector", sV, lambda e, s=sm[p], p=p: fn(e, s, p))
            W("vector", r)
            return r
        pair(lambda e, s, p: e.reduce_sum(s[:, 1:2], s[:, 8:16], AX))
        pair(lambda e, s, p: e.reciprocal(s[:, 2:3], s[:, 1:2]))
        pair(lambda e, s, p: e.tensor_mul(
            s[:, 8:16], s[:, 8:16], s[:, 2:3].to_broadcast([128, 8])))
        pair(lambda e, s, p: e.tensor_add(
            s[:, 16:24], s[:, 8:16], rbcol[:, 0:1].to_broadcast([128, 8])))
        pair(lambda e, s, p: e.max(out=s[:, 24:32], in_=s[:, 16:24]))
        pair(lambda e, s, p: e.tensor_tensor(
            s[:, 32:40], s[:, 16:24],
            s[:, 27:28].to_broadcast([128, 8]), ALU.is_ge))
        pair(lambda e, s, p: e.tensor_mul(
            s[:, 32:40], s[:, 32:40], s[:, 8:16]))
        pair(lambda e, s, p: e.reduce_sum(s[:, 3:4], s[:, 32:40], AX))
        pair(lambda e, s, p: e.tensor_scalar(
            s[:, 3:4], s[:, 3:4], 1e-20, None, ALU.add))
        pair(lambda e, s, p: e.reciprocal(s[:, 4:5], s[:, 3:4]))
        pair(lambda e, s, p, _tks=tks: e.tensor_mul(
            wsel_bf[:, ts(_tks[p], 8)], s[:, 32:40],
            s[:, 4:5].to_broadcast([128, 8])))
    ctx.flush()
    b6.close()

    b7 = ExitStack()
    b7sb = lambda n, s, d=BF16: b7.enter_context(nc.sbuf_tensor(n, s, d))
    gu_bf = [b7sb(f"gu{i}", [128, 8192]) for i in range(2)]
    dw_bf = [b7sb(f"dwb{i}", [128, 4096]) for i in range(2)]
    act_e = [b7sb(f"acte{s}", [128, NTOK]) for s in range(4)]
    rep7 = b7sb("rep7", [128, 128])
    wsb7 = [b7sb(f"wsb{i}", [128, 512]) for i in range(2)]
    sg7 = [b7sb(f"sg7{i}", [128, 512]) for i in range(2)]
    gu_ps = Pool([b7.enter_context(nc.psum_tensor(f"g7{i}", [128, 512], F32))
                  for i in range(4)])
    wb_ps = Pool([b7.enter_context(nc.psum_tensor(f"w7{i}", [128, 512], F32))
                  for i in range(2)])
    d8_ps = Pool([b7.enter_context(nc.psum_tensor(f"d8{i}", [128, 512], F32))
                  for i in range(2)])
    gu_p = Pool(gu_bf)
    for i, sl in enumerate(gu_p.slots):
        sl.sem = gu_sems[i]
    dw_p = Pool(dw_bf)
    for i, sl in enumerate(dw_p.slots):
        sl.sem = dw_sems[i]
    wsb_p = Pool(wsb7)
    sg_p = Pool(sg7)
    act_slots = [Slot(a) for a in act_e]

    barrier()
    for e_i in range(NEXP):
        gu = gu_p.next()
        W("gpsimd", gu.free)
        for d in range(8):
            dma("gpsimd", gu.sem, gu.res[:, ts(d, 512)], gwt[e_i, ts(d, 128), :])
        for d in range(8):
            gu.ready = dma("gpsimd", gu.sem,
                           gu.res[:, 4096 + d * 512:4096 + d * 512 + 512],
                           uwt[e_i, ts(d, 128), :])
        dwe = dw_p.next()
        W("gpsimd", dwe.free)
        for s in range(4):
            dwe.ready = dma("gpsimd", dwe.sem, dwe.res[:, ts(s, 1024)],
                            dwt[e_i, ts(s, 128), :])
        r_gu_last = None
        for j in range(NSLOT):
            wj = wsb_p.next()
            wp = wb_ps.next()
            W("tensor", wp.free)
            r_rep = None
            for q4 in range(4):
                tk = j * 4 + q4
                W("vector", r_rep)
                r_rep0 = op("vector", sV, lambda e, tk=tk, e_i=e_i: e.tensor_copy(
                    rep7[:], wsel_bf[:, tk * 8 + e_i:tk * 8 + e_i + 1]
                    .to_broadcast([128, 128])))
                W("tensor", r_rep0)
                r_rep = op("tensor", sT, lambda e, wp=wp.res, q4=q4: e.matmul(
                    wp[:, ts(q4, 128)], rep7[:], ident[:], start=True, stop=True))
            wp.ready = r_rep
            W("vector", wp.ready)
            W("vector", wj.free)
            wj.ready = op("vector", sV, lambda e, wj=wj.res, wp=wp.res:
                          e.tensor_copy(wj[:], wp[:]))
            wp.free = wj.ready
            if j == 0:
                W("tensor", gu.ready)
            for s in range(4):
                pg = gu_ps.next()
                pu = gu_ps.next()
                ac = act_slots[s]
                W("tensor", pg.free)
                W("tensor", pu.free)
                for d in range(8):
                    opq("tensor", lambda e, pg=pg.res, gu=gu.res, d=d, s=s, j=j:
                        e.matmul(pg[:], gu[:, d * 512 + s * 128:d * 512 + s * 128 + 128],
                                 h2_fm[d][:, ts(j, 512)], start=(d == 0), stop=(d == 7)))
                for d in range(8):
                    r_gu = op("tensor", sT,
                              lambda e, pu=pu.res, gu=gu.res, d=d, s=s, j=j:
                              e.matmul(pu[:],
                                       gu[:, 4096 + d * 512 + s * 128:
                                          4096 + d * 512 + s * 128 + 128],
                                       h2_fm[d][:, ts(j, 512)],
                                       start=(d == 0), stop=(d == 7)))
                r_gu_last = r_gu
                sg = sg_p.next()
                W("scalar", r_gu)
                W("scalar", sg.free)
                sg.ready = op("scalar", sS, lambda e, pg=pg.res, sg=sg.res:
                              e.activation(sg[:], pg[:], ACT.Silu))
                pg.free = sg.ready
                W("vector", sg.ready)
                W("vector", wj.ready)
                W("vector", ac.free)
                op("vector", sV, lambda e, pu=pu.res, sg=sg.res, ac=ac.res, s=s, j=j:
                   e.tensor_mul(ac[:, ts(j, 512)], sg[:], pu[:]))
                r_act = op("vector", sV, lambda e, ac=ac.res, wj=wj.res, s=s, j=j:
                           e.tensor_mul(ac[:, ts(j, 512)], ac[:, ts(j, 512)], wj[:]))
                pu.free = r_act
                sg.free = r_act
                ac.ready = r_act
            wj.free = r_act
        gu.free = r_gu_last
        r_dn = None
        for j in range(NSLOT):
            pd = d8_ps.next()
            W("tensor", pd.free)
            if j == 0:
                W("tensor", dwe.ready)
            for s in range(4):
                W("tensor", act_slots[s].ready)
            for c in range(8):
                if c > 0:
                    pd = d8_ps.next()
                    W("tensor", pd.free)
                for s in range(4):
                    pd.ready = op("tensor", sT,
                                  lambda e, pd=pd.res, dw=dwe.res, s=s, c=c, j=j,
                                  st_=(s == 0), sp=(s == 3):
                                  e.matmul(pd[:], dw[:, s * 1024 + c * 128:
                                                    s * 1024 + c * 128 + 128],
                                           act_e[s][:, ts(j, 512)],
                                           start=st_, stop=sp))
                W("vector", pd.ready)
                pd.free = op("vector", sV, lambda e, pd=pd.res, c=c, j=j:
                             e.scalar_tensor_tensor(
                    x_res[c][:, ts(j, 512)], pd[:], gb_fm[:, c:c + 1],
                    x_res[c][:, ts(j, 512)], ALU.mult, ALU.add))
                r_dn = pd.free
        dwe.free = r_dn
        for s in range(4):
            act_slots[s].free = r_dn
    ctx.flush()
    b7.close()

    b9 = ExitStack()
    b9sb = lambda n, s, d=BF16: b9.enter_context(nc.sbuf_tensor(n, s, d))
    sgu_bf = [b9sb(f"sgu{d}", [128, 2048]) for d in range(8)]
    s_act = [b9sb(f"sac{i}", [128, NTOK]) for i in range(8)]
    sdw_bf = [b9sb(f"sdwb{i}", [128, 1024]) for i in range(2)]
    sg9 = [b9sb(f"sg9{i}", [128, 512]) for i in range(2)]
    gu9 = Pool([b9.enter_context(nc.psum_tensor(f"g9{i}", [128, 512], F32))
                for i in range(4)])
    d9_ps = Pool([b9.enter_context(nc.psum_tensor(f"d9{i}", [128, 512], F32))
                  for i in range(2)])
    sdw_p = Pool(sdw_bf)
    for i, sl in enumerate(sdw_p.slots):
        sl.sem = sd_sems[i]
    sg9_p = Pool(sg9)

    prev_g9 = None
    prev_dn9 = None
    barrier()
    for ih in range(2):
        d_sg = None
        if ih == 1:
            W("gpsimd", prev_g9)
        for d in range(8):
            dma("gpsimd", s_sgu, sgu_bf[d][:, 0:1024],
                sgw[ts(d, 128), ih * 1024:ih * 1024 + 1024])
            d_sg = dma("gpsimd", s_sgu, sgu_bf[d][:, 1024:2048],
                       suw[ts(d, 128), ih * 1024:ih * 1024 + 1024])
        if ih == 1:
            W("vector", prev_dn9)
        s_act_ready = [None] * 8
        r_act9 = None
        for ic in range(8):
            for j in range(NSLOT):
                pg = gu9.next()
                pu = gu9.next()
                W("tensor", pg.free)
                W("tensor", pu.free)
                if ic == 0 and j == 0:
                    W("tensor", d_sg)
                for d in range(8):
                    opq("tensor", lambda e, pg=pg.res, d=d, ic=ic, j=j:
                        e.matmul(pg[:], sgu_bf[d][:, ts(ic, 128)],
                                 h2_fm[d][:, ts(j, 512)], start=(d == 0), stop=(d == 7)))
                for d in range(8):
                    r_g9 = op("tensor", sT,
                              lambda e, pu=pu.res, d=d, ic=ic, j=j:
                              e.matmul(pu[:], sgu_bf[d][:, 1024 + ic * 128:
                                                        1024 + ic * 128 + 128],
                                       h2_fm[d][:, ts(j, 512)],
                                       start=(d == 0), stop=(d == 7)))
                sg = sg9_p.next()
                W("scalar", r_g9)
                W("scalar", sg.free)
                sg.ready = op("scalar", sS, lambda e, pg=pg.res, sg=sg.res:
                              e.activation(sg[:], pg[:], ACT.Silu))
                pg.free = sg.ready
                W("vector", sg.ready)
                r_a = op("vector", sV, lambda e, pu=pu.res, sg=sg.res, ic=ic, j=j:
                         e.tensor_mul(s_act[ic][:, ts(j, 512)], sg[:], pu[:]))
                pu.free = r_a
                sg.free = r_a
                r_act9 = r_a
            s_act_ready[ic] = r_act9
        r_dn9 = None
        for c in range(8):
            sd = sdw_p.next()
            W("gpsimd", sd.free)
            sd.ready = dma("gpsimd", sd.sem, bass.AP(sd.res, 0,
                                                 [[1024, 128], [128, 8], [1, 128]]),
                           bass.AP(sdw.tensor, ih * 1024 * 1024 + c * 128,
                                   [[1024, 128], [128 * 1024, 8], [1, 128]]))
            for j in range(NSLOT):
                pd = d9_ps.next()
                W("tensor", pd.free)
                if j == 0:
                    W("tensor", sd.ready)
                    for ic in range(8):
                        W("tensor", s_act_ready[ic])
                for ic in range(8):
                    pd.ready = op("tensor", sT,
                                  lambda e, pd=pd.res, sd=sd.res, ic=ic, j=j,
                                  st_=(ic == 0), sp=(ic == 7):
                                  e.matmul(pd[:], sd[:, ts(ic, 128)],
                                           s_act[ic][:, ts(j, 512)],
                                           start=st_, stop=sp))
                W("vector", pd.ready)
                pd.free = op("vector", sV, lambda e, pd=pd.res, c=c, j=j:
                             e.scalar_tensor_tensor(
                    x_res[c][:, ts(j, 512)], pd[:], gb_fm[:, c:c + 1],
                    x_res[c][:, ts(j, 512)], ALU.mult, ALU.add))
                r_dn9 = pd.free
            sd.free = r_dn9
        prev_g9 = r_g9
        prev_dn9 = r_dn9
    ctx.flush()
    b9.close()

    b10 = ExitStack()
    yhl = [b10.enter_context(nc.sbuf_tensor(f"yhl{i}", [128, 256], BF16))
           for i in range(2)]
    y_sb = [b10.enter_context(nc.sbuf_tensor(f"ysb{i}", [128, D], F32))
            for i in range(2)]
    trA = Pool([b10.enter_context(nc.psum_tensor(f"tA{i}", [128, 128], F32))
                for i in range(2)])
    yhl_p = Pool(yhl)
    ysb_slots = [Slot(t) for t in y_sb]
    for i, sl in enumerate(ysb_slots):
        sl.sem = ys_sems[i]
    barrier()
    for jo in range(NSLOT):
        for i in range(4):
            ysl = ysb_slots[i % 2]
            r_tile = None
            for dd in (0, 2, 4, 6):
                hbs = [yhl_p.next(), yhl_p.next()]
                for p in (0, 1):
                    d = dd + p
                    W("vector", hbs[p].free)
                    op("vector", sV, lambda e, hb=hbs[p].res, d=d, jo=jo, i=i:
                       e.tensor_copy(hb[:, 0:128],
                                     x_res[d][:, jo * 512 + i * 128:
                                              jo * 512 + i * 128 + 128]))
                for p in (0, 1):
                    d = dd + p
                    hbs[p].ready = op("vector", sV,
                                      lambda e, hb=hbs[p].res, d=d, jo=jo, i=i:
                                      e.tensor_sub(
                        hb[:, 128:256],
                        x_res[d][:, jo * 512 + i * 128:jo * 512 + i * 128 + 128],
                        hb[:, 0:128]))
                for p in (0, 1):
                    d = dd + p
                    hb = hbs[p]
                    tp = trA.next()
                    W("tensor", hb.ready)
                    W("tensor", tp.free)
                    if d == 0:
                        W("tensor", ysl.free)
                    opq("tensor", lambda e, tp=tp.res, hb=hb.res: e.matmul(
                        tp[:], hb[:, 0:128], ident[:], start=True, stop=False))
                    tp.ready = op("tensor", sT, lambda e, tp=tp.res, hb=hb.res:
                                  e.matmul(tp[:], hb[:, 128:256], ident[:],
                                           start=False, stop=True))
                    hb.free = tp.ready
                    W("scalar", tp.ready)
                    tp.free = op("scalar", sS,
                                 lambda e, tp=tp.res, d=d, ysl=ysl.res:
                                 e.copy(ysl[:, ts(d, 128)], tp[:]))
                    r_tile = tp.free
            ysl.ready = r_tile
            W("sync", ysl.ready)
            ysl.free = dma("sync", ysl.sem,
                           y[jo * 512 + i * 128: jo * 512 + i * 128 + 128, :],
                           ysl.res[:])
    opq("sync", lambda e, v0=ys_sems[0].n, v1=ys_sems[1].n:
        (e.wait_ge(ys_sems[0].h, v0), e.wait_ge(ys_sems[1].h, v1)))
    ctx.flush()
    b10.close()
    root.close()
    return nc


_PROG = None


def _get_prog():
    global _PROG
    if _PROG is None:
        _PROG = build_program()
    return _PROG


def kernel(L0_attn_x, L0_attn_w_qkv, L0_attn_w_out, L0_ffn_router_w,
           L0_ffn_router_bias, L0_ffn_gate_w, L0_ffn_up_w, L0_ffn_down_w,
           L0_ffn_s_gate_w, L0_ffn_s_up_w, L0_ffn_s_down_w, L0_c, L0_wmod):
    bf = ml_dtypes.bfloat16
    x = np.ascontiguousarray(np.asarray(L0_attn_x, dtype=np.float32))
    wqkv_t = np.ascontiguousarray(np.asarray(L0_attn_w_qkv, np.float32).T)
    wout_t = np.ascontiguousarray(np.asarray(L0_attn_w_out, np.float32).T)
    wmod = np.ascontiguousarray(np.asarray(L0_wmod, np.float32))
    cmat = np.ascontiguousarray(np.asarray(L0_c, np.float32))
    rbv = np.asarray(L0_ffn_router_bias, np.float32).reshape(1, 1)
    rwm = np.asarray(L0_ffn_router_w, np.float32)
    rw_fm = np.ascontiguousarray(
        rwm.reshape(8, 128, 8).transpose(1, 0, 2).reshape(128, 64))
    gwm = np.ascontiguousarray(np.asarray(L0_ffn_gate_w, np.float32))
    uwm = np.ascontiguousarray(np.asarray(L0_ffn_up_w, np.float32))
    dwm = np.ascontiguousarray(np.asarray(L0_ffn_down_w, np.float32))
    sgwm = np.ascontiguousarray(np.asarray(L0_ffn_s_gate_w, np.float32))
    suwm = np.ascontiguousarray(np.asarray(L0_ffn_s_up_w, np.float32))
    sdwm = np.ascontiguousarray(np.asarray(L0_ffn_s_down_w, np.float32))

    mask_h = {}
    for half in (0, 1):
        m = np.zeros((NSLOT, 128, 4096), dtype=bf)
        for sl in range(NSLOT):
            gq = OWN_TGS[half][sl]
            for mc in range(8):
                kc = 4 * (EXT[sl] - 2) + mc
                keys = kc * 128 + np.arange(128)[:, None]
                qs = gq * 512 + np.arange(512)[None, :]
                m[sl, :, mc * 512:(mc + 1) * 512] = (keys <= qs).astype(bf)
        mask_h[half] = m

    nc = _get_prog()
    in_maps = []
    for core in range(8):
        b, half = core // 2, core % 2
        tgs = OWN_TGS[half]
        x_b = np.ascontiguousarray(x[b])
        x_own = np.ascontiguousarray(
            np.concatenate([x[b, ts(t, 512), :] for t in tgs], axis=0))
        in_maps.append({
            "x_full": x_b, "x_own": x_own, "wqkv_t": wqkv_t, "wout_t": wout_t,
            "wmod": wmod,
            "c_fm_in": np.ascontiguousarray(cmat[b].reshape(8, 128).T),
            "rb": rbv, "rw": rw_fm, "gwt": gwm, "uwt": uwm, "dwt": dwm,
            "sgw": sgwm, "suw": suwm, "sdw": sdwm, "masks": mask_h[half],
        })
    res = run_bass_kernel_spmd(nc, in_maps, core_ids=list(range(8)))
    out = np.empty((B, T, D), dtype=np.float32)
    for core in range(8):
        b, half = core // 2, core % 2
        yv = res.results[core]["y"]
        for sl, tg in enumerate(OWN_TGS[half]):
            out[b, ts(tg, 512), :] = yv[ts(sl, 512), :]
    return out


# version 21
# speedup vs baseline: 1.0079x; 1.0079x over previous
import numpy as np
import ml_dtypes
from contextlib import ExitStack

import concourse.bass as bass
import concourse.mybir as mybir
from concourse.bass_utils import run_bass_kernel_spmd

F32 = mybir.dt.float32
BF16 = mybir.dt.bfloat16
AX = mybir.AxisListType.X
ALU = mybir.AluOpType
ACT = mybir.ActivationFunctionType

B, T, D = 4, 4096, 1024
NTOK = 2048
NSLOT = 4
EXT = [2, 4, 6, 8]
OWN_TGS = {0: [0, 3, 4, 7], 1: [1, 2, 5, 6]}
SHIFT = 20.0
NEXP, IEXP, ISH = 8, 512, 2048
EPS = 1e-5


def ts(i, n):
    return slice(i * n, (i + 1) * n)


class Sem:
    def __init__(self, stk, nc, name):
        self.h = stk.enter_context(nc.semaphore(name))
        self.n = 0


class Ctx:
    ENGINES = ("sync", "gpsimd", "tensor", "vector", "scalar")

    def __init__(self, nc):
        self.nc = nc
        self.ops = {e: [] for e in self.ENGINES}

    def emit(self, eng, fn):
        self.ops[eng].append(fn)

    def wait(self, eng, dep):
        if dep is None:
            return
        sem, val = dep
        if val <= 0:
            return
        assert val <= sem.n, f"wait {val} > emitted {sem.n} on {eng}"
        self.ops[eng].append(lambda e, s=sem.h, v=val: e.wait_ge(s, v))

    def flush(self):
        with self.nc.Block() as blk:
            for name in self.ENGINES:
                lst = self.ops[name]
                if lst:
                    getattr(blk, name)(
                        lambda e, _l=tuple(lst): [f(e) for f in _l])
        self.ops = {e: [] for e in self.ENGINES}


class Slot:
    def __init__(self, res, sem=None):
        self.res = res
        self.sem = sem
        self.free = None
        self.ready = None


class Pool:
    def __init__(self, resources):
        self.slots = [Slot(r) for r in resources]
        self.i = 0

    def next(self):
        s = self.slots[self.i % len(self.slots)]
        self.i += 1
        return s


def build_program():
    nc = bass.Bass("TRN2", target_bir_lowering=False, debug=False,
                   detect_race_conditions=False)
    dri = lambda name, shape, dt=F32: nc.dram_tensor(
        name, shape, dt, kind="ExternalInput").ap()
    x_full = dri("x_full", [T, D])
    x_own = dri("x_own", [NTOK, D])
    wqkv_t = dri("wqkv_t", [D, 3 * D])
    wout_t = dri("wout_t", [D, D])
    wmod = dri("wmod", [D, 6 * D])
    c_fm_in = dri("c_fm_in", [128, 8])
    rb = dri("rb", [1, 1])
    rw = dri("rw", [128, 64])
    gwt = dri("gwt", [NEXP, D, IEXP])
    uwt = dri("uwt", [NEXP, D, IEXP])
    dwt = dri("dwt", [NEXP, IEXP, D])
    sgw = dri("sgw", [D, ISH])
    suw = dri("suw", [D, ISH])
    sdw = dri("sdw", [ISH, D])
    masks = dri("masks", [NSLOT, 128, 4096], BF16)
    y = nc.dram_tensor("y", [NTOK, D], F32, kind="ExternalOutput").ap()
    mod_dram = nc.dram_tensor("mod_scratch", [1, 4 * D], F32, kind="Internal").ap()

    root = ExitStack()
    rsb = lambda n, s, d=F32: root.enter_context(nc.sbuf_tensor(n, s, d))

    K_bf = [rsb(f"K{p}", [128, T], BF16) for p in range(4)]
    V_bf = rsb("Vb", [128, 32 * 520], BF16)
    attnout = [rsb(f"ao{d}", [128, NTOK], BF16) for d in range(8)]
    ident = rsb("ident", [128, 128], BF16)
    ones_bf = rsb("ones", [128, 512], BF16)
    brow = rsb("brow", [1, 3 * D], BF16)
    cbp1_tm = rsb("cbp1", [128, D])
    sbm_tm = rsb("sbm", [128, D], BF16)
    cap1_fm = rsb("cap1", [128, 8])
    ga_fm = rsb("ga", [128, 8])
    gb_fm = rsb("gb", [128, 8])
    neg20 = rsb("neg20", [128, 1])
    epsc = rsb("epsc", [128, 1])
    rbcol = rsb("rbcol", [128, 1])
    wsel_bf = rsb("wsel", [128, 128], BF16)
    sa_bf = rsb("sa_bf", [128, 8], BF16)
    x_res = [K_bf[d][:].bitcast(F32) for d in range(4)] + \
            [V_bf[:].bitcast(F32)[:, ts(i, NTOK)] for i in range(4)]
    h2_fm = attnout

    ctx = Ctx(nc)
    sG = Sem(root, nc, "sG")
    sT = Sem(root, nc, "sT")
    sV = Sem(root, nc, "sV")
    sS = Sem(root, nc, "sS")
    s_gp0 = Sem(root, nc, "s_gp0")
    s_wqkv = Sem(root, nc, "s_wqkv")
    s_wsl = Sem(root, nc, "s_wsl")
    s_sgu = Sem(root, nc, "s_sgu")
    wm_sems = [Sem(root, nc, f"s_wm{i}") for i in range(4)]
    xa_sems = [Sem(root, nc, f"s_xa{i}") for i in range(4)]
    s_msk = Sem(root, nc, "s_msk")
    gu_sems = [Sem(root, nc, f"s_gu{i}") for i in range(2)]
    dw_sems = [Sem(root, nc, f"s_dwe{i}") for i in range(2)]
    dc_sems = [Sem(root, nc, f"s_dc{i}") for i in range(2)]
    sd_sems = [Sem(root, nc, f"s_sd{i}") for i in range(2)]
    ys_sems = [Sem(root, nc, f"s_ys{i}") for i in range(2)]
    s_bc = Sem(root, nc, "s_bc")

    def dma(eng, sem, out, in_, slow=False):
        def f(e, o=out, i=in_, s=sem.h, sl=slow):
            kw = {"allow_slow_non_contiguous": True} if sl else {}
            e.dma_start(o, i, **kw).then_inc(s, 16)
        ctx.emit(eng, f)
        sem.n += 16
        return (sem, sem.n)

    def op(eng, sem, fn):
        def f(e, fn=fn, s=sem.h):
            fn(e).then_inc(s, 1)
        ctx.emit(eng, f)
        sem.n += 1
        return (sem, sem.n)

    def opq(eng, fn):
        ctx.emit(eng, fn)

    W = ctx.wait

    def _recl(e, out_ap, in_ap):
        with nc.allow_low_precision(reason="softmax denom recip to bf16"):
            return e.reciprocal(out_ap, in_ap)

    def barrier():
        snap = [(sT, sT.n), (sV, sV.n), (sS, sS.n)]
        for eng in Ctx.ENGINES:
            for dep in snap:
                W(eng, dep)

    b0 = ExitStack()
    bsb = lambda n, s, d=F32: b0.enter_context(nc.sbuf_tensor(n, s, d))
    bps = lambda n, s: b0.enter_context(nc.psum_tensor(n, s, F32))
    io_col = bsb("io_col", [128, 128])
    io_par = bsb("io_par", [128, 1])
    c_fm = bsb("c_fm", [128, 8])
    sil = bsb("sil", [128, 8])
    sil_hi = bsb("sil_hi", [128, 8], BF16)
    sil_lo = bsb("sil_lo", [128, 8], BF16)
    rep_hi = [bsb(f"rph{d}", [128, 128], BF16) for d in range(8)]
    rep_lo = [bsb(f"rpl{d}", [128, 128], BF16) for d in range(8)]
    wm_st = [bsb(f"wm{i}", [128, 512]) for i in range(4)]
    wm_hl = [bsb(f"wmh{i}", [128, 1024], BF16) for i in range(2)]
    modrow = bsb("modrow", [1, 4 * D])
    sga = bsb("sga", [128, 8])
    sgb = bsb("sgb", [128, 8])
    aba = bsb("aba", [128, 8])
    abb = bsb("abb", [128, 8])
    rb_sb = bsb("rb_sb", [1, 1], BF16)
    m32 = bsb("m32", [32, 128])
    m32h = bsb("m32h", [32, 128], BF16)
    m32l = bsb("m32l", [32, 128], BF16)
    mfm = bsb("mfm", [128, 32])
    mfm_ps = b0.enter_context(nc.psum_tensor("pmfm", [128, 32], F32))
    mod_ps = Pool([bps(f"p0_{i}", [128, 512]) for i in range(2)])
    rbc_ps = bps("prbc", [128, 512])

    d_iota = op("gpsimd", sG, lambda e: e.iota(
        io_col[:], [[1, 128]], channel_multiplier=0,
        allow_small_or_imprecise_dtypes=True))
    d_iota = op("gpsimd", sG, lambda e: e.iota(
        io_par[:], [[0, 1]], channel_multiplier=1,
        allow_small_or_imprecise_dtypes=True))
    W("vector", d_iota)
    r_ident = op("vector", sV, lambda e: e.tensor_scalar(
        ident[:], io_col[:], io_par[:, 0:1], None, ALU.is_equal))
    op("vector", sV, lambda e: e.memset(ones_bf[:], 1.0))
    op("vector", sV, lambda e: e.memset(epsc[:], EPS))
    r_ones = op("vector", sV, lambda e: e.memset(neg20[:], -SHIFT))

    dma("gpsimd", s_gp0, c_fm[:, 0:8], c_fm_in[:, :])
    d_crb = dma("gpsimd", s_gp0, rb_sb[0:1, 0:1], rb[0:1, 0:1])
    W("scalar", d_crb)
    r_sil = op("scalar", sS, lambda e: e.activation(sil[:], c_fm[:], ACT.Silu))
    W("vector", r_sil)
    a = op("vector", sV, lambda e: e.tensor_copy(sil_hi[:], sil[:]))
    W("vector", a)
    for d in range(8):
        r_rep = op("vector", sV, lambda e, d=d: e.tensor_copy(
            rep_hi[d][:], sil_hi[:, d:d + 1].to_broadcast([128, 128])))

    st_slots = Pool(wm_st)
    for i, sl in enumerate(st_slots.slots):
        sl.sem = wm_sems[i]
        sl.q = "sync" if i < 2 else "gpsimd"
    hl_slots = Pool(wm_hl)
    r_mod = {}
    for g in range(12):
        mp = mod_ps.next()
        W("tensor", mp.free)
        for d in range(8):
            k = g * 8 + d
            st = st_slots.next()
            W(st.q, st.free)
            st.ready = dma(st.q, st.sem, st.res[:], wmod[ts(d, 128), ts(g, 512)])
            hl = hl_slots.next()
            W("vector", st.ready)
            W("vector", hl.free)
            hl.ready = op("vector", sV, lambda e, a=hl.res, b=st.res:
                          e.tensor_copy(a[:, 0:512], b[:]))
            st.free = hl.ready
            W("tensor", hl.ready)
            if k == 0:
                W("tensor", r_rep)
            hl.free = op("tensor", sT, lambda e, d=d, a=hl.res, mp=mp.res,
                         st_=(d == 0), sp=(d == 7):
                         e.matmul(mp[:], rep_hi[d][:], a[:, 0:512],
                                  start=st_, stop=sp))
        mp.ready = hl.free
        m, hh = g // 2, g % 2
        W("vector", mp.ready)
        if m == 0:
            ev = op("vector", sV, lambda e, p=mp.res, hh=hh:
                    e.tensor_copy(modrow[0:1, ts(hh, 512)], p[0:1, :]))
        elif m == 1:
            ev = op("vector", sV, lambda e, p=mp.res, hh=hh: e.tensor_copy(
                modrow[0:1, 1024 + hh * 512:1024 + hh * 512 + 512], p[0:1, :]))
        elif m == 2:
            ev = op("vector", sV, lambda e, p=mp.res, hh=hh: e.tensor_copy(
                modrow[0:1, 2048 + hh * 512:2048 + hh * 512 + 512], p[0:1, :]))
        elif m == 3:
            ev = op("vector", sV, lambda e, p=mp.res, hh=hh:
                    e.tensor_copy(sbm_tm[:, ts(hh, 512)], p[:]))
        elif m == 4:
            ev = op("vector", sV, lambda e, p=mp.res, hh=hh: e.tensor_scalar(
                cbp1_tm[:, ts(hh, 512)], p[:], 1.0, None, ALU.add))
        else:
            ev = op("vector", sV, lambda e, p=mp.res, hh=hh: e.tensor_copy(
                modrow[0:1, 3072 + hh * 512:3072 + hh * 512 + 512], p[0:1, :]))
        mp.free = ev
        r_mod[m * 2 + hh] = ev

    W("gpsimd", r_mod[11])
    d_bounce = dma("gpsimd", s_gp0, mod_dram[0:1, :], modrow[0:1, :])
    W("gpsimd", d_bounce)
    d_m32 = dma("gpsimd", s_gp0, m32[:, :],
                bass.AP(mod_dram.tensor, 0, [[128, 32], [1, 128]]))
    W("vector", d_m32)
    r_m32 = op("vector", sV, lambda e: e.tensor_copy(m32h[:], m32[:]))
    W("tensor", r_m32)
    W("tensor", r_ident)
    t_mfm = op("tensor", sT, lambda e: e.matmul(mfm_ps[:], m32h[:],
                                                ident[0:32, 0:32],
                                                start=True, stop=True))
    W("vector", t_mfm)
    r_mfm = op("vector", sV, lambda e: e.tensor_copy(mfm[:], mfm_ps[0:128, 0:32]))
    W("vector", r_mfm)
    r_cap1 = op("vector", sV, lambda e: e.tensor_scalar(
        cap1_fm[:], mfm[:, 8:16], 1.0, None, ALU.add))
    r_sabf = op("vector", sV, lambda e: e.tensor_copy(sa_bf[:], mfm[:, 0:8]))
    op("vector", sV, lambda e: e.tensor_copy(ga_fm[:], mfm[:, 16:24]))
    r_gcp = op("vector", sV, lambda e: e.tensor_copy(gb_fm[:], mfm[:, 24:32]))
    W("scalar", r_gcp)
    op("scalar", sS, lambda e: e.activation(sga[:], ga_fm[:], ACT.Sign))
    op("scalar", sS, lambda e: e.activation(sgb[:], gb_fm[:], ACT.Sign))
    op("scalar", sS, lambda e: e.activation(aba[:], ga_fm[:], ACT.Abs))
    a_gab = op("scalar", sS, lambda e: e.activation(abb[:], gb_fm[:], ACT.Abs))
    W("vector", a_gab)
    op("vector", sV, lambda e: e.tensor_scalar(
        aba[:], aba[:], 0.25, None, ALU.max))
    a = op("vector", sV, lambda e: e.tensor_scalar(
        abb[:], abb[:], 0.25, None, ALU.max))
    W("vector", a)
    op("vector", sV, lambda e: e.tensor_mul(ga_fm[:], sga[:], aba[:]))
    r_gab = op("vector", sV, lambda e: e.tensor_mul(gb_fm[:], sgb[:], abb[:]))

    W("tensor", d_crb)
    W("tensor", r_ones)
    t_rbc = op("tensor", sT, lambda e: e.matmul(
        rbc_ps[0:128, 0:1], ones_bf[0:1, 0:128], rb_sb[0:1, 0:1],
        start=True, stop=True))
    W("vector", t_rbc)
    op("vector", sV, lambda e: e.tensor_copy(rbcol[:], rbc_ps[0:128, 0:1]))

    ctx.flush()
    b0.close()
    barrier()

    sa_stk = ExitStack()
    asb = lambda n, s, d=F32: sa_stk.enter_context(nc.sbuf_tensor(n, s, d))
    Q_bf = [asb(f"Q{p}", [128, NTOK], BF16) for p in range(4)]
    wsl = [asb(f"wsl{d}", [128, 1536], BF16) for d in range(8)]
    xa = [asb(f"xa{i}", [128, D]) for i in range(4)]
    xnb = [asb(f"xn{i}", [128, D], BF16) for i in range(2)]
    xfm = [[asb(f"xf{s}_{d}", [128, 512], BF16) for d in range(8)]
           for s in range(2)]
    st8 = asb("st8", [128, 32])
    P_sb = [asb(f"P{i}", [128, 1024], BF16) for i in range(4)]
    msk_sb = asb("msk", [128, 4096], BF16)
    avb_st = [asb(f"avst{i}", [64, 512], BF16) for i in range(2)]
    bc2 = [asb(f"bc2_{i}", [64, 512], BF16) for i in range(2)]
    rec1 = asb("rec1", [1, 512], BF16)

    xofm = [Q_bf[p][:].bitcast(F32) for p in range(4)]

    def ln_group(tiles):
        n = len(tiles)
        for k, (xt, xnt, st, dep, nfree) in enumerate(tiles):
            W("vector", dep)
            op("vector", sV, lambda e, xt=xt, st=st:
               e.reduce_sum(st[:, 0:1], xt[:], AX))
        a = None
        for k, (xt, xnt, st, dep, nfree) in enumerate(tiles):
            W("scalar", dep)
            W("scalar", nfree)
            a = op("scalar", sS, lambda e, xt=xt, xnt=xnt, st=st: e.activation(
                xnt[:], xt[:], ACT.Square, accum_out=st[:, 1:2]))
        W("vector", a)
        for k, (xt, xnt, st, dep, nfree) in enumerate(tiles):
            op("vector", sV, lambda e, st=st: e.tensor_scalar(
                st[:, 5:6], st[:, 1:2], 1.0 / D, EPS, ALU.mult, ALU.add))
        a = None
        for k, (xt, xnt, st, dep, nfree) in enumerate(tiles):
            a = op("vector", sV, lambda e, st=st: e.tensor_scalar(
                st[:, 2:3], st[:, 0:1], 1.0 / D, None, ALU.mult))
        W("scalar", a)
        for k, (xt, xnt, st, dep, nfree) in enumerate(tiles):
            a = op("scalar", sS, lambda e, st=st: e.activation(
                st[:, 3:4], st[:, 2:3], ACT.Square))
        W("vector", a)
        for k, (xt, xnt, st, dep, nfree) in enumerate(tiles):
            a = op("vector", sV, lambda e, st=st: e.scalar_tensor_tensor(
                st[:, 4:5], st[:, 3:4], -1.0, st[:, 5:6], ALU.mult, ALU.add))
        W("scalar", a)
        for k, (xt, xnt, st, dep, nfree) in enumerate(tiles):
            a = op("scalar", sS, lambda e, st=st: e.activation(
                st[:, 6:7], st[:, 4:5], ACT.Sqrt))
        W("vector", a)
        for k, (xt, xnt, st, dep, nfree) in enumerate(tiles):
            a = op("vector", sV, lambda e, st=st: e.reciprocal(
                st[:, 7:8], st[:, 6:7]))
        if n < 4:
            W("vector", a)
        fin = [None] * n
        for k, (xt, xnt, st, dep, nfree) in enumerate(tiles):
            W("vector", nfree)
            fin[k] = op("vector", sV, lambda e, xt=xt, xnt=xnt, st=st:
                        e.scalar_tensor_tensor(
                xnt[:], xt[:], st[:, 2:3],
                st[:, 7:8].to_broadcast([128, D]), ALU.subtract, ALU.mult))
        return fin

    def ln_tile(xt, xnt, stats, dep_x):
        return ln_group([(xt, xnt, stats, dep_x, None)])[0]

    xa_p = Pool(xa)
    for i, sl in enumerate(xa_p.slots):
        sl.sem = xa_sems[i]
    xn_p = Pool(xnb + [msk_sb[:, ts(i, 1024)] for i in range(4)])
    for hg in range(2):
        barrier()
        pa = ExitStack()
        tr_ps = Pool([pa.enter_context(nc.psum_tensor(f"tr{hg}_{i}", [128, 128], F32))
                      for i in range(2)])
        kv_ps = Pool([pa.enter_context(nc.psum_tensor(f"kv{hg}_{i}", [128, 512], F32))
                      for i in range(4)])

        d_w = None
        for d in range(8):
            dma("gpsimd", s_wsl, wsl[d][:, 0:512],
                wqkv_t[ts(d, 128), hg * 512:hg * 512 + 512])
            dma("gpsimd", s_wsl, wsl[d][:, 512:1024],
                wqkv_t[ts(d, 128), 1024 + hg * 512:1024 + hg * 512 + 512])
            d_w = dma("gpsimd", s_wsl, wsl[d][:, 1024:1536],
                      wqkv_t[ts(d, 128), 2048 + hg * 512:2048 + hg * 512 + 512])
        br_ps = Pool([pa.enter_context(nc.psum_tensor(f"br{hg}_{i}", [1, 512],
                                                      F32)) for i in range(2)])
        W("tensor", d_w)
        W("tensor", r_sabf)
        r_brow = None
        t_brow = None
        for part in range(3):
            bp = br_ps.next()
            W("tensor", bp.free)
            for d in range(8):
                t_brow = op("tensor", sT,
                            lambda e, d=d, part=part, bp=bp.res:
                            e.matmul(bp[:], sa_bf[:, d:d + 1],
                                     wsl[d][:, ts(part, 512)],
                                     start=(d == 0), stop=(d == 7)))
            bp.ready = t_brow
            W("vector", bp.ready)
            dstb = [hg * 512, 1024 + hg * 512, 2048 + hg * 512][part]
            if part == 0:
                ev = op("vector", sV, lambda e, bp=bp.res, dstb=dstb:
                        e.tensor_scalar(brow[0:1, dstb:dstb + 512],
                                        bp[0:1, :], 0.125, None, ALU.mult))
            else:
                ev = op("vector", sV, lambda e, bp=bp.res, dstb=dstb:
                        e.tensor_copy(brow[0:1, dstb:dstb + 512], bp[0:1, :]))
            bp.free = ev
            r_brow = ev
        W("vector", t_brow)
        W("vector", d_w)
        W("vector", r_cap1)
        for d in range(8):
            op("vector", sV, lambda e, d=d: e.tensor_mul(
                wsl[d][:], wsl[d][:],
                cap1_fm[:, d:d + 1].to_broadcast([128, 1536])))
            r_wsl = op("vector", sV, lambda e, d=d: e.tensor_scalar(
                wsl[d][:, 0:512], wsl[d][:, 0:512], 0.125, None, ALU.mult))
        for kc in range(32):
            r_vones = op("vector", sV, lambda e, kc=kc: e.memset(
                bass.AP(V_bf, 520 * kc + 64, [[32 * 520, 128], [65, 8], [1, 1]]),
                1.0))

        xfm_ready = [None, None]
        xfm_free = [None, None]
        for grp in range(12):
            qpass = grp >= 8
            s2 = grp % 2
            group, xss, nss = [], [], []
            for i in range(4):
                it = grp * 4 + i
                src = (x_own[ts(it - 32, 128), :] if qpass
                       else x_full[ts(it, 128), :])
                xs = xa_p.next()
                W("sync", xs.free)
                xs.ready = dma("sync", xs.sem, xs.res[:], src)
                ns = xn_p.next()
                xss.append(xs)
                nss.append(ns)
                group.append((xs.res, ns.res, st8[:, 8 * i:8 * i + 8],
                              xs.ready, ns.free))
            fins = ln_group(group)
            for i in range(4):
                xss[i].free = fins[i]
            if grp == 0:
                W("tensor", r_wsl)
                W("tensor", r_vones)
                W("tensor", r_brow)
                W("tensor", r_ident)
            W("tensor", xfm_free[s2])
            for i in range(4):
                ns = nss[i]
                W("tensor", fins[i])
                for d in range(8):
                    tp = tr_ps.next()
                    W("tensor", tp.free)
                    tp.ready = op("tensor", sT,
                                  lambda e, tp=tp.res, x=ns.res, d=d:
                                  e.matmul(tp[:], x[:, ts(d, 128)], ident[:],
                                           start=True, stop=True))
                    W("scalar", tp.ready)
                    tp.free = op("scalar", sS,
                                 lambda e, tp=tp.res, d=d, s2=s2, i=i:
                                 e.copy(xfm[s2][d][:, ts(i, 128)], tp[:]))
                    r_evac = tp.free
                ns.free = r_evac
            xfm_ready[s2] = r_evac
            W("tensor", xfm_ready[s2])
            r_cons = None
            if not qpass:
                tg = grp
                for p in range(4):
                    kp = kv_ps.next()
                    W("tensor", kp.free)
                    for d in range(8):
                        opq("tensor", lambda e, kp=kp.res, d=d, p=p, s2=s2:
                            e.matmul(kp[:], wsl[d][:, 512 + p * 128:512 + p * 128 + 128],
                                     xfm[s2][d][:], start=(d == 0), stop=False))
                    kp.ready = op("tensor", sT, lambda e, kp=kp.res, p=p, hg=hg:
                                  e.matmul(kp[:],
                                           brow[0:1, 1024 + hg * 512 + p * 128:
                                                1024 + hg * 512 + p * 128 + 128],
                                           ones_bf[0:1, :], start=False, stop=True))
                    W("vector", kp.ready)
                    kp.free = op("vector", sV, lambda e, kp=kp.res, p=p, tg=tg:
                                 e.tensor_copy(K_bf[p][:, ts(tg, 512)], kp[:]))
                    r_cons = kp.free
                for kk in range(4):
                    kc = tg * 4 + kk
                    vp = kv_ps.next()
                    W("tensor", vp.free)
                    for d in range(8):
                        opq("tensor", lambda e, vp=vp.res, d=d, kk=kk, s2=s2:
                            e.matmul(vp[:], xfm[s2][d][:, ts(kk, 128)],
                                     wsl[d][:, 1024:1536], start=(d == 0), stop=False))
                    vp.ready = op("tensor", sT, lambda e, vp=vp.res, hg=hg:
                                  e.matmul(vp[:], ones_bf[0:1, 0:128],
                                           brow[0:1, 2048 + hg * 512:
                                                2048 + hg * 512 + 512],
                                           start=False, stop=True))
                    W("vector", vp.ready)
                    vp.free = op("vector", sV, lambda e, vp=vp.res, kc=kc:
                                 e.tensor_copy(
                        bass.AP(V_bf, 520 * kc, [[32 * 520, 128], [65, 8], [1, 64]]),
                        vp[:]))
                    r_cons = vp.free
            else:
                jo = grp - 8
                for p in range(4):
                    qp = kv_ps.next()
                    W("tensor", qp.free)
                    for d in range(8):
                        opq("tensor", lambda e, qp=qp.res, d=d, p=p, s2=s2:
                            e.matmul(qp[:], wsl[d][:, p * 128:p * 128 + 128],
                                     xfm[s2][d][:], start=(d == 0), stop=False))
                    qp.ready = op("tensor", sT, lambda e, qp=qp.res, p=p, hg=hg:
                                  e.matmul(qp[:],
                                           brow[0:1, hg * 512 + p * 128:
                                                hg * 512 + p * 128 + 128],
                                           ones_bf[0:1, :], start=False, stop=True))
                    W("vector", qp.ready)
                    qp.free = op("vector", sV, lambda e, qp=qp.res, p=p, jo=jo:
                                 e.tensor_copy(Q_bf[p][:, ts(jo, 512)], qp[:]))
                    r_cons = qp.free
            xfm_free[s2] = r_cons
        ctx.flush()
        pa.close()
        barrier()

        pa2 = ExitStack()
        sc_ps = Pool([pa2.enter_context(nc.psum_tensor(f"sc{hg}_{i}", [128, 1024], F32))
                      for i in range(3)])
        av_ps = Pool([pa2.enter_context(nc.psum_tensor(f"av{hg}_{i}", [65, 512], F32))
                      for i in range(2)])
        P_p = Pool(P_sb)
        msk_slot = Slot(msk_sb)

        for j in range(NSLOT):
            W("sync", msk_slot.free)
            msk_slot.ready = dma("sync", s_msk, msk_sb[:], masks[j, :, :])
            r_mask_last = None
            npair = 2 * EXT[j]
            for hp in range(4):
                hls = (2 * hp, 2 * hp + 1)
                avs, scs, pbs = {}, {}, {}
                for hl in hls:
                    avs[hl] = av_ps.next()
                for pc in range(npair):
                    kc0, kc1 = 2 * pc, 2 * pc + 1
                    for hl in hls:
                        p, par = hl // 2, hl % 2
                        pr = slice(64 * par, 64 * par + 64)
                        sc = sc_ps.next()
                        scs[hl] = sc
                        W("tensor", sc.free)
                        opq("tensor",
                            lambda e, sc=sc.res, p=p, pr=pr, kc0=kc0, j=j:
                            e.matmul(sc[:, 0:512], K_bf[p][pr, ts(kc0, 128)],
                                     Q_bf[p][pr, ts(j, 512)],
                                     start=True, stop=True))
                        sc.ready = op("tensor", sT,
                                      lambda e, sc=sc.res, p=p, pr=pr,
                                      kc1=kc1, j=j:
                                      e.matmul(sc[:, 512:1024],
                                               K_bf[p][pr, ts(kc1, 128)],
                                               Q_bf[p][pr, ts(j, 512)],
                                               start=True, stop=True))
                    for hl in hls:
                        sc = scs[hl]
                        pb = P_p.next()
                        pbs[hl] = pb
                        W("scalar", sc.ready)
                        W("scalar", pb.free)
                        pb.ready = op("scalar", sS,
                                      lambda e, sc=sc.res, pb=pb.res:
                                      e.activation(pb[:], sc[:], ACT.Exp,
                                                   bias=neg20[:, 0:1]))
                        sc.free = pb.ready
                    masked = pc >= npair - 4
                    if masked:
                        mc = pc - (npair - 4)
                        for hl in hls:
                            pb = pbs[hl]
                            W("vector", pb.ready)
                            if hp == 0 and hl == hls[0] and mc == 0:
                                W("vector", msk_slot.ready)
                            pb.ready = op("vector", sV,
                                          lambda e, pb=pb.res, mc=mc:
                                          e.tensor_mul(pb[:], pb[:],
                                                       msk_sb[:, ts(mc, 1024)]))
                            r_mask_last = pb.ready
                    for hl in hls:
                        av, pb = avs[hl], pbs[hl]
                        if pc == 0:
                            W("tensor", av.free)
                        W("tensor", pb.ready)
                        opq("tensor",
                            lambda e, av=av.res, hl=hl, kc0=kc0, pb=pb.res,
                            st_=(pc == 0):
                            e.matmul(av[:], bass.AP(V_bf, 520 * kc0 + 65 * hl,
                                                    [[32 * 520, 128], [1, 65]]),
                                     pb[:, 0:512], start=st_, stop=False))
                        pb.free = op("tensor", sT,
                                     lambda e, av=av.res, hl=hl, kc1=kc1,
                                     pb=pb.res, sp=(pc == npair - 1):
                                     e.matmul(av[:],
                                              bass.AP(V_bf, 520 * kc1 + 65 * hl,
                                                      [[32 * 520, 128], [1, 65]]),
                                              pb[:, 512:1024],
                                              start=False, stop=sp))
                        av.ready = pb.free
                avb = [avb_st[0], avb_st[1]]
                for q, hl in enumerate(hls):
                    av = avs[hl]
                    W("vector", av.ready)
                    op("vector", sV, lambda e, av=av.res, b=avb[q]:
                       e.tensor_copy(b[:], av[0:64, :]))
                    r_st = op("vector", sV, lambda e, av=av.res: _recl(
                        e, rec1[0:1, :], av[64:65, :]))
                    av.free = r_st
                    W("gpsimd", r_st)
                    d_bc = dma("gpsimd", s_bc, bc2[q][:],
                               bass.AP(rec1, 0, [[1, 1], [0, 64], [1, 512]]))
                    W("vector", d_bc)
                    hd = (hg * 8 + hl) // 2
                    dst = slice(64 * ((hg * 8 + hl) % 2),
                                64 * ((hg * 8 + hl) % 2) + 64)
                    op("vector", sV,
                       lambda e, b=avb[q], hd=hd, dst=dst, j=j, q=q:
                       e.tensor_mul(attnout[hd][dst, ts(j, 512)],
                                    b[:], bc2[q][:]))
            msk_slot.free = r_mask_last
        ctx.flush()
        pa2.close()

    pb5 = ExitStack()
    tr5 = Pool([pb5.enter_context(nc.psum_tensor(f"t5{i}", [128, 128], F32))
                for i in range(2)])
    pr5 = Pool([pb5.enter_context(nc.psum_tensor(f"p5{i}", [128, 512], F32))
                for i in range(2)])
    xhl5_p = Pool([msk_sb[:, 0:2 * D], msk_sb[:, 2 * D:4 * D]])
    wout_bf = [wsl[d][:, 0:1024] for d in range(8)]
    barrier()
    d_wo = None
    for d in range(8):
        d_wo = dma("gpsimd", s_wsl, wout_bf[d], wout_t[ts(d, 128), :])

    for jo in range(NSLOT):
        r_xo = None
        for i in range(4):
            it = jo * 4 + i
            xs = xa_p.next()
            W("sync", xs.free)
            xs.ready = dma("sync", xs.sem, xs.res[:], x_own[ts(it, 128), :])
            W("vector", xs.ready)
            hb = xhl5_p.next()
            W("vector", hb.free)
            op("vector", sV, lambda e, x=xs.res, h=hb.res:
               e.tensor_copy(h[:, 0:D], x[:]))
            r_hl = op("vector", sV, lambda e, x=xs.res, h=hb.res:
                      e.tensor_sub(h[:, D:2 * D], x[:], h[:, 0:D]))
            xs.free = r_hl
            W("tensor", r_hl)
            if it == 0:
                W("tensor", d_wo)
            for d in range(8):
                tp = tr5.next()
                W("tensor", tp.free)
                opq("tensor", lambda e, tp=tp.res, d=d, h=hb.res: e.matmul(
                    tp[:], h[:, ts(d, 128)], ident[:], start=True, stop=False))
                tp.ready = op("tensor", sT, lambda e, tp=tp.res, d=d, h=hb.res:
                              e.matmul(tp[:], h[:, D + d * 128:D + d * 128 + 128],
                                       ident[:], start=False, stop=True))
                hb.free = tp.ready
                W("scalar", tp.ready)
                tp.free = op("scalar", sS, lambda e, tp=tp.res, d=d, i=i: e.copy(
                    xofm[d // 2][:, (d % 2) * 512 + i * 128:
                                 (d % 2) * 512 + i * 128 + 128], tp[:]))
                r_xo = tp.free
        W("tensor", r_xo)
        for c in range(8):
            pp = pr5.next()
            W("tensor", pp.free)
            for d in range(8):
                pp.ready = op("tensor", sT, lambda e, pp=pp.res, c=c, d=d, jo=jo:
                              e.matmul(pp[:], wout_bf[d][:, ts(c, 128)],
                                       attnout[d][:, ts(jo, 512)],
                                       start=(d == 0), stop=(d == 7)))
            W("vector", pp.ready)
            pp.free = op("vector", sV, lambda e, pp=pp.res, c=c, jo=jo:
                         e.scalar_tensor_tensor(
                x_res[c][:, ts(jo, 512)], pp[:], ga_fm[:, c:c + 1],
                xofm[c // 2][:, (c % 2) * 512:(c % 2) * 512 + 512],
                ALU.mult, ALU.add))
    ctx.flush()
    pb5.close()
    sa_stk.close()

    b6 = ExitStack()
    b6sb = lambda n, s, d=F32: b6.enter_context(nc.sbuf_tensor(n, s, d))
    x2tm = [b6sb(f"x2_{i}", [128, D]) for i in range(2)]
    h2tm = [b6sb(f"h2_{i}", [128, D], BF16) for i in range(2)]
    hi6 = [b6sb(f"hi6_{p}", [128, 128], BF16) for p in range(2)]
    lo6 = [b6sb(f"lo6_{p}", [128, 128], BF16) for p in range(2)]
    st6 = b6sb("st6", [128, 16])
    rw_bf = b6sb("rw_bf", [128, 64], BF16)
    sm8 = b6sb("sm8", [128, 192])
    tr6 = Pool([b6.enter_context(nc.psum_tensor(f"t6{i}", [128, 128], F32))
                for i in range(2)])
    lg6 = Pool([b6.enter_context(nc.psum_tensor(f"l6{i}", [128, 8], F32))
                for i in range(4)])

    barrier()
    d_rw = dma("gpsimd", s_gp0, rw_bf[:, 0:64], rw[:, :])

    x2_p = Pool(x2tm)
    h2_p = Pool(h2tm)
    r_h2last = None
    for it in range(16):
        jo, i = divmod(it, 4)
        x2 = x2_p.next()
        h2s = h2_p.next()
        r_tx2 = [None, None]
        for dd in (0, 2, 4, 6):
            r_hl = [None, None]
            for p in (0, 1):
                d = dd + p
                W("vector", r_tx2[p])
                op("vector", sV, lambda e, d=d, jo=jo, i=i, p=p: e.tensor_copy(
                    hi6[p][:],
                    x_res[d][:, jo * 512 + i * 128:jo * 512 + i * 128 + 128]))
            for p in (0, 1):
                d = dd + p
                r_hl[p] = op("vector", sV, lambda e, d=d, jo=jo, i=i, p=p:
                             e.tensor_sub(
                    lo6[p][:],
                    x_res[d][:, jo * 512 + i * 128:jo * 512 + i * 128 + 128],
                    hi6[p][:]))
            for p in (0, 1):
                d = dd + p
                tp = tr6.next()
                W("tensor", r_hl[p])
                W("tensor", tp.free)
                W("tensor", x2.free)
                opq("tensor", lambda e, tp=tp.res, p=p: e.matmul(
                    tp[:], hi6[p][:], ident[:], start=True, stop=False))
                tp.ready = op("tensor", sT, lambda e, tp=tp.res, p=p: e.matmul(
                    tp[:], lo6[p][:], ident[:], start=False, stop=True))
                r_tx2[p] = tp.ready
                W("scalar", tp.ready)
                tp.free = op("scalar", sS, lambda e, tp=tp.res, d=d, x2=x2.res:
                             e.copy(x2[:, ts(d, 128)], tp[:]))
                r_x2 = tp.free
        x2.ready = r_x2
        W("vector", h2s.free)
        r_ln = ln_tile(x2.res, h2s.res, st6, x2.ready)
        op("vector", sV, lambda e, h=h2s.res: e.tensor_mul(h[:], h[:], cbp1_tm[:]))
        r_h2 = op("vector", sV, lambda e, h=h2s.res:
                  e.tensor_add(h[:], h[:], sbm_tm[:]))
        x2.free = r_h2
        W("tensor", r_h2)
        for d in range(8):
            tp = tr6.next()
            W("tensor", tp.free)
            tp.ready = op("tensor", sT, lambda e, tp=tp.res, d=d, h=h2s.res:
                          e.matmul(tp[:], h[:, ts(d, 128)], ident[:],
                                   start=True, stop=True))
            W("scalar", tp.ready)
            tp.free = op("scalar", sS, lambda e, tp=tp.res, d=d, it=it:
                         e.copy(h2_fm[d][:, ts(it, 128)], tp[:]))
            r_h2last = tp.free
        h2s.free = r_h2last

    W("tensor", r_h2last)
    W("tensor", d_rw)
    for tkp in range(4):
        tks = tuple(4 * tkp + q for q in range(4))
        sm = [sm8[:, ts(q, 48)] for q in range(4)]
        lgs = []
        for tk in tks:
            lg = lg6.next()
            W("tensor", lg.free)
            for d in range(8):
                lg.ready = op("tensor", sT, lambda e, lg=lg.res, d=d, tk=tk:
                              e.matmul(lg[:], h2_fm[d][:, ts(tk, 128)],
                                       rw_bf[:, ts(d, 8)],
                                       start=(d == 0), stop=(d == 7)))
            lgs.append(lg)
        for p, lg in enumerate(lgs):
            W("vector", lg.ready)
            a = op("vector", sV, lambda e, lg=lg.res, s=sm[p]:
                   e.reduce_max(s[:, 0:1], lg[:], AX, negate=True))
            W("scalar", a)
            a = op("scalar", sS, lambda e, lg=lg.res, s=sm[p]: e.activation(
                s[:, 8:16], lg[:], ACT.Exp, bias=s[:, 0:1]))
            lg.free = a
            W("vector", a)

        def pair(fn):
            r = None
            for p in range(4):
                r = op("vector", sV, lambda e, s=sm[p], p=p: fn(e, s, p))
            W("vector", r)
            return r
        pair(lambda e, s, p: e.reduce_sum(s[:, 1:2], s[:, 8:16], AX))
        pair(lambda e, s, p: e.reciprocal(s[:, 2:3], s[:, 1:2]))
        pair(lambda e, s, p: e.tensor_mul(
            s[:, 8:16], s[:, 8:16], s[:, 2:3].to_broadcast([128, 8])))
        pair(lambda e, s, p: e.tensor_add(
            s[:, 16:24], s[:, 8:16], rbcol[:, 0:1].to_broadcast([128, 8])))
        pair(lambda e, s, p: e.max(out=s[:, 24:32], in_=s[:, 16:24]))
        pair(lambda e, s, p: e.tensor_tensor(
            s[:, 32:40], s[:, 16:24],
            s[:, 27:28].to_broadcast([128, 8]), ALU.is_ge))
        pair(lambda e, s, p: e.tensor_mul(
            s[:, 32:40], s[:, 32:40], s[:, 8:16]))
        pair(lambda e, s, p: e.reduce_sum(s[:, 3:4], s[:, 32:40], AX))
        pair(lambda e, s, p: e.tensor_scalar(
            s[:, 3:4], s[:, 3:4], 1e-20, None, ALU.add))
        pair(lambda e, s, p: e.reciprocal(s[:, 4:5], s[:, 3:4]))
        pair(lambda e, s, p, _tks=tks: e.tensor_mul(
            wsel_bf[:, ts(_tks[p], 8)], s[:, 32:40],
            s[:, 4:5].to_broadcast([128, 8])))
    ctx.flush()
    b6.close()

    b7 = ExitStack()
    b7sb = lambda n, s, d=BF16: b7.enter_context(nc.sbuf_tensor(n, s, d))
    gu_bf = [b7sb(f"gu{i}", [128, 8192]) for i in range(2)]
    dw_bf = [b7sb(f"dwb{i}", [128, 4096]) for i in range(2)]
    act_e = [b7sb(f"acte{s}", [128, NTOK]) for s in range(4)]
    rep7 = b7sb("rep7", [128, 128])
    wsb7 = [b7sb(f"wsb{i}", [128, 512]) for i in range(2)]
    sg7 = [b7sb(f"sg7{i}", [128, 512]) for i in range(2)]
    gu_ps = Pool([b7.enter_context(nc.psum_tensor(f"g7{i}", [128, 512], F32))
                  for i in range(4)])
    wb_ps = Pool([b7.enter_context(nc.psum_tensor(f"w7{i}", [128, 512], F32))
                  for i in range(2)])
    d8_ps = Pool([b7.enter_context(nc.psum_tensor(f"d8{i}", [128, 512], F32))
                  for i in range(2)])
    gu_p = Pool(gu_bf)
    for i, sl in enumerate(gu_p.slots):
        sl.sem = gu_sems[i]
    dw_p = Pool(dw_bf)
    for i, sl in enumerate(dw_p.slots):
        sl.sem = dw_sems[i]
    wsb_p = Pool(wsb7)
    sg_p = Pool(sg7)
    act_slots = [Slot(a) for a in act_e]

    barrier()
    for e_i in range(NEXP):
        gu = gu_p.next()
        W("gpsimd", gu.free)
        for d in range(8):
            dma("gpsimd", gu.sem, gu.res[:, ts(d, 512)], gwt[e_i, ts(d, 128), :])
        for d in range(8):
            gu.ready = dma("gpsimd", gu.sem,
                           gu.res[:, 4096 + d * 512:4096 + d * 512 + 512],
                           uwt[e_i, ts(d, 128), :])
        dwe = dw_p.next()
        W("gpsimd", dwe.free)
        for s in range(4):
            dwe.ready = dma("gpsimd", dwe.sem, dwe.res[:, ts(s, 1024)],
                            dwt[e_i, ts(s, 128), :])
        r_gu_last = None
        for j in range(NSLOT):
            wj = wsb_p.next()
            wp = wb_ps.next()
            W("tensor", wp.free)
            r_rep = None
            for q4 in range(4):
                tk = j * 4 + q4
                W("vector", r_rep)
                r_rep0 = op("vector", sV, lambda e, tk=tk, e_i=e_i: e.tensor_copy(
                    rep7[:], wsel_bf[:, tk * 8 + e_i:tk * 8 + e_i + 1]
                    .to_broadcast([128, 128])))
                W("tensor", r_rep0)
                r_rep = op("tensor", sT, lambda e, wp=wp.res, q4=q4: e.matmul(
                    wp[:, ts(q4, 128)], rep7[:], ident[:], start=True, stop=True))
            wp.ready = r_rep
            W("vector", wp.ready)
            W("vector", wj.free)
            wj.ready = op("vector", sV, lambda e, wj=wj.res, wp=wp.res:
                          e.tensor_copy(wj[:], wp[:]))
            wp.free = wj.ready
            if j == 0:
                W("tensor", gu.ready)
            for s in range(4):
                pg = gu_ps.next()
                pu = gu_ps.next()
                ac = act_slots[s]
                W("tensor", pg.free)
                W("tensor", pu.free)
                for d in range(8):
                    opq("tensor", lambda e, pg=pg.res, gu=gu.res, d=d, s=s, j=j:
                        e.matmul(pg[:], gu[:, d * 512 + s * 128:d * 512 + s * 128 + 128],
                                 h2_fm[d][:, ts(j, 512)], start=(d == 0), stop=(d == 7)))
                for d in range(8):
                    r_gu = op("tensor", sT,
                              lambda e, pu=pu.res, gu=gu.res, d=d, s=s, j=j:
                              e.matmul(pu[:],
                                       gu[:, 4096 + d * 512 + s * 128:
                                          4096 + d * 512 + s * 128 + 128],
                                       h2_fm[d][:, ts(j, 512)],
                                       start=(d == 0), stop=(d == 7)))
                r_gu_last = r_gu
                sg = sg_p.next()
                W("scalar", r_gu)
                W("scalar", sg.free)
                sg.ready = op("scalar", sS, lambda e, pg=pg.res, sg=sg.res:
                              e.activation(sg[:], pg[:], ACT.Silu))
                pg.free = sg.ready
                W("vector", sg.ready)
                W("vector", wj.ready)
                W("vector", ac.free)
                op("vector", sV, lambda e, pu=pu.res, sg=sg.res, ac=ac.res, s=s, j=j:
                   e.tensor_mul(ac[:, ts(j, 512)], sg[:], pu[:]))
                r_act = op("vector", sV, lambda e, ac=ac.res, wj=wj.res, s=s, j=j:
                           e.tensor_mul(ac[:, ts(j, 512)], ac[:, ts(j, 512)], wj[:]))
                pu.free = r_act
                sg.free = r_act
                ac.ready = r_act
            wj.free = r_act
        gu.free = r_gu_last
        r_dn = None
        for j in range(NSLOT):
            pd = d8_ps.next()
            W("tensor", pd.free)
            if j == 0:
                W("tensor", dwe.ready)
            for s in range(4):
                W("tensor", act_slots[s].ready)
            for c in range(8):
                if c > 0:
                    pd = d8_ps.next()
                    W("tensor", pd.free)
                for s in range(4):
                    pd.ready = op("tensor", sT,
                                  lambda e, pd=pd.res, dw=dwe.res, s=s, c=c, j=j,
                                  st_=(s == 0), sp=(s == 3):
                                  e.matmul(pd[:], dw[:, s * 1024 + c * 128:
                                                    s * 1024 + c * 128 + 128],
                                           act_e[s][:, ts(j, 512)],
                                           start=st_, stop=sp))
                W("vector", pd.ready)
                pd.free = op("vector", sV, lambda e, pd=pd.res, c=c, j=j:
                             e.scalar_tensor_tensor(
                    x_res[c][:, ts(j, 512)], pd[:], gb_fm[:, c:c + 1],
                    x_res[c][:, ts(j, 512)], ALU.mult, ALU.add))
                r_dn = pd.free
        dwe.free = r_dn
        for s in range(4):
            act_slots[s].free = r_dn
    ctx.flush()
    b7.close()

    b9 = ExitStack()
    b9sb = lambda n, s, d=BF16: b9.enter_context(nc.sbuf_tensor(n, s, d))
    sgu_bf = [b9sb(f"sgu{d}", [128, 2048]) for d in range(8)]
    s_act = [b9sb(f"sac{i}", [128, NTOK]) for i in range(8)]
    sdw_bf = [b9sb(f"sdwb{i}", [128, 1024]) for i in range(2)]
    sg9 = [b9sb(f"sg9{i}", [128, 512]) for i in range(2)]
    gu9 = Pool([b9.enter_context(nc.psum_tensor(f"g9{i}", [128, 512], F32))
                for i in range(4)])
    d9_ps = Pool([b9.enter_context(nc.psum_tensor(f"d9{i}", [128, 512], F32))
                  for i in range(2)])
    sdw_p = Pool(sdw_bf)
    for i, sl in enumerate(sdw_p.slots):
        sl.sem = sd_sems[i]
    sg9_p = Pool(sg9)

    prev_g9 = None
    prev_dn9 = None
    barrier()
    for ih in range(2):
        d_sg = None
        if ih == 1:
            W("gpsimd", prev_g9)
        for d in range(8):
            dma("gpsimd", s_sgu, sgu_bf[d][:, 0:1024],
                sgw[ts(d, 128), ih * 1024:ih * 1024 + 1024])
            d_sg = dma("gpsimd", s_sgu, sgu_bf[d][:, 1024:2048],
                       suw[ts(d, 128), ih * 1024:ih * 1024 + 1024])
        if ih == 1:
            W("vector", prev_dn9)
        s_act_ready = [None] * 8
        r_act9 = None
        for ic in range(8):
            for j in range(NSLOT):
                pg = gu9.next()
                pu = gu9.next()
                W("tensor", pg.free)
                W("tensor", pu.free)
                if ic == 0 and j == 0:
                    W("tensor", d_sg)
                for d in range(8):
                    opq("tensor", lambda e, pg=pg.res, d=d, ic=ic, j=j:
                        e.matmul(pg[:], sgu_bf[d][:, ts(ic, 128)],
                                 h2_fm[d][:, ts(j, 512)], start=(d == 0), stop=(d == 7)))
                for d in range(8):
                    r_g9 = op("tensor", sT,
                              lambda e, pu=pu.res, d=d, ic=ic, j=j:
                              e.matmul(pu[:], sgu_bf[d][:, 1024 + ic * 128:
                                                        1024 + ic * 128 + 128],
                                       h2_fm[d][:, ts(j, 512)],
                                       start=(d == 0), stop=(d == 7)))
                sg = sg9_p.next()
                W("scalar", r_g9)
                W("scalar", sg.free)
                sg.ready = op("scalar", sS, lambda e, pg=pg.res, sg=sg.res:
                              e.activation(sg[:], pg[:], ACT.Silu))
                pg.free = sg.ready
                W("vector", sg.ready)
                r_a = op("vector", sV, lambda e, pu=pu.res, sg=sg.res, ic=ic, j=j:
                         e.tensor_mul(s_act[ic][:, ts(j, 512)], sg[:], pu[:]))
                pu.free = r_a
                sg.free = r_a
                r_act9 = r_a
            s_act_ready[ic] = r_act9
        r_dn9 = None
        for c in range(8):
            sd = sdw_p.next()
            W("gpsimd", sd.free)
            sd.ready = dma("gpsimd", sd.sem, bass.AP(sd.res, 0,
                                                 [[1024, 128], [128, 8], [1, 128]]),
                           bass.AP(sdw.tensor, ih * 1024 * 1024 + c * 128,
                                   [[1024, 128], [128 * 1024, 8], [1, 128]]))
            for j in range(NSLOT):
                pd = d9_ps.next()
                W("tensor", pd.free)
                if j == 0:
                    W("tensor", sd.ready)
                    for ic in range(8):
                        W("tensor", s_act_ready[ic])
                for ic in range(8):
                    pd.ready = op("tensor", sT,
                                  lambda e, pd=pd.res, sd=sd.res, ic=ic, j=j,
                                  st_=(ic == 0), sp=(ic == 7):
                                  e.matmul(pd[:], sd[:, ts(ic, 128)],
                                           s_act[ic][:, ts(j, 512)],
                                           start=st_, stop=sp))
                W("vector", pd.ready)
                pd.free = op("vector", sV, lambda e, pd=pd.res, c=c, j=j:
                             e.scalar_tensor_tensor(
                    x_res[c][:, ts(j, 512)], pd[:], gb_fm[:, c:c + 1],
                    x_res[c][:, ts(j, 512)], ALU.mult, ALU.add))
                r_dn9 = pd.free
            sd.free = r_dn9
        prev_g9 = r_g9
        prev_dn9 = r_dn9
    ctx.flush()
    b9.close()

    b10 = ExitStack()
    yhl = [b10.enter_context(nc.sbuf_tensor(f"yhl{i}", [128, 256], BF16))
           for i in range(2)]
    y_sb = [b10.enter_context(nc.sbuf_tensor(f"ysb{i}", [128, D], F32))
            for i in range(2)]
    trA = Pool([b10.enter_context(nc.psum_tensor(f"tA{i}", [128, 128], F32))
                for i in range(2)])
    yhl_p = Pool(yhl)
    ysb_slots = [Slot(t) for t in y_sb]
    for i, sl in enumerate(ysb_slots):
        sl.sem = ys_sems[i]
    barrier()
    for jo in range(NSLOT):
        for i in range(4):
            ysl = ysb_slots[i % 2]
            r_tile = None
            for dd in (0, 2, 4, 6):
                hbs = [yhl_p.next(), yhl_p.next()]
                for p in (0, 1):
                    d = dd + p
                    W("vector", hbs[p].free)
                    op("vector", sV, lambda e, hb=hbs[p].res, d=d, jo=jo, i=i:
                       e.tensor_copy(hb[:, 0:128],
                                     x_res[d][:, jo * 512 + i * 128:
                                              jo * 512 + i * 128 + 128]))
                for p in (0, 1):
                    d = dd + p
                    hbs[p].ready = op("vector", sV,
                                      lambda e, hb=hbs[p].res, d=d, jo=jo, i=i:
                                      e.tensor_sub(
                        hb[:, 128:256],
                        x_res[d][:, jo * 512 + i * 128:jo * 512 + i * 128 + 128],
                        hb[:, 0:128]))
                for p in (0, 1):
                    d = dd + p
                    hb = hbs[p]
                    tp = trA.next()
                    W("tensor", hb.ready)
                    W("tensor", tp.free)
                    if d == 0:
                        W("tensor", ysl.free)
                    opq("tensor", lambda e, tp=tp.res, hb=hb.res: e.matmul(
                        tp[:], hb[:, 0:128], ident[:], start=True, stop=False))
                    tp.ready = op("tensor", sT, lambda e, tp=tp.res, hb=hb.res:
                                  e.matmul(tp[:], hb[:, 128:256], ident[:],
                                           start=False, stop=True))
                    hb.free = tp.ready
                    W("scalar", tp.ready)
                    tp.free = op("scalar", sS,
                                 lambda e, tp=tp.res, d=d, ysl=ysl.res:
                                 e.copy(ysl[:, ts(d, 128)], tp[:]))
                    r_tile = tp.free
            ysl.ready = r_tile
            W("sync", ysl.ready)
            ysl.free = dma("sync", ysl.sem,
                           y[jo * 512 + i * 128: jo * 512 + i * 128 + 128, :],
                           ysl.res[:])
    opq("sync", lambda e, v0=ys_sems[0].n, v1=ys_sems[1].n:
        (e.wait_ge(ys_sems[0].h, v0), e.wait_ge(ys_sems[1].h, v1)))
    ctx.flush()
    b10.close()
    root.close()
    return nc


_PROG = None


def _get_prog():
    global _PROG
    if _PROG is None:
        _PROG = build_program()
    return _PROG


def kernel(L0_attn_x, L0_attn_w_qkv, L0_attn_w_out, L0_ffn_router_w,
           L0_ffn_router_bias, L0_ffn_gate_w, L0_ffn_up_w, L0_ffn_down_w,
           L0_ffn_s_gate_w, L0_ffn_s_up_w, L0_ffn_s_down_w, L0_c, L0_wmod):
    bf = ml_dtypes.bfloat16
    x = np.ascontiguousarray(np.asarray(L0_attn_x, dtype=np.float32))
    wqkv_t = np.ascontiguousarray(np.asarray(L0_attn_w_qkv, np.float32).T)
    wout_t = np.ascontiguousarray(np.asarray(L0_attn_w_out, np.float32).T)
    wmod = np.ascontiguousarray(np.asarray(L0_wmod, np.float32))
    cmat = np.ascontiguousarray(np.asarray(L0_c, np.float32))
    rbv = np.asarray(L0_ffn_router_bias, np.float32).reshape(1, 1)
    rwm = np.asarray(L0_ffn_router_w, np.float32)
    rw_fm = np.ascontiguousarray(
        rwm.reshape(8, 128, 8).transpose(1, 0, 2).reshape(128, 64))
    gwm = np.ascontiguousarray(np.asarray(L0_ffn_gate_w, np.float32))
    uwm = np.ascontiguousarray(np.asarray(L0_ffn_up_w, np.float32))
    dwm = np.ascontiguousarray(np.asarray(L0_ffn_down_w, np.float32))
    sgwm = np.ascontiguousarray(np.asarray(L0_ffn_s_gate_w, np.float32))
    suwm = np.ascontiguousarray(np.asarray(L0_ffn_s_up_w, np.float32))
    sdwm = np.ascontiguousarray(np.asarray(L0_ffn_s_down_w, np.float32))

    mask_h = {}
    for half in (0, 1):
        m = np.zeros((NSLOT, 128, 4096), dtype=bf)
        for sl in range(NSLOT):
            gq = OWN_TGS[half][sl]
            for mc in range(8):
                kc = 4 * (EXT[sl] - 2) + mc
                keys = kc * 128 + np.arange(128)[:, None]
                qs = gq * 512 + np.arange(512)[None, :]
                m[sl, :, mc * 512:(mc + 1) * 512] = (keys <= qs).astype(bf)
        mask_h[half] = m

    nc = _get_prog()
    in_maps = []
    for core in range(8):
        b, half = core // 2, core % 2
        tgs = OWN_TGS[half]
        x_b = np.ascontiguousarray(x[b])
        x_own = np.ascontiguousarray(
            np.concatenate([x[b, ts(t, 512), :] for t in tgs], axis=0))
        in_maps.append({
            "x_full": x_b, "x_own": x_own, "wqkv_t": wqkv_t, "wout_t": wout_t,
            "wmod": wmod,
            "c_fm_in": np.ascontiguousarray(cmat[b].reshape(8, 128).T),
            "rb": rbv, "rw": rw_fm, "gwt": gwm, "uwt": uwm, "dwt": dwm,
            "sgw": sgwm, "suw": suwm, "sdw": sdwm, "masks": mask_h[half],
        })
    res = run_bass_kernel_spmd(nc, in_maps, core_ids=list(range(8)))
    out = np.empty((B, T, D), dtype=np.float32)
    for core in range(8):
        b, half = core // 2, core % 2
        yv = res.results[core]["y"]
        for sl, tg in enumerate(OWN_TGS[half]):
            out[b, ts(tg, 512), :] = yv[ts(sl, 512), :]
    return out


# version 23
# speedup vs baseline: 1.1692x; 1.1600x over previous
import numpy as np
import ml_dtypes
from contextlib import ExitStack

import concourse.bass as bass
import concourse.mybir as mybir
from concourse.bass_utils import run_bass_kernel_spmd

F32 = mybir.dt.float32
BF16 = mybir.dt.bfloat16
AX = mybir.AxisListType.X
ALU = mybir.AluOpType
ACT = mybir.ActivationFunctionType

B, T, D = 4, 4096, 1024
NTOK = 2048
NSLOT = 4
EXT = [2, 4, 6, 8]
OWN_TGS = {0: [0, 3, 4, 7], 1: [1, 2, 5, 6]}
SHIFT = 20.0
NEXP, IEXP, ISH = 8, 512, 2048
EPS = 1e-5


def ts(i, n):
    return slice(i * n, (i + 1) * n)


class Sem:
    def __init__(self, stk, nc, name):
        self.h = stk.enter_context(nc.semaphore(name))
        self.n = 0


class Ctx:
    ENGINES = ("sync", "gpsimd", "tensor", "vector", "scalar")

    def __init__(self, nc):
        self.nc = nc
        self.ops = {e: [] for e in self.ENGINES}

    def emit(self, eng, fn):
        self.ops[eng].append(fn)

    def wait(self, eng, dep):
        if dep is None:
            return
        sem, val = dep
        if val <= 0:
            return
        assert val <= sem.n, f"wait {val} > emitted {sem.n} on {eng}"
        self.ops[eng].append(lambda e, s=sem.h, v=val: e.wait_ge(s, v))

    def flush(self):
        with self.nc.Block() as blk:
            for name in self.ENGINES:
                lst = self.ops[name]
                if lst:
                    getattr(blk, name)(
                        lambda e, _l=tuple(lst): [f(e) for f in _l])
        self.ops = {e: [] for e in self.ENGINES}


class Slot:
    def __init__(self, res, sem=None):
        self.res = res
        self.sem = sem
        self.free = None
        self.ready = None


class Pool:
    def __init__(self, resources):
        self.slots = [Slot(r) for r in resources]
        self.i = 0

    def next(self):
        s = self.slots[self.i % len(self.slots)]
        self.i += 1
        return s


def build_program():
    nc = bass.Bass("TRN2", target_bir_lowering=False, debug=False,
                   detect_race_conditions=False)
    dri = lambda name, shape, dt=F32: nc.dram_tensor(
        name, shape, dt, kind="ExternalInput").ap()
    x_full = dri("x_full", [T, D])
    x_own = dri("x_own", [NTOK, D])
    wqkv_t = dri("wqkv_t", [D, 3 * D])
    wout_t = dri("wout_t", [D, D])
    wmod = dri("wmod", [D, 6 * D])
    c_fm_in = dri("c_fm_in", [128, 8])
    rb = dri("rb", [1, 1])
    rw = dri("rw", [128, 64])
    gwt = dri("gwt", [NEXP, D, IEXP])
    uwt = dri("uwt", [NEXP, D, IEXP])
    dwt = dri("dwt", [NEXP, IEXP, D])
    sgw = dri("sgw", [D, ISH])
    suw = dri("suw", [D, ISH])
    sdw = dri("sdw", [ISH, D])
    masks = dri("masks", [NSLOT, 128, 4096], BF16)
    y = nc.dram_tensor("y", [NTOK, D], F32, kind="ExternalOutput").ap()
    mod_dram = nc.dram_tensor("mod_scratch", [1, 4 * D], F32, kind="Internal").ap()

    root = ExitStack()
    rsb = lambda n, s, d=F32: root.enter_context(nc.sbuf_tensor(n, s, d))

    K_bf = [rsb(f"K{p}", [128, T], BF16) for p in range(4)]
    V_bf = rsb("Vb", [128, 32 * 520], BF16)
    attnout = [rsb(f"ao{d}", [128, NTOK], BF16) for d in range(8)]
    ident = rsb("ident", [128, 128], BF16)
    ones_bf = rsb("ones", [128, 512], BF16)
    brow = rsb("brow", [1, 3 * D], BF16)
    cbp1_tm = rsb("cbp1", [128, D])
    sbm_tm = rsb("sbm", [128, D], BF16)
    cap1_fm = rsb("cap1", [128, 8])
    ga_fm = rsb("ga", [128, 8])
    gb_fm = rsb("gb", [128, 8])
    neg20 = rsb("neg20", [128, 1])
    epsc = rsb("epsc", [128, 1])
    rbcol = rsb("rbcol", [128, 1])
    wsel_bf = rsb("wsel", [128, 128], BF16)
    sa_bf = rsb("sa_bf", [128, 8], BF16)
    x_res = [K_bf[d][:].bitcast(F32) for d in range(4)] + \
            [V_bf[:].bitcast(F32)[:, ts(i, NTOK)] for i in range(4)]
    h2_fm = attnout

    ctx = Ctx(nc)
    sG = Sem(root, nc, "sG")
    sT = Sem(root, nc, "sT")
    sV = Sem(root, nc, "sV")
    sS = Sem(root, nc, "sS")
    s_gp0 = Sem(root, nc, "s_gp0")
    s_wqkv = Sem(root, nc, "s_wqkv")
    s_wsl = Sem(root, nc, "s_wsl")
    s_sgu = Sem(root, nc, "s_sgu")
    wm_sems = [Sem(root, nc, f"s_wm{i}") for i in range(4)]
    xa_sems = [Sem(root, nc, f"s_xa{i}") for i in range(4)]
    s_msk = Sem(root, nc, "s_msk")
    gu_sems = [Sem(root, nc, f"s_gu{i}") for i in range(2)]
    dw_sems = [Sem(root, nc, f"s_dwe{i}") for i in range(2)]
    dc_sems = [Sem(root, nc, f"s_dc{i}") for i in range(2)]
    sd_sems = [Sem(root, nc, f"s_sd{i}") for i in range(2)]
    ys_sems = [Sem(root, nc, f"s_ys{i}") for i in range(2)]
    s_bc = Sem(root, nc, "s_bc")

    def dma(eng, sem, out, in_, slow=False):
        def f(e, o=out, i=in_, s=sem.h, sl=slow):
            kw = {"allow_slow_non_contiguous": True} if sl else {}
            e.dma_start(o, i, **kw).then_inc(s, 16)
        ctx.emit(eng, f)
        sem.n += 16
        return (sem, sem.n)

    def op(eng, sem, fn):
        def f(e, fn=fn, s=sem.h):
            fn(e).then_inc(s, 1)
        ctx.emit(eng, f)
        sem.n += 1
        return (sem, sem.n)

    def opq(eng, fn):
        ctx.emit(eng, fn)

    W = ctx.wait

    def _recl(e, out_ap, in_ap):
        with nc.allow_low_precision(reason="softmax denom recip to bf16"):
            return e.reciprocal(out_ap, in_ap)

    def barrier():
        snap = [(sT, sT.n), (sV, sV.n), (sS, sS.n)]
        for eng in Ctx.ENGINES:
            for dep in snap:
                W(eng, dep)

    b0 = ExitStack()
    bsb = lambda n, s, d=F32: b0.enter_context(nc.sbuf_tensor(n, s, d))
    bps = lambda n, s: b0.enter_context(nc.psum_tensor(n, s, F32))
    io_col = bsb("io_col", [128, 128])
    io_par = bsb("io_par", [128, 1])
    c_fm = bsb("c_fm", [128, 8])
    sil = bsb("sil", [128, 8])
    sil_hi = bsb("sil_hi", [128, 8], BF16)
    sil_lo = bsb("sil_lo", [128, 8], BF16)
    rep_hi = [bsb(f"rph{d}", [128, 128], BF16) for d in range(8)]
    rep_lo = [bsb(f"rpl{d}", [128, 128], BF16) for d in range(8)]
    wm_st = [bsb(f"wm{i}", [128, 512]) for i in range(4)]
    wm_hl = [bsb(f"wmh{i}", [128, 1024], BF16) for i in range(2)]
    modrow = bsb("modrow", [1, 4 * D])
    sga = bsb("sga", [128, 8])
    sgb = bsb("sgb", [128, 8])
    aba = bsb("aba", [128, 8])
    abb = bsb("abb", [128, 8])
    rb_sb = bsb("rb_sb", [1, 1], BF16)
    m32 = bsb("m32", [32, 128])
    m32h = bsb("m32h", [32, 128], BF16)
    m32l = bsb("m32l", [32, 128], BF16)
    mfm = bsb("mfm", [128, 32])
    mfm_ps = b0.enter_context(nc.psum_tensor("pmfm", [128, 32], F32))
    mod_ps = Pool([bps(f"p0_{i}", [128, 512]) for i in range(2)])
    rbc_ps = bps("prbc", [128, 512])

    d_iota = op("gpsimd", sG, lambda e: e.iota(
        io_col[:], [[1, 128]], channel_multiplier=0,
        allow_small_or_imprecise_dtypes=True))
    d_iota = op("gpsimd", sG, lambda e: e.iota(
        io_par[:], [[0, 1]], channel_multiplier=1,
        allow_small_or_imprecise_dtypes=True))
    W("vector", d_iota)
    r_ident = op("vector", sV, lambda e: e.tensor_scalar(
        ident[:], io_col[:], io_par[:, 0:1], None, ALU.is_equal))
    op("vector", sV, lambda e: e.memset(ones_bf[:], 1.0))
    op("vector", sV, lambda e: e.memset(epsc[:], EPS))
    r_ones = op("vector", sV, lambda e: e.memset(neg20[:], -SHIFT))

    dma("gpsimd", s_gp0, c_fm[:, 0:8], c_fm_in[:, :])
    d_crb = dma("gpsimd", s_gp0, rb_sb[0:1, 0:1], rb[0:1, 0:1])
    W("scalar", d_crb)
    r_sil = op("scalar", sS, lambda e: e.activation(sil[:], c_fm[:], ACT.Silu))
    W("vector", r_sil)
    a = op("vector", sV, lambda e: e.tensor_copy(sil_hi[:], sil[:]))
    W("vector", a)
    for d in range(8):
        r_rep = op("vector", sV, lambda e, d=d: e.tensor_copy(
            rep_hi[d][:], sil_hi[:, d:d + 1].to_broadcast([128, 128])))

    st_slots = Pool(wm_st)
    for i, sl in enumerate(st_slots.slots):
        sl.sem = wm_sems[i]
        sl.q = "sync" if i < 2 else "gpsimd"
    hl_slots = Pool(wm_hl)
    r_mod = {}
    for g in range(12):
        mp = mod_ps.next()
        W("tensor", mp.free)
        for d in range(8):
            k = g * 8 + d
            st = st_slots.next()
            W(st.q, st.free)
            st.ready = dma(st.q, st.sem, st.res[:], wmod[ts(d, 128), ts(g, 512)])
            hl = hl_slots.next()
            W("vector", st.ready)
            W("vector", hl.free)
            hl.ready = op("vector", sV, lambda e, a=hl.res, b=st.res:
                          e.tensor_copy(a[:, 0:512], b[:]))
            st.free = hl.ready
            W("tensor", hl.ready)
            if k == 0:
                W("tensor", r_rep)
            hl.free = op("tensor", sT, lambda e, d=d, a=hl.res, mp=mp.res,
                         st_=(d == 0), sp=(d == 7):
                         e.matmul(mp[:], rep_hi[d][:], a[:, 0:512],
                                  start=st_, stop=sp))
        mp.ready = hl.free
        m, hh = g // 2, g % 2
        W("vector", mp.ready)
        if m == 0:
            ev = op("vector", sV, lambda e, p=mp.res, hh=hh:
                    e.tensor_copy(modrow[0:1, ts(hh, 512)], p[0:1, :]))
        elif m == 1:
            ev = op("vector", sV, lambda e, p=mp.res, hh=hh: e.tensor_copy(
                modrow[0:1, 1024 + hh * 512:1024 + hh * 512 + 512], p[0:1, :]))
        elif m == 2:
            ev = op("vector", sV, lambda e, p=mp.res, hh=hh: e.tensor_copy(
                modrow[0:1, 2048 + hh * 512:2048 + hh * 512 + 512], p[0:1, :]))
        elif m == 3:
            ev = op("vector", sV, lambda e, p=mp.res, hh=hh:
                    e.tensor_copy(sbm_tm[:, ts(hh, 512)], p[:]))
        elif m == 4:
            ev = op("vector", sV, lambda e, p=mp.res, hh=hh: e.tensor_scalar(
                cbp1_tm[:, ts(hh, 512)], p[:], 1.0, None, ALU.add))
        else:
            ev = op("vector", sV, lambda e, p=mp.res, hh=hh: e.tensor_copy(
                modrow[0:1, 3072 + hh * 512:3072 + hh * 512 + 512], p[0:1, :]))
        mp.free = ev
        r_mod[m * 2 + hh] = ev

    W("gpsimd", r_mod[11])
    d_bounce = dma("gpsimd", s_gp0, mod_dram[0:1, :], modrow[0:1, :])
    W("gpsimd", d_bounce)
    d_m32 = dma("gpsimd", s_gp0, m32[:, :],
                bass.AP(mod_dram.tensor, 0, [[128, 32], [1, 128]]))
    W("vector", d_m32)
    r_m32 = op("vector", sV, lambda e: e.tensor_copy(m32h[:], m32[:]))
    W("tensor", r_m32)
    W("tensor", r_ident)
    t_mfm = op("tensor", sT, lambda e: e.matmul(mfm_ps[:], m32h[:],
                                                ident[0:32, 0:32],
                                                start=True, stop=True))
    W("vector", t_mfm)
    r_mfm = op("vector", sV, lambda e: e.tensor_copy(mfm[:], mfm_ps[0:128, 0:32]))
    W("vector", r_mfm)
    r_cap1 = op("vector", sV, lambda e: e.tensor_scalar(
        cap1_fm[:], mfm[:, 8:16], 1.0, None, ALU.add))
    r_sabf = op("vector", sV, lambda e: e.tensor_copy(sa_bf[:], mfm[:, 0:8]))
    op("vector", sV, lambda e: e.tensor_copy(ga_fm[:], mfm[:, 16:24]))
    r_gcp = op("vector", sV, lambda e: e.tensor_copy(gb_fm[:], mfm[:, 24:32]))
    W("scalar", r_gcp)
    op("scalar", sS, lambda e: e.activation(sga[:], ga_fm[:], ACT.Sign))
    op("scalar", sS, lambda e: e.activation(sgb[:], gb_fm[:], ACT.Sign))
    op("scalar", sS, lambda e: e.activation(aba[:], ga_fm[:], ACT.Abs))
    a_gab = op("scalar", sS, lambda e: e.activation(abb[:], gb_fm[:], ACT.Abs))
    W("vector", a_gab)
    op("vector", sV, lambda e: e.tensor_scalar(
        aba[:], aba[:], 0.25, None, ALU.max))
    a = op("vector", sV, lambda e: e.tensor_scalar(
        abb[:], abb[:], 0.25, None, ALU.max))
    W("vector", a)
    op("vector", sV, lambda e: e.tensor_mul(ga_fm[:], sga[:], aba[:]))
    r_gab = op("vector", sV, lambda e: e.tensor_mul(gb_fm[:], sgb[:], abb[:]))

    W("tensor", d_crb)
    W("tensor", r_ones)
    t_rbc = op("tensor", sT, lambda e: e.matmul(
        rbc_ps[0:128, 0:1], ones_bf[0:1, 0:128], rb_sb[0:1, 0:1],
        start=True, stop=True))
    W("vector", t_rbc)
    op("vector", sV, lambda e: e.tensor_copy(rbcol[:], rbc_ps[0:128, 0:1]))

    ctx.flush()
    b0.close()
    barrier()

    sa_stk = ExitStack()
    asb = lambda n, s, d=F32: sa_stk.enter_context(nc.sbuf_tensor(n, s, d))
    Q_bf = [asb(f"Q{p}", [128, NTOK], BF16) for p in range(4)]
    wsl = [asb(f"wsl{d}", [128, 1536], BF16) for d in range(8)]
    xa = [asb(f"xa{i}", [128, D]) for i in range(4)]
    xnb = [asb(f"xn{i}", [128, D], BF16) for i in range(2)]
    xfm = [[asb(f"xf{s}_{d}", [128, 512], BF16) for d in range(8)]
           for s in range(2)]
    st8 = asb("st8", [128, 32])
    P_sb = [asb(f"P{i}", [128, 1024], BF16) for i in range(4)]
    msk_sb = asb("msk", [128, 4096], BF16)
    avb_st = [asb(f"avst{i}", [64, 512], BF16) for i in range(2)]
    bc2 = [asb(f"bc2_{i}", [64, 512], BF16) for i in range(2)]
    rec1 = asb("rec1", [1, 512], BF16)

    xofm = [Q_bf[p][:].bitcast(F32) for p in range(4)]

    def ln_group(tiles):
        n = len(tiles)
        for k, (xt, xnt, st, dep, nfree) in enumerate(tiles):
            W("vector", dep)
            op("vector", sV, lambda e, xt=xt, st=st:
               e.reduce_sum(st[:, 0:1], xt[:], AX))
        a = None
        for k, (xt, xnt, st, dep, nfree) in enumerate(tiles):
            W("scalar", dep)
            W("scalar", nfree)
            a = op("scalar", sS, lambda e, xt=xt, xnt=xnt, st=st: e.activation(
                xnt[:], xt[:], ACT.Square, accum_out=st[:, 1:2]))
        W("vector", a)
        for k, (xt, xnt, st, dep, nfree) in enumerate(tiles):
            op("vector", sV, lambda e, st=st: e.tensor_scalar(
                st[:, 5:6], st[:, 1:2], 1.0 / D, EPS, ALU.mult, ALU.add))
        a = None
        for k, (xt, xnt, st, dep, nfree) in enumerate(tiles):
            a = op("vector", sV, lambda e, st=st: e.tensor_scalar(
                st[:, 2:3], st[:, 0:1], 1.0 / D, None, ALU.mult))
        W("scalar", a)
        for k, (xt, xnt, st, dep, nfree) in enumerate(tiles):
            a = op("scalar", sS, lambda e, st=st: e.activation(
                st[:, 3:4], st[:, 2:3], ACT.Square))
        W("vector", a)
        for k, (xt, xnt, st, dep, nfree) in enumerate(tiles):
            a = op("vector", sV, lambda e, st=st: e.scalar_tensor_tensor(
                st[:, 4:5], st[:, 3:4], -1.0, st[:, 5:6], ALU.mult, ALU.add))
        W("scalar", a)
        for k, (xt, xnt, st, dep, nfree) in enumerate(tiles):
            a = op("scalar", sS, lambda e, st=st: e.activation(
                st[:, 6:7], st[:, 4:5], ACT.Sqrt))
        W("vector", a)
        for k, (xt, xnt, st, dep, nfree) in enumerate(tiles):
            a = op("vector", sV, lambda e, st=st: e.reciprocal(
                st[:, 7:8], st[:, 6:7]))
        if n < 4:
            W("vector", a)
        fin = [None] * n
        for k, (xt, xnt, st, dep, nfree) in enumerate(tiles):
            W("vector", nfree)
            fin[k] = op("vector", sV, lambda e, xt=xt, xnt=xnt, st=st:
                        e.scalar_tensor_tensor(
                xnt[:], xt[:], st[:, 2:3],
                st[:, 7:8].to_broadcast([128, D]), ALU.subtract, ALU.mult))
        return fin

    def ln_tile(xt, xnt, stats, dep_x):
        return ln_group([(xt, xnt, stats, dep_x, None)])[0]

    xa_p = Pool(xa)
    for i, sl in enumerate(xa_p.slots):
        sl.sem = xa_sems[i]
    xn_p = Pool(xnb + [msk_sb[:, ts(i, 1024)] for i in range(4)])
    for hg in range(2):
        barrier()
        pa = ExitStack()
        tr_ps = Pool([pa.enter_context(nc.psum_tensor(f"tr{hg}_{i}", [128, 128], F32))
                      for i in range(2)])
        kv_ps = Pool([pa.enter_context(nc.psum_tensor(f"kv{hg}_{i}", [128, 512], F32))
                      for i in range(4)])

        d_w = None
        for d in range(8):
            dma("gpsimd", s_wsl, wsl[d][:, 0:512],
                wqkv_t[ts(d, 128), hg * 512:hg * 512 + 512])
            dma("gpsimd", s_wsl, wsl[d][:, 512:1024],
                wqkv_t[ts(d, 128), 1024 + hg * 512:1024 + hg * 512 + 512])
            d_w = dma("gpsimd", s_wsl, wsl[d][:, 1024:1536],
                      wqkv_t[ts(d, 128), 2048 + hg * 512:2048 + hg * 512 + 512])
        br_ps = Pool([pa.enter_context(nc.psum_tensor(f"br{hg}_{i}", [1, 512],
                                                      F32)) for i in range(2)])
        W("tensor", d_w)
        W("tensor", r_sabf)
        r_brow = None
        t_brow = None
        for part in range(3):
            bp = br_ps.next()
            W("tensor", bp.free)
            for d in range(8):
                t_brow = op("tensor", sT,
                            lambda e, d=d, part=part, bp=bp.res:
                            e.matmul(bp[:], sa_bf[:, d:d + 1],
                                     wsl[d][:, ts(part, 512)],
                                     start=(d == 0), stop=(d == 7)))
            bp.ready = t_brow
            W("vector", bp.ready)
            dstb = [hg * 512, 1024 + hg * 512, 2048 + hg * 512][part]
            if part == 0:
                ev = op("vector", sV, lambda e, bp=bp.res, dstb=dstb:
                        e.tensor_scalar(brow[0:1, dstb:dstb + 512],
                                        bp[0:1, :], 0.125, None, ALU.mult))
            else:
                ev = op("vector", sV, lambda e, bp=bp.res, dstb=dstb:
                        e.tensor_copy(brow[0:1, dstb:dstb + 512], bp[0:1, :]))
            bp.free = ev
            r_brow = ev
        W("vector", t_brow)
        W("vector", d_w)
        W("vector", r_cap1)
        for d in range(8):
            op("vector", sV, lambda e, d=d: e.tensor_mul(
                wsl[d][:], wsl[d][:],
                cap1_fm[:, d:d + 1].to_broadcast([128, 1536])))
            r_wsl = op("vector", sV, lambda e, d=d: e.tensor_scalar(
                wsl[d][:, 0:512], wsl[d][:, 0:512], 0.125, None, ALU.mult))
        for kc in range(32):
            r_vones = op("vector", sV, lambda e, kc=kc: e.memset(
                bass.AP(V_bf, 520 * kc + 64, [[32 * 520, 128], [65, 8], [1, 1]]),
                1.0))

        xfm_ready = [None, None]
        xfm_free = [None, None]
        for grp in range(12):
            qpass = grp >= 8
            s2 = grp % 2
            group, xss, nss = [], [], []
            for i in range(4):
                it = grp * 4 + i
                src = (x_own[ts(it - 32, 128), :] if qpass
                       else x_full[ts(it, 128), :])
                xs = xa_p.next()
                W("sync", xs.free)
                xs.ready = dma("sync", xs.sem, xs.res[:], src)
                ns = xn_p.next()
                xss.append(xs)
                nss.append(ns)
                group.append((xs.res, ns.res, st8[:, 8 * i:8 * i + 8],
                              xs.ready, ns.free))
            fins = ln_group(group)
            for i in range(4):
                xss[i].free = fins[i]
            if grp == 0:
                W("tensor", r_wsl)
                W("tensor", r_vones)
                W("tensor", r_brow)
                W("tensor", r_ident)
            W("tensor", xfm_free[s2])
            for i in range(4):
                ns = nss[i]
                W("tensor", fins[i])
                for d in range(8):
                    tp = tr_ps.next()
                    W("tensor", tp.free)
                    tp.ready = op("tensor", sT,
                                  lambda e, tp=tp.res, x=ns.res, d=d:
                                  e.matmul(tp[:], x[:, ts(d, 128)], ident[:],
                                           start=True, stop=True))
                    W("scalar", tp.ready)
                    tp.free = op("scalar", sS,
                                 lambda e, tp=tp.res, d=d, s2=s2, i=i:
                                 e.copy(xfm[s2][d][:, ts(i, 128)], tp[:]))
                    r_evac = tp.free
                ns.free = r_evac
            xfm_ready[s2] = r_evac
            W("tensor", xfm_ready[s2])
            r_cons = None
            if not qpass:
                tg = grp
                for p in range(4):
                    kp = kv_ps.next()
                    W("tensor", kp.free)
                    for d in range(8):
                        opq("tensor", lambda e, kp=kp.res, d=d, p=p, s2=s2:
                            e.matmul(kp[:], wsl[d][:, 512 + p * 128:512 + p * 128 + 128],
                                     xfm[s2][d][:], start=(d == 0), stop=False))
                    kp.ready = op("tensor", sT, lambda e, kp=kp.res, p=p, hg=hg:
                                  e.matmul(kp[:],
                                           brow[0:1, 1024 + hg * 512 + p * 128:
                                                1024 + hg * 512 + p * 128 + 128],
                                           ones_bf[0:1, :], start=False, stop=True))
                    W("vector", kp.ready)
                    kp.free = op("vector", sV, lambda e, kp=kp.res, p=p, tg=tg:
                                 e.tensor_copy(K_bf[p][:, ts(tg, 512)], kp[:]))
                    r_cons = kp.free
                for kk in range(4):
                    kc = tg * 4 + kk
                    vp = kv_ps.next()
                    W("tensor", vp.free)
                    for d in range(8):
                        opq("tensor", lambda e, vp=vp.res, d=d, kk=kk, s2=s2:
                            e.matmul(vp[:], xfm[s2][d][:, ts(kk, 128)],
                                     wsl[d][:, 1024:1536], start=(d == 0), stop=False))
                    vp.ready = op("tensor", sT, lambda e, vp=vp.res, hg=hg:
                                  e.matmul(vp[:], ones_bf[0:1, 0:128],
                                           brow[0:1, 2048 + hg * 512:
                                                2048 + hg * 512 + 512],
                                           start=False, stop=True))
                    W("vector", vp.ready)
                    vp.free = op("vector", sV, lambda e, vp=vp.res, kc=kc:
                                 e.tensor_copy(
                        bass.AP(V_bf, 520 * kc, [[32 * 520, 128], [65, 8], [1, 64]]),
                        vp[:]))
                    r_cons = vp.free
            else:
                jo = grp - 8
                for p in range(4):
                    qp = kv_ps.next()
                    W("tensor", qp.free)
                    for d in range(8):
                        opq("tensor", lambda e, qp=qp.res, d=d, p=p, s2=s2:
                            e.matmul(qp[:], wsl[d][:, p * 128:p * 128 + 128],
                                     xfm[s2][d][:], start=(d == 0), stop=False))
                    qp.ready = op("tensor", sT, lambda e, qp=qp.res, p=p, hg=hg:
                                  e.matmul(qp[:],
                                           brow[0:1, hg * 512 + p * 128:
                                                hg * 512 + p * 128 + 128],
                                           ones_bf[0:1, :], start=False, stop=True))
                    W("vector", qp.ready)
                    qp.free = op("vector", sV, lambda e, qp=qp.res, p=p, jo=jo:
                                 e.tensor_copy(Q_bf[p][:, ts(jo, 512)], qp[:]))
                    r_cons = qp.free
            xfm_free[s2] = r_cons
        ctx.flush()
        pa.close()
        barrier()

        pa2 = ExitStack()
        sc_ps = Pool([pa2.enter_context(nc.psum_tensor(f"sc{hg}_{i}", [128, 1024], F32))
                      for i in range(3)])
        av_ps = Pool([pa2.enter_context(nc.psum_tensor(f"av{hg}_{i}", [65, 512], F32))
                      for i in range(2)])
        P_p = Pool(P_sb)
        msk_slot = Slot(msk_sb)

        for j in range(NSLOT):
            W("sync", msk_slot.free)
            msk_slot.ready = dma("sync", s_msk, msk_sb[:], masks[j, :, :])
            r_mask_last = None
            npair = 2 * EXT[j]
            for hp in range(4):
                hls = (2 * hp, 2 * hp + 1)
                avs, scs, pbs = {}, {}, {}
                for hl in hls:
                    avs[hl] = av_ps.next()
                def emit_scs(pc):
                    kc0, kc1 = 2 * pc, 2 * pc + 1
                    out = {}
                    for hl in hls:
                        p, par = hl // 2, hl % 2
                        pr = slice(64 * par, 64 * par + 64)
                        sc = sc_ps.next()
                        out[hl] = sc
                        W("tensor", sc.free)
                        opq("tensor",
                            lambda e, sc=sc.res, p=p, pr=pr, kc0=kc0, j=j:
                            e.matmul(sc[:, 0:512], K_bf[p][pr, ts(kc0, 128)],
                                     Q_bf[p][pr, ts(j, 512)],
                                     start=True, stop=True))
                        sc.ready = op("tensor", sT,
                                      lambda e, sc=sc.res, p=p, pr=pr,
                                      kc1=kc1, j=j:
                                      e.matmul(sc[:, 512:1024],
                                               K_bf[p][pr, ts(kc1, 128)],
                                               Q_bf[p][pr, ts(j, 512)],
                                               start=True, stop=True))
                    return out

                scs_next = emit_scs(0)
                for pc in range(npair):
                    kc0, kc1 = 2 * pc, 2 * pc + 1
                    scs = scs_next
                    for hl in hls:
                        sc = scs[hl]
                        pb = P_p.next()
                        pbs[hl] = pb
                        W("scalar", sc.ready)
                        W("scalar", pb.free)
                        pb.ready = op("scalar", sS,
                                      lambda e, sc=sc.res, pb=pb.res:
                                      e.activation(pb[:], sc[:], ACT.Exp,
                                                   bias=neg20[:, 0:1]))
                        sc.free = pb.ready
                    scs_next = emit_scs(pc + 1) if pc + 1 < npair else None
                    masked = pc >= npair - 4
                    if masked:
                        mc = pc - (npair - 4)
                        for hl in hls:
                            pb = pbs[hl]
                            W("vector", pb.ready)
                            if hp == 0 and hl == hls[0] and mc == 0:
                                W("vector", msk_slot.ready)
                            pb.ready = op("vector", sV,
                                          lambda e, pb=pb.res, mc=mc:
                                          e.tensor_mul(pb[:], pb[:],
                                                       msk_sb[:, ts(mc, 1024)]))
                            r_mask_last = pb.ready
                    for hl in hls:
                        av, pb = avs[hl], pbs[hl]
                        if pc == 0:
                            W("tensor", av.free)
                        W("tensor", pb.ready)
                        opq("tensor",
                            lambda e, av=av.res, hl=hl, kc0=kc0, pb=pb.res,
                            st_=(pc == 0):
                            e.matmul(av[:], bass.AP(V_bf, 520 * kc0 + 65 * hl,
                                                    [[32 * 520, 128], [1, 65]]),
                                     pb[:, 0:512], start=st_, stop=False))
                        pb.free = op("tensor", sT,
                                     lambda e, av=av.res, hl=hl, kc1=kc1,
                                     pb=pb.res, sp=(pc == npair - 1):
                                     e.matmul(av[:],
                                              bass.AP(V_bf, 520 * kc1 + 65 * hl,
                                                      [[32 * 520, 128], [1, 65]]),
                                              pb[:, 512:1024],
                                              start=False, stop=sp))
                        av.ready = pb.free
                avb = [avb_st[0], avb_st[1]]
                for q, hl in enumerate(hls):
                    av = avs[hl]
                    W("vector", av.ready)
                    op("vector", sV, lambda e, av=av.res, b=avb[q]:
                       e.tensor_copy(b[:], av[0:64, :]))
                    r_st = op("vector", sV, lambda e, av=av.res: _recl(
                        e, rec1[0:1, :], av[64:65, :]))
                    av.free = r_st
                    W("gpsimd", r_st)
                    d_bc = dma("gpsimd", s_bc, bc2[q][:],
                               bass.AP(rec1, 0, [[1, 1], [0, 64], [1, 512]]))
                    W("vector", d_bc)
                    hd = (hg * 8 + hl) // 2
                    dst = slice(64 * ((hg * 8 + hl) % 2),
                                64 * ((hg * 8 + hl) % 2) + 64)
                    op("vector", sV,
                       lambda e, b=avb[q], hd=hd, dst=dst, j=j, q=q:
                       e.tensor_mul(attnout[hd][dst, ts(j, 512)],
                                    b[:], bc2[q][:]))
            msk_slot.free = r_mask_last
        ctx.flush()
        pa2.close()

    pb5 = ExitStack()
    tr5 = Pool([pb5.enter_context(nc.psum_tensor(f"t5{i}", [128, 128], F32))
                for i in range(2)])
    pr5 = Pool([pb5.enter_context(nc.psum_tensor(f"p5{i}", [128, 512], F32))
                for i in range(2)])
    xhl5_p = Pool([msk_sb[:, 0:2 * D], msk_sb[:, 2 * D:4 * D]])
    wout_bf = [wsl[d][:, 0:1024] for d in range(8)]
    barrier()
    d_wo = None
    for d in range(8):
        d_wo = dma("gpsimd", s_wsl, wout_bf[d], wout_t[ts(d, 128), :])

    for jo in range(NSLOT):
        r_xo = None
        for i in range(4):
            it = jo * 4 + i
            xs = xa_p.next()
            W("sync", xs.free)
            xs.ready = dma("sync", xs.sem, xs.res[:], x_own[ts(it, 128), :])
            W("vector", xs.ready)
            hb = xhl5_p.next()
            W("vector", hb.free)
            op("vector", sV, lambda e, x=xs.res, h=hb.res:
               e.tensor_copy(h[:, 0:D], x[:]))
            r_hl = op("vector", sV, lambda e, x=xs.res, h=hb.res:
                      e.tensor_sub(h[:, D:2 * D], x[:], h[:, 0:D]))
            xs.free = r_hl
            W("tensor", r_hl)
            if it == 0:
                W("tensor", d_wo)
            for d in range(8):
                tp = tr5.next()
                W("tensor", tp.free)
                opq("tensor", lambda e, tp=tp.res, d=d, h=hb.res: e.matmul(
                    tp[:], h[:, ts(d, 128)], ident[:], start=True, stop=False))
                tp.ready = op("tensor", sT, lambda e, tp=tp.res, d=d, h=hb.res:
                              e.matmul(tp[:], h[:, D + d * 128:D + d * 128 + 128],
                                       ident[:], start=False, stop=True))
                hb.free = tp.ready
                W("scalar", tp.ready)
                tp.free = op("scalar", sS, lambda e, tp=tp.res, d=d, i=i: e.copy(
                    xofm[d // 2][:, (d % 2) * 512 + i * 128:
                                 (d % 2) * 512 + i * 128 + 128], tp[:]))
                r_xo = tp.free
        W("tensor", r_xo)
        for c in range(8):
            pp = pr5.next()
            W("tensor", pp.free)
            for d in range(8):
                pp.ready = op("tensor", sT, lambda e, pp=pp.res, c=c, d=d, jo=jo:
                              e.matmul(pp[:], wout_bf[d][:, ts(c, 128)],
                                       attnout[d][:, ts(jo, 512)],
                                       start=(d == 0), stop=(d == 7)))
            W("vector", pp.ready)
            pp.free = op("vector", sV, lambda e, pp=pp.res, c=c, jo=jo:
                         e.scalar_tensor_tensor(
                x_res[c][:, ts(jo, 512)], pp[:], ga_fm[:, c:c + 1],
                xofm[c // 2][:, (c % 2) * 512:(c % 2) * 512 + 512],
                ALU.mult, ALU.add))
    ctx.flush()
    pb5.close()
    sa_stk.close()

    b6 = ExitStack()
    b6sb = lambda n, s, d=F32: b6.enter_context(nc.sbuf_tensor(n, s, d))
    x2tm = [b6sb(f"x2_{i}", [128, D]) for i in range(2)]
    h2tm = [b6sb(f"h2_{i}", [128, D], BF16) for i in range(2)]
    hi6 = [b6sb(f"hi6_{p}", [128, 128], BF16) for p in range(2)]
    lo6 = [b6sb(f"lo6_{p}", [128, 128], BF16) for p in range(2)]
    st6 = b6sb("st6", [128, 16])
    rw_bf = b6sb("rw_bf", [128, 64], BF16)
    sm8 = b6sb("sm8", [128, 192])
    tr6 = Pool([b6.enter_context(nc.psum_tensor(f"t6{i}", [128, 128], F32))
                for i in range(2)])
    lg6 = Pool([b6.enter_context(nc.psum_tensor(f"l6{i}", [128, 8], F32))
                for i in range(4)])

    barrier()
    d_rw = dma("gpsimd", s_gp0, rw_bf[:, 0:64], rw[:, :])

    x2_p = Pool(x2tm)
    h2_p = Pool(h2tm)
    r_h2last = None
    for it in range(16):
        jo, i = divmod(it, 4)
        x2 = x2_p.next()
        h2s = h2_p.next()
        r_tx2 = [None, None]
        for dd in (0, 2, 4, 6):
            r_hl = [None, None]
            for p in (0, 1):
                d = dd + p
                W("vector", r_tx2[p])
                op("vector", sV, lambda e, d=d, jo=jo, i=i, p=p: e.tensor_copy(
                    hi6[p][:],
                    x_res[d][:, jo * 512 + i * 128:jo * 512 + i * 128 + 128]))
            for p in (0, 1):
                d = dd + p
                r_hl[p] = op("vector", sV, lambda e, d=d, jo=jo, i=i, p=p:
                             e.tensor_sub(
                    lo6[p][:],
                    x_res[d][:, jo * 512 + i * 128:jo * 512 + i * 128 + 128],
                    hi6[p][:]))
            for p in (0, 1):
                d = dd + p
                tp = tr6.next()
                W("tensor", r_hl[p])
                W("tensor", tp.free)
                W("tensor", x2.free)
                opq("tensor", lambda e, tp=tp.res, p=p: e.matmul(
                    tp[:], hi6[p][:], ident[:], start=True, stop=False))
                tp.ready = op("tensor", sT, lambda e, tp=tp.res, p=p: e.matmul(
                    tp[:], lo6[p][:], ident[:], start=False, stop=True))
                r_tx2[p] = tp.ready
                W("scalar", tp.ready)
                tp.free = op("scalar", sS, lambda e, tp=tp.res, d=d, x2=x2.res:
                             e.copy(x2[:, ts(d, 128)], tp[:]))
                r_x2 = tp.free
        x2.ready = r_x2
        W("vector", h2s.free)
        r_ln = ln_tile(x2.res, h2s.res, st6, x2.ready)
        op("vector", sV, lambda e, h=h2s.res: e.tensor_mul(h[:], h[:], cbp1_tm[:]))
        r_h2 = op("vector", sV, lambda e, h=h2s.res:
                  e.tensor_add(h[:], h[:], sbm_tm[:]))
        x2.free = r_h2
        W("tensor", r_h2)
        for d in range(8):
            tp = tr6.next()
            W("tensor", tp.free)
            tp.ready = op("tensor", sT, lambda e, tp=tp.res, d=d, h=h2s.res:
                          e.matmul(tp[:], h[:, ts(d, 128)], ident[:],
                                   start=True, stop=True))
            W("scalar", tp.ready)
            tp.free = op("scalar", sS, lambda e, tp=tp.res, d=d, it=it:
                         e.copy(h2_fm[d][:, ts(it, 128)], tp[:]))
            r_h2last = tp.free
        h2s.free = r_h2last

    W("tensor", r_h2last)
    W("tensor", d_rw)
    for tkp in range(4):
        tks = tuple(4 * tkp + q for q in range(4))
        sm = [sm8[:, ts(q, 48)] for q in range(4)]
        lgs = []
        for tk in tks:
            lg = lg6.next()
            W("tensor", lg.free)
            for d in range(8):
                lg.ready = op("tensor", sT, lambda e, lg=lg.res, d=d, tk=tk:
                              e.matmul(lg[:], h2_fm[d][:, ts(tk, 128)],
                                       rw_bf[:, ts(d, 8)],
                                       start=(d == 0), stop=(d == 7)))
            lgs.append(lg)
        for p, lg in enumerate(lgs):
            W("vector", lg.ready)
            a = op("vector", sV, lambda e, lg=lg.res, s=sm[p]:
                   e.reduce_max(s[:, 0:1], lg[:], AX, negate=True))
            W("scalar", a)
            a = op("scalar", sS, lambda e, lg=lg.res, s=sm[p]: e.activation(
                s[:, 8:16], lg[:], ACT.Exp, bias=s[:, 0:1]))
            lg.free = a
            W("vector", a)

        def pair(fn):
            r = None
            for p in range(4):
                r = op("vector", sV, lambda e, s=sm[p], p=p: fn(e, s, p))
            W("vector", r)
            return r
        pair(lambda e, s, p: e.reduce_sum(s[:, 1:2], s[:, 8:16], AX))
        pair(lambda e, s, p: e.reciprocal(s[:, 2:3], s[:, 1:2]))
        pair(lambda e, s, p: e.tensor_mul(
            s[:, 8:16], s[:, 8:16], s[:, 2:3].to_broadcast([128, 8])))
        pair(lambda e, s, p: e.tensor_add(
            s[:, 16:24], s[:, 8:16], rbcol[:, 0:1].to_broadcast([128, 8])))
        pair(lambda e, s, p: e.max(out=s[:, 24:32], in_=s[:, 16:24]))
        pair(lambda e, s, p: e.tensor_tensor(
            s[:, 32:40], s[:, 16:24],
            s[:, 27:28].to_broadcast([128, 8]), ALU.is_ge))
        pair(lambda e, s, p: e.tensor_mul(
            s[:, 32:40], s[:, 32:40], s[:, 8:16]))
        pair(lambda e, s, p: e.reduce_sum(s[:, 3:4], s[:, 32:40], AX))
        pair(lambda e, s, p: e.tensor_scalar(
            s[:, 3:4], s[:, 3:4], 1e-20, None, ALU.add))
        pair(lambda e, s, p: e.reciprocal(s[:, 4:5], s[:, 3:4]))
        pair(lambda e, s, p, _tks=tks: e.tensor_mul(
            wsel_bf[:, ts(_tks[p], 8)], s[:, 32:40],
            s[:, 4:5].to_broadcast([128, 8])))
    ctx.flush()
    b6.close()

    b7 = ExitStack()
    b7sb = lambda n, s, d=BF16: b7.enter_context(nc.sbuf_tensor(n, s, d))
    gu_bf = [b7sb(f"gu{i}", [128, 8192]) for i in range(2)]
    dw_bf = [b7sb(f"dwb{i}", [128, 4096]) for i in range(2)]
    act_e = [b7sb(f"acte{s}", [128, NTOK]) for s in range(4)]
    rep7 = b7sb("rep7", [128, 128])
    wsb7 = [b7sb(f"wsb{i}", [128, 512]) for i in range(2)]
    sg7 = [b7sb(f"sg7{i}", [128, 512]) for i in range(2)]
    gu_ps = Pool([b7.enter_context(nc.psum_tensor(f"g7{i}", [128, 512], F32))
                  for i in range(4)])
    wb_ps = Pool([b7.enter_context(nc.psum_tensor(f"w7{i}", [128, 512], F32))
                  for i in range(2)])
    d8_ps = Pool([b7.enter_context(nc.psum_tensor(f"d8{i}", [128, 512], F32))
                  for i in range(2)])
    gu_p = Pool(gu_bf)
    for i, sl in enumerate(gu_p.slots):
        sl.sem = gu_sems[i]
    dw_p = Pool(dw_bf)
    for i, sl in enumerate(dw_p.slots):
        sl.sem = dw_sems[i]
    wsb_p = Pool(wsb7)
    sg_p = Pool(sg7)
    act_slots = [Slot(a) for a in act_e]

    barrier()
    for e_i in range(NEXP):
        gu = gu_p.next()
        W("gpsimd", gu.free)
        for d in range(8):
            dma("gpsimd", gu.sem, gu.res[:, ts(d, 512)], gwt[e_i, ts(d, 128), :])
        for d in range(8):
            gu.ready = dma("gpsimd", gu.sem,
                           gu.res[:, 4096 + d * 512:4096 + d * 512 + 512],
                           uwt[e_i, ts(d, 128), :])
        dwe = dw_p.next()
        W("gpsimd", dwe.free)
        for s in range(4):
            dwe.ready = dma("gpsimd", dwe.sem, dwe.res[:, ts(s, 1024)],
                            dwt[e_i, ts(s, 128), :])
        r_gu_last = None
        for j in range(NSLOT):
            wj = wsb_p.next()
            wp = wb_ps.next()
            W("tensor", wp.free)
            r_rep = None
            for q4 in range(4):
                tk = j * 4 + q4
                W("vector", r_rep)
                r_rep0 = op("vector", sV, lambda e, tk=tk, e_i=e_i: e.tensor_copy(
                    rep7[:], wsel_bf[:, tk * 8 + e_i:tk * 8 + e_i + 1]
                    .to_broadcast([128, 128])))
                W("tensor", r_rep0)
                r_rep = op("tensor", sT, lambda e, wp=wp.res, q4=q4: e.matmul(
                    wp[:, ts(q4, 128)], rep7[:], ident[:], start=True, stop=True))
            wp.ready = r_rep
            W("vector", wp.ready)
            W("vector", wj.free)
            wj.ready = op("vector", sV, lambda e, wj=wj.res, wp=wp.res:
                          e.tensor_copy(wj[:], wp[:]))
            wp.free = wj.ready
            if j == 0:
                W("tensor", gu.ready)
            for s in range(4):
                pg = gu_ps.next()
                pu = gu_ps.next()
                ac = act_slots[s]
                W("tensor", pg.free)
                W("tensor", pu.free)
                for d in range(8):
                    opq("tensor", lambda e, pg=pg.res, gu=gu.res, d=d, s=s, j=j:
                        e.matmul(pg[:], gu[:, d * 512 + s * 128:d * 512 + s * 128 + 128],
                                 h2_fm[d][:, ts(j, 512)], start=(d == 0), stop=(d == 7)))
                for d in range(8):
                    r_gu = op("tensor", sT,
                              lambda e, pu=pu.res, gu=gu.res, d=d, s=s, j=j:
                              e.matmul(pu[:],
                                       gu[:, 4096 + d * 512 + s * 128:
                                          4096 + d * 512 + s * 128 + 128],
                                       h2_fm[d][:, ts(j, 512)],
                                       start=(d == 0), stop=(d == 7)))
                r_gu_last = r_gu
                sg = sg_p.next()
                W("scalar", r_gu)
                W("scalar", sg.free)
                sg.ready = op("scalar", sS, lambda e, pg=pg.res, sg=sg.res:
                              e.activation(sg[:], pg[:], ACT.Silu))
                pg.free = sg.ready
                W("vector", sg.ready)
                W("vector", wj.ready)
                W("vector", ac.free)
                op("vector", sV, lambda e, pu=pu.res, sg=sg.res, ac=ac.res, s=s, j=j:
                   e.tensor_mul(ac[:, ts(j, 512)], sg[:], pu[:]))
                r_act = op("vector", sV, lambda e, ac=ac.res, wj=wj.res, s=s, j=j:
                           e.tensor_mul(ac[:, ts(j, 512)], ac[:, ts(j, 512)], wj[:]))
                pu.free = r_act
                sg.free = r_act
                ac.ready = r_act
            wj.free = r_act
        gu.free = r_gu_last
        r_dn = None
        for j in range(NSLOT):
            pd = d8_ps.next()
            W("tensor", pd.free)
            if j == 0:
                W("tensor", dwe.ready)
            for s in range(4):
                W("tensor", act_slots[s].ready)
            for c in range(8):
                if c > 0:
                    pd = d8_ps.next()
                    W("tensor", pd.free)
                for s in range(4):
                    pd.ready = op("tensor", sT,
                                  lambda e, pd=pd.res, dw=dwe.res, s=s, c=c, j=j,
                                  st_=(s == 0), sp=(s == 3):
                                  e.matmul(pd[:], dw[:, s * 1024 + c * 128:
                                                    s * 1024 + c * 128 + 128],
                                           act_e[s][:, ts(j, 512)],
                                           start=st_, stop=sp))
                W("vector", pd.ready)
                pd.free = op("vector", sV, lambda e, pd=pd.res, c=c, j=j:
                             e.scalar_tensor_tensor(
                    x_res[c][:, ts(j, 512)], pd[:], gb_fm[:, c:c + 1],
                    x_res[c][:, ts(j, 512)], ALU.mult, ALU.add))
                r_dn = pd.free
        dwe.free = r_dn
        for s in range(4):
            act_slots[s].free = r_dn
    ctx.flush()
    b7.close()

    b9 = ExitStack()
    b9sb = lambda n, s, d=BF16: b9.enter_context(nc.sbuf_tensor(n, s, d))
    sgu_bf = [b9sb(f"sgu{d}", [128, 2048]) for d in range(8)]
    s_act = [b9sb(f"sac{i}", [128, NTOK]) for i in range(8)]
    sdw_bf = [b9sb(f"sdwb{i}", [128, 1024]) for i in range(2)]
    sg9 = [b9sb(f"sg9{i}", [128, 512]) for i in range(2)]
    gu9 = Pool([b9.enter_context(nc.psum_tensor(f"g9{i}", [128, 512], F32))
                for i in range(4)])
    d9_ps = Pool([b9.enter_context(nc.psum_tensor(f"d9{i}", [128, 512], F32))
                  for i in range(2)])
    sdw_p = Pool(sdw_bf)
    for i, sl in enumerate(sdw_p.slots):
        sl.sem = sd_sems[i]
    sg9_p = Pool(sg9)

    prev_g9 = None
    prev_dn9 = None
    barrier()
    for ih in range(2):
        d_sg = None
        if ih == 1:
            W("gpsimd", prev_g9)
        for d in range(8):
            dma("gpsimd", s_sgu, sgu_bf[d][:, 0:1024],
                sgw[ts(d, 128), ih * 1024:ih * 1024 + 1024])
            d_sg = dma("gpsimd", s_sgu, sgu_bf[d][:, 1024:2048],
                       suw[ts(d, 128), ih * 1024:ih * 1024 + 1024])
        if ih == 1:
            W("vector", prev_dn9)
        s_act_ready = [None] * 8
        r_act9 = None
        for ic in range(8):
            for j in range(NSLOT):
                pg = gu9.next()
                pu = gu9.next()
                W("tensor", pg.free)
                W("tensor", pu.free)
                if ic == 0 and j == 0:
                    W("tensor", d_sg)
                for d in range(8):
                    opq("tensor", lambda e, pg=pg.res, d=d, ic=ic, j=j:
                        e.matmul(pg[:], sgu_bf[d][:, ts(ic, 128)],
                                 h2_fm[d][:, ts(j, 512)], start=(d == 0), stop=(d == 7)))
                for d in range(8):
                    r_g9 = op("tensor", sT,
                              lambda e, pu=pu.res, d=d, ic=ic, j=j:
                              e.matmul(pu[:], sgu_bf[d][:, 1024 + ic * 128:
                                                        1024 + ic * 128 + 128],
                                       h2_fm[d][:, ts(j, 512)],
                                       start=(d == 0), stop=(d == 7)))
                sg = sg9_p.next()
                W("scalar", r_g9)
                W("scalar", sg.free)
                sg.ready = op("scalar", sS, lambda e, pg=pg.res, sg=sg.res:
                              e.activation(sg[:], pg[:], ACT.Silu))
                pg.free = sg.ready
                W("vector", sg.ready)
                r_a = op("vector", sV, lambda e, pu=pu.res, sg=sg.res, ic=ic, j=j:
                         e.tensor_mul(s_act[ic][:, ts(j, 512)], sg[:], pu[:]))
                pu.free = r_a
                sg.free = r_a
                r_act9 = r_a
            s_act_ready[ic] = r_act9
        r_dn9 = None
        for c in range(8):
            sd = sdw_p.next()
            W("gpsimd", sd.free)
            sd.ready = dma("gpsimd", sd.sem, bass.AP(sd.res, 0,
                                                 [[1024, 128], [128, 8], [1, 128]]),
                           bass.AP(sdw.tensor, ih * 1024 * 1024 + c * 128,
                                   [[1024, 128], [128 * 1024, 8], [1, 128]]))
            for j in range(NSLOT):
                pd = d9_ps.next()
                W("tensor", pd.free)
                if j == 0:
                    W("tensor", sd.ready)
                    for ic in range(8):
                        W("tensor", s_act_ready[ic])
                for ic in range(8):
                    pd.ready = op("tensor", sT,
                                  lambda e, pd=pd.res, sd=sd.res, ic=ic, j=j,
                                  st_=(ic == 0), sp=(ic == 7):
                                  e.matmul(pd[:], sd[:, ts(ic, 128)],
                                           s_act[ic][:, ts(j, 512)],
                                           start=st_, stop=sp))
                W("vector", pd.ready)
                pd.free = op("vector", sV, lambda e, pd=pd.res, c=c, j=j:
                             e.scalar_tensor_tensor(
                    x_res[c][:, ts(j, 512)], pd[:], gb_fm[:, c:c + 1],
                    x_res[c][:, ts(j, 512)], ALU.mult, ALU.add))
                r_dn9 = pd.free
            sd.free = r_dn9
        prev_g9 = r_g9
        prev_dn9 = r_dn9
    ctx.flush()
    b9.close()

    b10 = ExitStack()
    yhl = [b10.enter_context(nc.sbuf_tensor(f"yhl{i}", [128, 256], BF16))
           for i in range(2)]
    y_sb = [b10.enter_context(nc.sbuf_tensor(f"ysb{i}", [128, D], F32))
            for i in range(2)]
    trA = Pool([b10.enter_context(nc.psum_tensor(f"tA{i}", [128, 128], F32))
                for i in range(2)])
    yhl_p = Pool(yhl)
    ysb_slots = [Slot(t) for t in y_sb]
    for i, sl in enumerate(ysb_slots):
        sl.sem = ys_sems[i]
    barrier()
    for jo in range(NSLOT):
        for i in range(4):
            ysl = ysb_slots[i % 2]
            r_tile = None
            for dd in (0, 2, 4, 6):
                hbs = [yhl_p.next(), yhl_p.next()]
                for p in (0, 1):
                    d = dd + p
                    W("vector", hbs[p].free)
                    op("vector", sV, lambda e, hb=hbs[p].res, d=d, jo=jo, i=i:
                       e.tensor_copy(hb[:, 0:128],
                                     x_res[d][:, jo * 512 + i * 128:
                                              jo * 512 + i * 128 + 128]))
                for p in (0, 1):
                    d = dd + p
                    hbs[p].ready = op("vector", sV,
                                      lambda e, hb=hbs[p].res, d=d, jo=jo, i=i:
                                      e.tensor_sub(
                        hb[:, 128:256],
                        x_res[d][:, jo * 512 + i * 128:jo * 512 + i * 128 + 128],
                        hb[:, 0:128]))
                for p in (0, 1):
                    d = dd + p
                    hb = hbs[p]
                    tp = trA.next()
                    W("tensor", hb.ready)
                    W("tensor", tp.free)
                    if d == 0:
                        W("tensor", ysl.free)
                    opq("tensor", lambda e, tp=tp.res, hb=hb.res: e.matmul(
                        tp[:], hb[:, 0:128], ident[:], start=True, stop=False))
                    tp.ready = op("tensor", sT, lambda e, tp=tp.res, hb=hb.res:
                                  e.matmul(tp[:], hb[:, 128:256], ident[:],
                                           start=False, stop=True))
                    hb.free = tp.ready
                    W("scalar", tp.ready)
                    tp.free = op("scalar", sS,
                                 lambda e, tp=tp.res, d=d, ysl=ysl.res:
                                 e.copy(ysl[:, ts(d, 128)], tp[:]))
                    r_tile = tp.free
            ysl.ready = r_tile
            W("sync", ysl.ready)
            ysl.free = dma("sync", ysl.sem,
                           y[jo * 512 + i * 128: jo * 512 + i * 128 + 128, :],
                           ysl.res[:])
    opq("sync", lambda e, v0=ys_sems[0].n, v1=ys_sems[1].n:
        (e.wait_ge(ys_sems[0].h, v0), e.wait_ge(ys_sems[1].h, v1)))
    ctx.flush()
    b10.close()
    root.close()
    return nc


_PROG = None


def _get_prog():
    global _PROG
    if _PROG is None:
        _PROG = build_program()
    return _PROG


def kernel(L0_attn_x, L0_attn_w_qkv, L0_attn_w_out, L0_ffn_router_w,
           L0_ffn_router_bias, L0_ffn_gate_w, L0_ffn_up_w, L0_ffn_down_w,
           L0_ffn_s_gate_w, L0_ffn_s_up_w, L0_ffn_s_down_w, L0_c, L0_wmod):
    bf = ml_dtypes.bfloat16
    x = np.ascontiguousarray(np.asarray(L0_attn_x, dtype=np.float32))
    wqkv_t = np.ascontiguousarray(np.asarray(L0_attn_w_qkv, np.float32).T)
    wout_t = np.ascontiguousarray(np.asarray(L0_attn_w_out, np.float32).T)
    wmod = np.ascontiguousarray(np.asarray(L0_wmod, np.float32))
    cmat = np.ascontiguousarray(np.asarray(L0_c, np.float32))
    rbv = np.asarray(L0_ffn_router_bias, np.float32).reshape(1, 1)
    rwm = np.asarray(L0_ffn_router_w, np.float32)
    rw_fm = np.ascontiguousarray(
        rwm.reshape(8, 128, 8).transpose(1, 0, 2).reshape(128, 64))
    gwm = np.ascontiguousarray(np.asarray(L0_ffn_gate_w, np.float32))
    uwm = np.ascontiguousarray(np.asarray(L0_ffn_up_w, np.float32))
    dwm = np.ascontiguousarray(np.asarray(L0_ffn_down_w, np.float32))
    sgwm = np.ascontiguousarray(np.asarray(L0_ffn_s_gate_w, np.float32))
    suwm = np.ascontiguousarray(np.asarray(L0_ffn_s_up_w, np.float32))
    sdwm = np.ascontiguousarray(np.asarray(L0_ffn_s_down_w, np.float32))

    mask_h = {}
    for half in (0, 1):
        m = np.zeros((NSLOT, 128, 4096), dtype=bf)
        for sl in range(NSLOT):
            gq = OWN_TGS[half][sl]
            for mc in range(8):
                kc = 4 * (EXT[sl] - 2) + mc
                keys = kc * 128 + np.arange(128)[:, None]
                qs = gq * 512 + np.arange(512)[None, :]
                m[sl, :, mc * 512:(mc + 1) * 512] = (keys <= qs).astype(bf)
        mask_h[half] = m

    nc = _get_prog()
    in_maps = []
    for core in range(8):
        b, half = core // 2, core % 2
        tgs = OWN_TGS[half]
        x_b = np.ascontiguousarray(x[b])
        x_own = np.ascontiguousarray(
            np.concatenate([x[b, ts(t, 512), :] for t in tgs], axis=0))
        in_maps.append({
            "x_full": x_b, "x_own": x_own, "wqkv_t": wqkv_t, "wout_t": wout_t,
            "wmod": wmod,
            "c_fm_in": np.ascontiguousarray(cmat[b].reshape(8, 128).T),
            "rb": rbv, "rw": rw_fm, "gwt": gwm, "uwt": uwm, "dwt": dwm,
            "sgw": sgwm, "suw": suwm, "sdw": sdwm, "masks": mask_h[half],
        })
    res = run_bass_kernel_spmd(nc, in_maps, core_ids=list(range(8)))
    out = np.empty((B, T, D), dtype=np.float32)
    for core in range(8):
        b, half = core // 2, core % 2
        yv = res.results[core]["y"]
        for sl, tg in enumerate(OWN_TGS[half]):
            out[b, ts(tg, 512), :] = yv[ts(sl, 512), :]
    return out


# version 24
# speedup vs baseline: 1.1758x; 1.0056x over previous
import numpy as np
import ml_dtypes
from contextlib import ExitStack

import concourse.bass as bass
import concourse.mybir as mybir
from concourse.bass_utils import run_bass_kernel_spmd

F32 = mybir.dt.float32
BF16 = mybir.dt.bfloat16
AX = mybir.AxisListType.X
ALU = mybir.AluOpType
ACT = mybir.ActivationFunctionType

B, T, D = 4, 4096, 1024
NTOK = 2048
NSLOT = 4
EXT = [2, 4, 6, 8]
OWN_TGS = {0: [0, 3, 4, 7], 1: [1, 2, 5, 6]}
SHIFT = 20.0
NEXP, IEXP, ISH = 8, 512, 2048
EPS = 1e-5


def ts(i, n):
    return slice(i * n, (i + 1) * n)


class Sem:
    def __init__(self, stk, nc, name):
        self.h = stk.enter_context(nc.semaphore(name))
        self.n = 0


class Ctx:
    ENGINES = ("sync", "gpsimd", "tensor", "vector", "scalar")

    def __init__(self, nc):
        self.nc = nc
        self.ops = {e: [] for e in self.ENGINES}

    def emit(self, eng, fn):
        self.ops[eng].append(fn)

    def wait(self, eng, dep):
        if dep is None:
            return
        sem, val = dep
        if val <= 0:
            return
        assert val <= sem.n, f"wait {val} > emitted {sem.n} on {eng}"
        self.ops[eng].append(lambda e, s=sem.h, v=val: e.wait_ge(s, v))

    def flush(self):
        with self.nc.Block() as blk:
            for name in self.ENGINES:
                lst = self.ops[name]
                if lst:
                    getattr(blk, name)(
                        lambda e, _l=tuple(lst): [f(e) for f in _l])
        self.ops = {e: [] for e in self.ENGINES}


class Slot:
    def __init__(self, res, sem=None):
        self.res = res
        self.sem = sem
        self.free = None
        self.ready = None


class Pool:
    def __init__(self, resources):
        self.slots = [Slot(r) for r in resources]
        self.i = 0

    def next(self):
        s = self.slots[self.i % len(self.slots)]
        self.i += 1
        return s


def build_program():
    nc = bass.Bass("TRN2", target_bir_lowering=False, debug=False,
                   detect_race_conditions=False)
    dri = lambda name, shape, dt=F32: nc.dram_tensor(
        name, shape, dt, kind="ExternalInput").ap()
    x_full = dri("x_full", [T, D])
    x_own = dri("x_own", [NTOK, D])
    wqkv_t = dri("wqkv_t", [D, 3 * D])
    wout_t = dri("wout_t", [D, D])
    wmod = dri("wmod", [D, 6 * D])
    c_fm_in = dri("c_fm_in", [128, 8])
    rb = dri("rb", [1, 1])
    rw = dri("rw", [128, 64])
    gwt = dri("gwt", [NEXP, D, IEXP])
    uwt = dri("uwt", [NEXP, D, IEXP])
    dwt = dri("dwt", [NEXP, IEXP, D])
    sgw = dri("sgw", [D, ISH])
    suw = dri("suw", [D, ISH])
    sdw = dri("sdw", [ISH, D])
    masks = dri("masks", [NSLOT, 128, 4096], BF16)
    y = nc.dram_tensor("y", [NTOK, D], F32, kind="ExternalOutput").ap()
    mod_dram = nc.dram_tensor("mod_scratch", [1, 4 * D], F32, kind="Internal").ap()

    root = ExitStack()
    rsb = lambda n, s, d=F32: root.enter_context(nc.sbuf_tensor(n, s, d))

    K_bf = [rsb(f"K{p}", [128, T], BF16) for p in range(4)]
    V_bf = rsb("Vb", [128, 32 * 520], BF16)
    attnout = [rsb(f"ao{d}", [128, NTOK], BF16) for d in range(8)]
    ident = rsb("ident", [128, 128], BF16)
    ones_bf = rsb("ones", [128, 512], BF16)
    brow = rsb("brow", [1, 3 * D], BF16)
    cbp1_tm = rsb("cbp1", [128, D])
    sbm_tm = rsb("sbm", [128, D], BF16)
    cap1_fm = rsb("cap1", [128, 8])
    ga_fm = rsb("ga", [128, 8])
    gb_fm = rsb("gb", [128, 8])
    neg20 = rsb("neg20", [128, 1])
    epsc = rsb("epsc", [128, 1])
    rbcol = rsb("rbcol", [128, 1])
    wsel_bf = rsb("wsel", [128, 128], BF16)
    sa_bf = rsb("sa_bf", [128, 8], BF16)
    x_res = [K_bf[d][:].bitcast(F32) for d in range(4)] + \
            [V_bf[:].bitcast(F32)[:, ts(i, NTOK)] for i in range(4)]
    h2_fm = attnout

    ctx = Ctx(nc)
    sG = Sem(root, nc, "sG")
    sT = Sem(root, nc, "sT")
    sV = Sem(root, nc, "sV")
    sS = Sem(root, nc, "sS")
    s_gp0 = Sem(root, nc, "s_gp0")
    s_wqkv = Sem(root, nc, "s_wqkv")
    s_wsl = Sem(root, nc, "s_wsl")
    s_sgu = Sem(root, nc, "s_sgu")
    wm_sems = [Sem(root, nc, f"s_wm{i}") for i in range(4)]
    xa_sems = [Sem(root, nc, f"s_xa{i}") for i in range(4)]
    s_msk = Sem(root, nc, "s_msk")
    gu_sems = [Sem(root, nc, f"s_gu{i}") for i in range(2)]
    dw_sems = [Sem(root, nc, f"s_dwe{i}") for i in range(2)]
    dc_sems = [Sem(root, nc, f"s_dc{i}") for i in range(2)]
    sd_sems = [Sem(root, nc, f"s_sd{i}") for i in range(2)]
    ys_sems = [Sem(root, nc, f"s_ys{i}") for i in range(2)]
    s_bc = Sem(root, nc, "s_bc")

    def dma(eng, sem, out, in_, slow=False):
        def f(e, o=out, i=in_, s=sem.h, sl=slow):
            kw = {"allow_slow_non_contiguous": True} if sl else {}
            e.dma_start(o, i, **kw).then_inc(s, 16)
        ctx.emit(eng, f)
        sem.n += 16
        return (sem, sem.n)

    def op(eng, sem, fn):
        def f(e, fn=fn, s=sem.h):
            fn(e).then_inc(s, 1)
        ctx.emit(eng, f)
        sem.n += 1
        return (sem, sem.n)

    def opq(eng, fn):
        ctx.emit(eng, fn)

    W = ctx.wait

    def _recl(e, out_ap, in_ap):
        with nc.allow_low_precision(reason="softmax denom recip to bf16"):
            return e.reciprocal(out_ap, in_ap)

    def barrier():
        snap = [(sT, sT.n), (sV, sV.n), (sS, sS.n)]
        for eng in Ctx.ENGINES:
            for dep in snap:
                W(eng, dep)

    b0 = ExitStack()
    bsb = lambda n, s, d=F32: b0.enter_context(nc.sbuf_tensor(n, s, d))
    bps = lambda n, s: b0.enter_context(nc.psum_tensor(n, s, F32))
    io_col = bsb("io_col", [128, 128])
    io_par = bsb("io_par", [128, 1])
    c_fm = bsb("c_fm", [128, 8])
    sil = bsb("sil", [128, 8])
    sil_hi = bsb("sil_hi", [128, 8], BF16)
    sil_lo = bsb("sil_lo", [128, 8], BF16)
    rep_hi = [bsb(f"rph{d}", [128, 128], BF16) for d in range(8)]
    rep_lo = [bsb(f"rpl{d}", [128, 128], BF16) for d in range(8)]
    wm_st = [bsb(f"wm{i}", [128, 512]) for i in range(4)]
    wm_hl = [bsb(f"wmh{i}", [128, 1024], BF16) for i in range(2)]
    modrow = bsb("modrow", [1, 4 * D])
    sga = bsb("sga", [128, 8])
    sgb = bsb("sgb", [128, 8])
    aba = bsb("aba", [128, 8])
    abb = bsb("abb", [128, 8])
    rb_sb = bsb("rb_sb", [1, 1], BF16)
    m32 = bsb("m32", [32, 128])
    m32h = bsb("m32h", [32, 128], BF16)
    m32l = bsb("m32l", [32, 128], BF16)
    mfm = bsb("mfm", [128, 32])
    mfm_ps = b0.enter_context(nc.psum_tensor("pmfm", [128, 32], F32))
    mod_ps = Pool([bps(f"p0_{i}", [128, 512]) for i in range(2)])
    rbc_ps = bps("prbc", [128, 512])

    d_iota = op("gpsimd", sG, lambda e: e.iota(
        io_col[:], [[1, 128]], channel_multiplier=0,
        allow_small_or_imprecise_dtypes=True))
    d_iota = op("gpsimd", sG, lambda e: e.iota(
        io_par[:], [[0, 1]], channel_multiplier=1,
        allow_small_or_imprecise_dtypes=True))
    W("vector", d_iota)
    r_ident = op("vector", sV, lambda e: e.tensor_scalar(
        ident[:], io_col[:], io_par[:, 0:1], None, ALU.is_equal))
    op("vector", sV, lambda e: e.memset(ones_bf[:], 1.0))
    op("vector", sV, lambda e: e.memset(epsc[:], EPS))
    r_ones = op("vector", sV, lambda e: e.memset(neg20[:], -SHIFT))

    dma("gpsimd", s_gp0, c_fm[:, 0:8], c_fm_in[:, :])
    d_crb = dma("gpsimd", s_gp0, rb_sb[0:1, 0:1], rb[0:1, 0:1])
    W("scalar", d_crb)
    r_sil = op("scalar", sS, lambda e: e.activation(sil[:], c_fm[:], ACT.Silu))
    W("vector", r_sil)
    a = op("vector", sV, lambda e: e.tensor_copy(sil_hi[:], sil[:]))
    W("vector", a)
    for d in range(8):
        r_rep = op("vector", sV, lambda e, d=d: e.tensor_copy(
            rep_hi[d][:], sil_hi[:, d:d + 1].to_broadcast([128, 128])))

    st_slots = Pool(wm_st)
    for i, sl in enumerate(st_slots.slots):
        sl.sem = wm_sems[i]
        sl.q = "sync" if i < 2 else "gpsimd"
    hl_slots = Pool(wm_hl)
    r_mod = {}
    for g in range(12):
        mp = mod_ps.next()
        W("tensor", mp.free)
        for d in range(8):
            k = g * 8 + d
            st = st_slots.next()
            W(st.q, st.free)
            st.ready = dma(st.q, st.sem, st.res[:], wmod[ts(d, 128), ts(g, 512)])
            hl = hl_slots.next()
            W("vector", st.ready)
            W("vector", hl.free)
            hl.ready = op("vector", sV, lambda e, a=hl.res, b=st.res:
                          e.tensor_copy(a[:, 0:512], b[:]))
            st.free = hl.ready
            W("tensor", hl.ready)
            if k == 0:
                W("tensor", r_rep)
            hl.free = op("tensor", sT, lambda e, d=d, a=hl.res, mp=mp.res,
                         st_=(d == 0), sp=(d == 7):
                         e.matmul(mp[:], rep_hi[d][:], a[:, 0:512],
                                  start=st_, stop=sp))
        mp.ready = hl.free
        m, hh = g // 2, g % 2
        W("vector", mp.ready)
        if m == 0:
            ev = op("vector", sV, lambda e, p=mp.res, hh=hh:
                    e.tensor_copy(modrow[0:1, ts(hh, 512)], p[0:1, :]))
        elif m == 1:
            ev = op("vector", sV, lambda e, p=mp.res, hh=hh: e.tensor_copy(
                modrow[0:1, 1024 + hh * 512:1024 + hh * 512 + 512], p[0:1, :]))
        elif m == 2:
            ev = op("vector", sV, lambda e, p=mp.res, hh=hh: e.tensor_copy(
                modrow[0:1, 2048 + hh * 512:2048 + hh * 512 + 512], p[0:1, :]))
        elif m == 3:
            ev = op("vector", sV, lambda e, p=mp.res, hh=hh:
                    e.tensor_copy(sbm_tm[:, ts(hh, 512)], p[:]))
        elif m == 4:
            ev = op("vector", sV, lambda e, p=mp.res, hh=hh: e.tensor_scalar(
                cbp1_tm[:, ts(hh, 512)], p[:], 1.0, None, ALU.add))
        else:
            ev = op("vector", sV, lambda e, p=mp.res, hh=hh: e.tensor_copy(
                modrow[0:1, 3072 + hh * 512:3072 + hh * 512 + 512], p[0:1, :]))
        mp.free = ev
        r_mod[m * 2 + hh] = ev

    W("gpsimd", r_mod[11])
    d_bounce = dma("gpsimd", s_gp0, mod_dram[0:1, :], modrow[0:1, :])
    W("gpsimd", d_bounce)
    d_m32 = dma("gpsimd", s_gp0, m32[:, :],
                bass.AP(mod_dram.tensor, 0, [[128, 32], [1, 128]]))
    W("vector", d_m32)
    r_m32 = op("vector", sV, lambda e: e.tensor_copy(m32h[:], m32[:]))
    W("tensor", r_m32)
    W("tensor", r_ident)
    t_mfm = op("tensor", sT, lambda e: e.matmul(mfm_ps[:], m32h[:],
                                                ident[0:32, 0:32],
                                                start=True, stop=True))
    W("vector", t_mfm)
    r_mfm = op("vector", sV, lambda e: e.tensor_copy(mfm[:], mfm_ps[0:128, 0:32]))
    W("vector", r_mfm)
    r_cap1 = op("vector", sV, lambda e: e.tensor_scalar(
        cap1_fm[:], mfm[:, 8:16], 1.0, None, ALU.add))
    r_sabf = op("vector", sV, lambda e: e.tensor_copy(sa_bf[:], mfm[:, 0:8]))
    op("vector", sV, lambda e: e.tensor_copy(ga_fm[:], mfm[:, 16:24]))
    r_gcp = op("vector", sV, lambda e: e.tensor_copy(gb_fm[:], mfm[:, 24:32]))
    W("scalar", r_gcp)
    op("scalar", sS, lambda e: e.activation(sga[:], ga_fm[:], ACT.Sign))
    op("scalar", sS, lambda e: e.activation(sgb[:], gb_fm[:], ACT.Sign))
    op("scalar", sS, lambda e: e.activation(aba[:], ga_fm[:], ACT.Abs))
    a_gab = op("scalar", sS, lambda e: e.activation(abb[:], gb_fm[:], ACT.Abs))
    W("vector", a_gab)
    op("vector", sV, lambda e: e.tensor_scalar(
        aba[:], aba[:], 0.25, None, ALU.max))
    a = op("vector", sV, lambda e: e.tensor_scalar(
        abb[:], abb[:], 0.25, None, ALU.max))
    W("vector", a)
    op("vector", sV, lambda e: e.tensor_mul(ga_fm[:], sga[:], aba[:]))
    r_gab = op("vector", sV, lambda e: e.tensor_mul(gb_fm[:], sgb[:], abb[:]))

    W("tensor", d_crb)
    W("tensor", r_ones)
    t_rbc = op("tensor", sT, lambda e: e.matmul(
        rbc_ps[0:128, 0:1], ones_bf[0:1, 0:128], rb_sb[0:1, 0:1],
        start=True, stop=True))
    W("vector", t_rbc)
    op("vector", sV, lambda e: e.tensor_copy(rbcol[:], rbc_ps[0:128, 0:1]))

    ctx.flush()
    b0.close()
    barrier()

    sa_stk = ExitStack()
    asb = lambda n, s, d=F32: sa_stk.enter_context(nc.sbuf_tensor(n, s, d))
    Q_bf = [asb(f"Q{p}", [128, NTOK], BF16) for p in range(4)]
    wsl = [asb(f"wsl{d}", [128, 1536], BF16) for d in range(8)]
    xa = [asb(f"xa{i}", [128, D]) for i in range(4)]
    xnb = [asb(f"xn{i}", [128, D], BF16) for i in range(2)]
    xfm = [[asb(f"xf{s}_{d}", [128, 512], BF16) for d in range(8)]
           for s in range(2)]
    st8 = asb("st8", [128, 32])
    P_sb = [asb(f"P{i}", [128, 1024], BF16) for i in range(4)]
    msk_sb = asb("msk", [128, 4096], BF16)
    avb_st = [asb(f"avst{i}", [64, 512], BF16) for i in range(2)]
    bc2 = [asb(f"bc2_{i}", [64, 512], BF16) for i in range(2)]
    rec1 = asb("rec1", [1, 512], BF16)

    xofm = [Q_bf[p][:].bitcast(F32) for p in range(4)]

    def ln_group(tiles):
        n = len(tiles)
        for k, (xt, xnt, st, dep, nfree) in enumerate(tiles):
            W("vector", dep)
            op("vector", sV, lambda e, xt=xt, st=st:
               e.reduce_sum(st[:, 0:1], xt[:], AX))
        a = None
        for k, (xt, xnt, st, dep, nfree) in enumerate(tiles):
            W("scalar", dep)
            W("scalar", nfree)
            a = op("scalar", sS, lambda e, xt=xt, xnt=xnt, st=st: e.activation(
                xnt[:], xt[:], ACT.Square, accum_out=st[:, 1:2]))
        W("vector", a)
        for k, (xt, xnt, st, dep, nfree) in enumerate(tiles):
            op("vector", sV, lambda e, st=st: e.tensor_scalar(
                st[:, 5:6], st[:, 1:2], 1.0 / D, EPS, ALU.mult, ALU.add))
        a = None
        for k, (xt, xnt, st, dep, nfree) in enumerate(tiles):
            a = op("vector", sV, lambda e, st=st: e.tensor_scalar(
                st[:, 2:3], st[:, 0:1], 1.0 / D, None, ALU.mult))
        W("scalar", a)
        for k, (xt, xnt, st, dep, nfree) in enumerate(tiles):
            a = op("scalar", sS, lambda e, st=st: e.activation(
                st[:, 3:4], st[:, 2:3], ACT.Square))
        W("vector", a)
        for k, (xt, xnt, st, dep, nfree) in enumerate(tiles):
            a = op("vector", sV, lambda e, st=st: e.scalar_tensor_tensor(
                st[:, 4:5], st[:, 3:4], -1.0, st[:, 5:6], ALU.mult, ALU.add))
        W("scalar", a)
        for k, (xt, xnt, st, dep, nfree) in enumerate(tiles):
            a = op("scalar", sS, lambda e, st=st: e.activation(
                st[:, 6:7], st[:, 4:5], ACT.Sqrt))
        W("vector", a)
        for k, (xt, xnt, st, dep, nfree) in enumerate(tiles):
            a = op("vector", sV, lambda e, st=st: e.reciprocal(
                st[:, 7:8], st[:, 6:7]))
        if n < 4:
            W("vector", a)
        fin = [None] * n
        for k, (xt, xnt, st, dep, nfree) in enumerate(tiles):
            W("vector", nfree)
            fin[k] = op("vector", sV, lambda e, xt=xt, xnt=xnt, st=st:
                        e.scalar_tensor_tensor(
                xnt[:], xt[:], st[:, 2:3],
                st[:, 7:8].to_broadcast([128, D]), ALU.subtract, ALU.mult))
        return fin

    def ln_tile(xt, xnt, stats, dep_x):
        return ln_group([(xt, xnt, stats, dep_x, None)])[0]

    xa_p = Pool(xa)
    for i, sl in enumerate(xa_p.slots):
        sl.sem = xa_sems[i]
    xn_p = Pool(xnb + [msk_sb[:, ts(i, 1024)] for i in range(4)])
    for hg in range(2):
        barrier()
        pa = ExitStack()
        tr_ps = Pool([pa.enter_context(nc.psum_tensor(f"tr{hg}_{i}", [128, 128], F32))
                      for i in range(2)])
        kv_ps = Pool([pa.enter_context(nc.psum_tensor(f"kv{hg}_{i}", [128, 512], F32))
                      for i in range(4)])

        d_w = None
        for d in range(8):
            dma("gpsimd", s_wsl, wsl[d][:, 0:512],
                wqkv_t[ts(d, 128), hg * 512:hg * 512 + 512])
            dma("gpsimd", s_wsl, wsl[d][:, 512:1024],
                wqkv_t[ts(d, 128), 1024 + hg * 512:1024 + hg * 512 + 512])
            d_w = dma("gpsimd", s_wsl, wsl[d][:, 1024:1536],
                      wqkv_t[ts(d, 128), 2048 + hg * 512:2048 + hg * 512 + 512])
        br_ps = Pool([pa.enter_context(nc.psum_tensor(f"br{hg}_{i}", [1, 512],
                                                      F32)) for i in range(2)])
        W("tensor", d_w)
        W("tensor", r_sabf)
        r_brow = None
        t_brow = None
        for part in range(3):
            bp = br_ps.next()
            W("tensor", bp.free)
            for d in range(8):
                t_brow = op("tensor", sT,
                            lambda e, d=d, part=part, bp=bp.res:
                            e.matmul(bp[:], sa_bf[:, d:d + 1],
                                     wsl[d][:, ts(part, 512)],
                                     start=(d == 0), stop=(d == 7)))
            bp.ready = t_brow
            W("vector", bp.ready)
            dstb = [hg * 512, 1024 + hg * 512, 2048 + hg * 512][part]
            if part == 0:
                ev = op("vector", sV, lambda e, bp=bp.res, dstb=dstb:
                        e.tensor_scalar(brow[0:1, dstb:dstb + 512],
                                        bp[0:1, :], 0.125, None, ALU.mult))
            else:
                ev = op("vector", sV, lambda e, bp=bp.res, dstb=dstb:
                        e.tensor_copy(brow[0:1, dstb:dstb + 512], bp[0:1, :]))
            bp.free = ev
            r_brow = ev
        W("vector", t_brow)
        W("vector", d_w)
        W("vector", r_cap1)
        for d in range(8):
            op("vector", sV, lambda e, d=d: e.tensor_mul(
                wsl[d][:], wsl[d][:],
                cap1_fm[:, d:d + 1].to_broadcast([128, 1536])))
            r_wsl = op("vector", sV, lambda e, d=d: e.tensor_scalar(
                wsl[d][:, 0:512], wsl[d][:, 0:512], 0.125, None, ALU.mult))
        for kc in range(32):
            r_vones = op("vector", sV, lambda e, kc=kc: e.memset(
                bass.AP(V_bf, 520 * kc + 64, [[32 * 520, 128], [65, 8], [1, 1]]),
                1.0))

        xfm_ready = [None, None]
        xfm_free = [None, None]

        def emit_kv(s2, qpass, grp):
            W("tensor", xfm_ready[s2])
            r_cons = None
            if not qpass:
                tg = grp
                for p in range(4):
                    kp = kv_ps.next()
                    W("tensor", kp.free)
                    for d in range(8):
                        opq("tensor", lambda e, kp=kp.res, d=d, p=p, s2=s2:
                            e.matmul(kp[:], wsl[d][:, 512 + p * 128:512 + p * 128 + 128],
                                     xfm[s2][d][:], start=(d == 0), stop=False))
                    kp.ready = op("tensor", sT, lambda e, kp=kp.res, p=p, hg=hg:
                                  e.matmul(kp[:],
                                           brow[0:1, 1024 + hg * 512 + p * 128:
                                                1024 + hg * 512 + p * 128 + 128],
                                           ones_bf[0:1, :], start=False, stop=True))
                    W("vector", kp.ready)
                    kp.free = op("vector", sV, lambda e, kp=kp.res, p=p, tg=tg:
                                 e.tensor_copy(K_bf[p][:, ts(tg, 512)], kp[:]))
                    r_cons = kp.free
                for kk in range(4):
                    kc = tg * 4 + kk
                    vp = kv_ps.next()
                    W("tensor", vp.free)
                    for d in range(8):
                        opq("tensor", lambda e, vp=vp.res, d=d, kk=kk, s2=s2:
                            e.matmul(vp[:], xfm[s2][d][:, ts(kk, 128)],
                                     wsl[d][:, 1024:1536], start=(d == 0), stop=False))
                    vp.ready = op("tensor", sT, lambda e, vp=vp.res, hg=hg:
                                  e.matmul(vp[:], ones_bf[0:1, 0:128],
                                           brow[0:1, 2048 + hg * 512:
                                                2048 + hg * 512 + 512],
                                           start=False, stop=True))
                    W("vector", vp.ready)
                    vp.free = op("vector", sV, lambda e, vp=vp.res, kc=kc:
                                 e.tensor_copy(
                        bass.AP(V_bf, 520 * kc, [[32 * 520, 128], [65, 8], [1, 64]]),
                        vp[:]))
                    r_cons = vp.free
            else:
                jo = grp - 8
                for p in range(4):
                    qp = kv_ps.next()
                    W("tensor", qp.free)
                    for d in range(8):
                        opq("tensor", lambda e, qp=qp.res, d=d, p=p, s2=s2:
                            e.matmul(qp[:], wsl[d][:, p * 128:p * 128 + 128],
                                     xfm[s2][d][:], start=(d == 0), stop=False))
                    qp.ready = op("tensor", sT, lambda e, qp=qp.res, p=p, hg=hg:
                                  e.matmul(qp[:],
                                           brow[0:1, hg * 512 + p * 128:
                                                hg * 512 + p * 128 + 128],
                                           ones_bf[0:1, :], start=False, stop=True))
                    W("vector", qp.ready)
                    qp.free = op("vector", sV, lambda e, qp=qp.res, p=p, jo=jo:
                                 e.tensor_copy(Q_bf[p][:, ts(jo, 512)], qp[:]))
                    r_cons = qp.free
            xfm_free[s2] = r_cons

        pend = None
        for grp in range(12):
            qpass = grp >= 8
            s2 = grp % 2
            group, xss, nss = [], [], []
            for i in range(4):
                it = grp * 4 + i
                src = (x_own[ts(it - 32, 128), :] if qpass
                       else x_full[ts(it, 128), :])
                xs = xa_p.next()
                W("sync", xs.free)
                xs.ready = dma("sync", xs.sem, xs.res[:], src)
                ns = xn_p.next()
                xss.append(xs)
                nss.append(ns)
                group.append((xs.res, ns.res, st8[:, 8 * i:8 * i + 8],
                              xs.ready, ns.free))
            fins = ln_group(group)
            for i in range(4):
                xss[i].free = fins[i]
            if grp == 0:
                W("tensor", r_wsl)
                W("tensor", r_vones)
                W("tensor", r_brow)
                W("tensor", r_ident)
            W("tensor", xfm_free[s2])
            for i in range(4):
                ns = nss[i]
                W("tensor", fins[i])
                for d in range(8):
                    tp = tr_ps.next()
                    W("tensor", tp.free)
                    tp.ready = op("tensor", sT,
                                  lambda e, tp=tp.res, x=ns.res, d=d:
                                  e.matmul(tp[:], x[:, ts(d, 128)], ident[:],
                                           start=True, stop=True))
                    W("scalar", tp.ready)
                    tp.free = op("scalar", sS,
                                 lambda e, tp=tp.res, d=d, s2=s2, i=i:
                                 e.copy(xfm[s2][d][:, ts(i, 128)], tp[:]))
                    r_evac = tp.free
                ns.free = r_evac
            xfm_ready[s2] = r_evac
            if pend is not None:
                emit_kv(*pend)
            pend = (s2, qpass, grp)
        emit_kv(*pend)
        ctx.flush()
        pa.close()
        barrier()

        pa2 = ExitStack()
        sc_ps = Pool([pa2.enter_context(nc.psum_tensor(f"sc{hg}_{i}", [128, 1024], F32))
                      for i in range(3)])
        av_ps = Pool([pa2.enter_context(nc.psum_tensor(f"av{hg}_{i}", [65, 512], F32))
                      for i in range(2)])
        P_p = Pool(P_sb)
        msk_slot = Slot(msk_sb)

        for j in range(NSLOT):
            W("sync", msk_slot.free)
            msk_slot.ready = dma("sync", s_msk, msk_sb[:], masks[j, :, :])
            r_mask_last = None
            npair = 2 * EXT[j]
            for hp in range(4):
                hls = (2 * hp, 2 * hp + 1)
                avs, scs, pbs = {}, {}, {}
                for hl in hls:
                    avs[hl] = av_ps.next()
                def emit_scs(pc):
                    kc0, kc1 = 2 * pc, 2 * pc + 1
                    out = {}
                    for hl in hls:
                        p, par = hl // 2, hl % 2
                        pr = slice(64 * par, 64 * par + 64)
                        sc = sc_ps.next()
                        out[hl] = sc
                        W("tensor", sc.free)
                        opq("tensor",
                            lambda e, sc=sc.res, p=p, pr=pr, kc0=kc0, j=j:
                            e.matmul(sc[:, 0:512], K_bf[p][pr, ts(kc0, 128)],
                                     Q_bf[p][pr, ts(j, 512)],
                                     start=True, stop=True))
                        sc.ready = op("tensor", sT,
                                      lambda e, sc=sc.res, p=p, pr=pr,
                                      kc1=kc1, j=j:
                                      e.matmul(sc[:, 512:1024],
                                               K_bf[p][pr, ts(kc1, 128)],
                                               Q_bf[p][pr, ts(j, 512)],
                                               start=True, stop=True))
                    return out

                scs_next = emit_scs(0)
                for pc in range(npair):
                    kc0, kc1 = 2 * pc, 2 * pc + 1
                    scs = scs_next
                    for hl in hls:
                        sc = scs[hl]
                        pb = P_p.next()
                        pbs[hl] = pb
                        W("scalar", sc.ready)
                        W("scalar", pb.free)
                        pb.ready = op("scalar", sS,
                                      lambda e, sc=sc.res, pb=pb.res:
                                      e.activation(pb[:], sc[:], ACT.Exp,
                                                   bias=neg20[:, 0:1]))
                        sc.free = pb.ready
                    scs_next = emit_scs(pc + 1) if pc + 1 < npair else None
                    masked = pc >= npair - 4
                    if masked:
                        mc = pc - (npair - 4)
                        for hl in hls:
                            pb = pbs[hl]
                            W("vector", pb.ready)
                            if hp == 0 and hl == hls[0] and mc == 0:
                                W("vector", msk_slot.ready)
                            pb.ready = op("vector", sV,
                                          lambda e, pb=pb.res, mc=mc:
                                          e.tensor_mul(pb[:], pb[:],
                                                       msk_sb[:, ts(mc, 1024)]))
                            r_mask_last = pb.ready
                    for hl in hls:
                        av, pb = avs[hl], pbs[hl]
                        if pc == 0:
                            W("tensor", av.free)
                        W("tensor", pb.ready)
                        opq("tensor",
                            lambda e, av=av.res, hl=hl, kc0=kc0, pb=pb.res,
                            st_=(pc == 0):
                            e.matmul(av[:], bass.AP(V_bf, 520 * kc0 + 65 * hl,
                                                    [[32 * 520, 128], [1, 65]]),
                                     pb[:, 0:512], start=st_, stop=False))
                        pb.free = op("tensor", sT,
                                     lambda e, av=av.res, hl=hl, kc1=kc1,
                                     pb=pb.res, sp=(pc == npair - 1):
                                     e.matmul(av[:],
                                              bass.AP(V_bf, 520 * kc1 + 65 * hl,
                                                      [[32 * 520, 128], [1, 65]]),
                                              pb[:, 512:1024],
                                              start=False, stop=sp))
                        av.ready = pb.free
                avb = [avb_st[0], avb_st[1]]
                for q, hl in enumerate(hls):
                    av = avs[hl]
                    W("vector", av.ready)
                    op("vector", sV, lambda e, av=av.res, b=avb[q]:
                       e.tensor_copy(b[:], av[0:64, :]))
                    r_st = op("vector", sV, lambda e, av=av.res: _recl(
                        e, rec1[0:1, :], av[64:65, :]))
                    av.free = r_st
                    W("gpsimd", r_st)
                    d_bc = dma("gpsimd", s_bc, bc2[q][:],
                               bass.AP(rec1, 0, [[1, 1], [0, 64], [1, 512]]))
                    W("vector", d_bc)
                    hd = (hg * 8 + hl) // 2
                    dst = slice(64 * ((hg * 8 + hl) % 2),
                                64 * ((hg * 8 + hl) % 2) + 64)
                    op("vector", sV,
                       lambda e, b=avb[q], hd=hd, dst=dst, j=j, q=q:
                       e.tensor_mul(attnout[hd][dst, ts(j, 512)],
                                    b[:], bc2[q][:]))
            msk_slot.free = r_mask_last
        ctx.flush()
        pa2.close()

    pb5 = ExitStack()
    tr5 = Pool([pb5.enter_context(nc.psum_tensor(f"t5{i}", [128, 128], F32))
                for i in range(2)])
    pr5 = Pool([pb5.enter_context(nc.psum_tensor(f"p5{i}", [128, 512], F32))
                for i in range(2)])
    xhl5_p = Pool([msk_sb[:, 0:2 * D], msk_sb[:, 2 * D:4 * D]])
    wout_bf = [wsl[d][:, 0:1024] for d in range(8)]
    barrier()
    d_wo = None
    for d in range(8):
        d_wo = dma("gpsimd", s_wsl, wout_bf[d], wout_t[ts(d, 128), :])

    for jo in range(NSLOT):
        r_xo = None
        for i in range(4):
            it = jo * 4 + i
            xs = xa_p.next()
            W("sync", xs.free)
            xs.ready = dma("sync", xs.sem, xs.res[:], x_own[ts(it, 128), :])
            W("vector", xs.ready)
            hb = xhl5_p.next()
            W("vector", hb.free)
            op("vector", sV, lambda e, x=xs.res, h=hb.res:
               e.tensor_copy(h[:, 0:D], x[:]))
            r_hl = op("vector", sV, lambda e, x=xs.res, h=hb.res:
                      e.tensor_sub(h[:, D:2 * D], x[:], h[:, 0:D]))
            xs.free = r_hl
            W("tensor", r_hl)
            if it == 0:
                W("tensor", d_wo)
            for d in range(8):
                tp = tr5.next()
                W("tensor", tp.free)
                opq("tensor", lambda e, tp=tp.res, d=d, h=hb.res: e.matmul(
                    tp[:], h[:, ts(d, 128)], ident[:], start=True, stop=False))
                tp.ready = op("tensor", sT, lambda e, tp=tp.res, d=d, h=hb.res:
                              e.matmul(tp[:], h[:, D + d * 128:D + d * 128 + 128],
                                       ident[:], start=False, stop=True))
                hb.free = tp.ready
                W("scalar", tp.ready)
                tp.free = op("scalar", sS, lambda e, tp=tp.res, d=d, i=i: e.copy(
                    xofm[d // 2][:, (d % 2) * 512 + i * 128:
                                 (d % 2) * 512 + i * 128 + 128], tp[:]))
                r_xo = tp.free
        W("tensor", r_xo)
        for c in range(8):
            pp = pr5.next()
            W("tensor", pp.free)
            for d in range(8):
                pp.ready = op("tensor", sT, lambda e, pp=pp.res, c=c, d=d, jo=jo:
                              e.matmul(pp[:], wout_bf[d][:, ts(c, 128)],
                                       attnout[d][:, ts(jo, 512)],
                                       start=(d == 0), stop=(d == 7)))
            W("vector", pp.ready)
            pp.free = op("vector", sV, lambda e, pp=pp.res, c=c, jo=jo:
                         e.scalar_tensor_tensor(
                x_res[c][:, ts(jo, 512)], pp[:], ga_fm[:, c:c + 1],
                xofm[c // 2][:, (c % 2) * 512:(c % 2) * 512 + 512],
                ALU.mult, ALU.add))
    ctx.flush()
    pb5.close()
    sa_stk.close()

    b6 = ExitStack()
    b6sb = lambda n, s, d=F32: b6.enter_context(nc.sbuf_tensor(n, s, d))
    x2tm = [b6sb(f"x2_{i}", [128, D]) for i in range(2)]
    h2tm = [b6sb(f"h2_{i}", [128, D], BF16) for i in range(2)]
    hi6 = [b6sb(f"hi6_{p}", [128, 128], BF16) for p in range(2)]
    lo6 = [b6sb(f"lo6_{p}", [128, 128], BF16) for p in range(2)]
    st6 = b6sb("st6", [128, 16])
    rw_bf = b6sb("rw_bf", [128, 64], BF16)
    sm8 = b6sb("sm8", [128, 192])
    tr6 = Pool([b6.enter_context(nc.psum_tensor(f"t6{i}", [128, 128], F32))
                for i in range(2)])
    lg6 = Pool([b6.enter_context(nc.psum_tensor(f"l6{i}", [128, 8], F32))
                for i in range(4)])

    barrier()
    d_rw = dma("gpsimd", s_gp0, rw_bf[:, 0:64], rw[:, :])

    x2_p = Pool(x2tm)
    h2_p = Pool(h2tm)
    r_h2last = None
    for it in range(16):
        jo, i = divmod(it, 4)
        x2 = x2_p.next()
        h2s = h2_p.next()
        r_tx2 = [None, None]
        for dd in (0, 2, 4, 6):
            r_hl = [None, None]
            for p in (0, 1):
                d = dd + p
                W("vector", r_tx2[p])
                op("vector", sV, lambda e, d=d, jo=jo, i=i, p=p: e.tensor_copy(
                    hi6[p][:],
                    x_res[d][:, jo * 512 + i * 128:jo * 512 + i * 128 + 128]))
            for p in (0, 1):
                d = dd + p
                r_hl[p] = op("vector", sV, lambda e, d=d, jo=jo, i=i, p=p:
                             e.tensor_sub(
                    lo6[p][:],
                    x_res[d][:, jo * 512 + i * 128:jo * 512 + i * 128 + 128],
                    hi6[p][:]))
            for p in (0, 1):
                d = dd + p
                tp = tr6.next()
                W("tensor", r_hl[p])
                W("tensor", tp.free)
                W("tensor", x2.free)
                opq("tensor", lambda e, tp=tp.res, p=p: e.matmul(
                    tp[:], hi6[p][:], ident[:], start=True, stop=False))
                tp.ready = op("tensor", sT, lambda e, tp=tp.res, p=p: e.matmul(
                    tp[:], lo6[p][:], ident[:], start=False, stop=True))
                r_tx2[p] = tp.ready
                W("scalar", tp.ready)
                tp.free = op("scalar", sS, lambda e, tp=tp.res, d=d, x2=x2.res:
                             e.copy(x2[:, ts(d, 128)], tp[:]))
                r_x2 = tp.free
        x2.ready = r_x2
        W("vector", h2s.free)
        r_ln = ln_tile(x2.res, h2s.res, st6, x2.ready)
        op("vector", sV, lambda e, h=h2s.res: e.tensor_mul(h[:], h[:], cbp1_tm[:]))
        r_h2 = op("vector", sV, lambda e, h=h2s.res:
                  e.tensor_add(h[:], h[:], sbm_tm[:]))
        x2.free = r_h2
        W("tensor", r_h2)
        for d in range(8):
            tp = tr6.next()
            W("tensor", tp.free)
            tp.ready = op("tensor", sT, lambda e, tp=tp.res, d=d, h=h2s.res:
                          e.matmul(tp[:], h[:, ts(d, 128)], ident[:],
                                   start=True, stop=True))
            W("scalar", tp.ready)
            tp.free = op("scalar", sS, lambda e, tp=tp.res, d=d, it=it:
                         e.copy(h2_fm[d][:, ts(it, 128)], tp[:]))
            r_h2last = tp.free
        h2s.free = r_h2last

    W("tensor", r_h2last)
    W("tensor", d_rw)
    for tkp in range(4):
        tks = tuple(4 * tkp + q for q in range(4))
        sm = [sm8[:, ts(q, 48)] for q in range(4)]
        lgs = []
        for tk in tks:
            lg = lg6.next()
            W("tensor", lg.free)
            for d in range(8):
                lg.ready = op("tensor", sT, lambda e, lg=lg.res, d=d, tk=tk:
                              e.matmul(lg[:], h2_fm[d][:, ts(tk, 128)],
                                       rw_bf[:, ts(d, 8)],
                                       start=(d == 0), stop=(d == 7)))
            lgs.append(lg)
        for p, lg in enumerate(lgs):
            W("vector", lg.ready)
            a = op("vector", sV, lambda e, lg=lg.res, s=sm[p]:
                   e.reduce_max(s[:, 0:1], lg[:], AX, negate=True))
            W("scalar", a)
            a = op("scalar", sS, lambda e, lg=lg.res, s=sm[p]: e.activation(
                s[:, 8:16], lg[:], ACT.Exp, bias=s[:, 0:1]))
            lg.free = a
            W("vector", a)

        def pair(fn):
            r = None
            for p in range(4):
                r = op("vector", sV, lambda e, s=sm[p], p=p: fn(e, s, p))
            W("vector", r)
            return r
        pair(lambda e, s, p: e.reduce_sum(s[:, 1:2], s[:, 8:16], AX))
        pair(lambda e, s, p: e.reciprocal(s[:, 2:3], s[:, 1:2]))
        pair(lambda e, s, p: e.tensor_mul(
            s[:, 8:16], s[:, 8:16], s[:, 2:3].to_broadcast([128, 8])))
        pair(lambda e, s, p: e.tensor_add(
            s[:, 16:24], s[:, 8:16], rbcol[:, 0:1].to_broadcast([128, 8])))
        pair(lambda e, s, p: e.max(out=s[:, 24:32], in_=s[:, 16:24]))
        pair(lambda e, s, p: e.tensor_tensor(
            s[:, 32:40], s[:, 16:24],
            s[:, 27:28].to_broadcast([128, 8]), ALU.is_ge))
        pair(lambda e, s, p: e.tensor_mul(
            s[:, 32:40], s[:, 32:40], s[:, 8:16]))
        pair(lambda e, s, p: e.reduce_sum(s[:, 3:4], s[:, 32:40], AX))
        pair(lambda e, s, p: e.tensor_scalar(
            s[:, 3:4], s[:, 3:4], 1e-20, None, ALU.add))
        pair(lambda e, s, p: e.reciprocal(s[:, 4:5], s[:, 3:4]))
        pair(lambda e, s, p, _tks=tks: e.tensor_mul(
            wsel_bf[:, ts(_tks[p], 8)], s[:, 32:40],
            s[:, 4:5].to_broadcast([128, 8])))
    ctx.flush()
    b6.close()

    b7 = ExitStack()
    b7sb = lambda n, s, d=BF16: b7.enter_context(nc.sbuf_tensor(n, s, d))
    gu_bf = [b7sb(f"gu{i}", [128, 8192]) for i in range(2)]
    dw_bf = [b7sb(f"dwb{i}", [128, 4096]) for i in range(2)]
    act_e = [b7sb(f"acte{s}", [128, NTOK]) for s in range(4)]
    rep7 = b7sb("rep7", [128, 128])
    wsb7 = [b7sb(f"wsb{i}", [128, 512]) for i in range(2)]
    sg7 = [b7sb(f"sg7{i}", [128, 512]) for i in range(2)]
    gu_ps = Pool([b7.enter_context(nc.psum_tensor(f"g7{i}", [128, 512], F32))
                  for i in range(4)])
    wb_ps = Pool([b7.enter_context(nc.psum_tensor(f"w7{i}", [128, 512], F32))
                  for i in range(2)])
    d8_ps = Pool([b7.enter_context(nc.psum_tensor(f"d8{i}", [128, 512], F32))
                  for i in range(2)])
    gu_p = Pool(gu_bf)
    for i, sl in enumerate(gu_p.slots):
        sl.sem = gu_sems[i]
    dw_p = Pool(dw_bf)
    for i, sl in enumerate(dw_p.slots):
        sl.sem = dw_sems[i]
    wsb_p = Pool(wsb7)
    sg_p = Pool(sg7)
    act_slots = [Slot(a) for a in act_e]

    barrier()
    for e_i in range(NEXP):
        gu = gu_p.next()
        W("gpsimd", gu.free)
        for d in range(8):
            dma("gpsimd", gu.sem, gu.res[:, ts(d, 512)], gwt[e_i, ts(d, 128), :])
        for d in range(8):
            gu.ready = dma("gpsimd", gu.sem,
                           gu.res[:, 4096 + d * 512:4096 + d * 512 + 512],
                           uwt[e_i, ts(d, 128), :])
        dwe = dw_p.next()
        W("gpsimd", dwe.free)
        for s in range(4):
            dwe.ready = dma("gpsimd", dwe.sem, dwe.res[:, ts(s, 1024)],
                            dwt[e_i, ts(s, 128), :])
        r_gu_last = None
        for j in range(NSLOT):
            wj = wsb_p.next()
            wp = wb_ps.next()
            W("tensor", wp.free)
            r_rep = None
            for q4 in range(4):
                tk = j * 4 + q4
                W("vector", r_rep)
                r_rep0 = op("vector", sV, lambda e, tk=tk, e_i=e_i: e.tensor_copy(
                    rep7[:], wsel_bf[:, tk * 8 + e_i:tk * 8 + e_i + 1]
                    .to_broadcast([128, 128])))
                W("tensor", r_rep0)
                r_rep = op("tensor", sT, lambda e, wp=wp.res, q4=q4: e.matmul(
                    wp[:, ts(q4, 128)], rep7[:], ident[:], start=True, stop=True))
            wp.ready = r_rep
            W("vector", wp.ready)
            W("vector", wj.free)
            wj.ready = op("vector", sV, lambda e, wj=wj.res, wp=wp.res:
                          e.tensor_copy(wj[:], wp[:]))
            wp.free = wj.ready
            if j == 0:
                W("tensor", gu.ready)
            for s in range(4):
                pg = gu_ps.next()
                pu = gu_ps.next()
                ac = act_slots[s]
                W("tensor", pg.free)
                W("tensor", pu.free)
                for d in range(8):
                    opq("tensor", lambda e, pg=pg.res, gu=gu.res, d=d, s=s, j=j:
                        e.matmul(pg[:], gu[:, d * 512 + s * 128:d * 512 + s * 128 + 128],
                                 h2_fm[d][:, ts(j, 512)], start=(d == 0), stop=(d == 7)))
                for d in range(8):
                    r_gu = op("tensor", sT,
                              lambda e, pu=pu.res, gu=gu.res, d=d, s=s, j=j:
                              e.matmul(pu[:],
                                       gu[:, 4096 + d * 512 + s * 128:
                                          4096 + d * 512 + s * 128 + 128],
                                       h2_fm[d][:, ts(j, 512)],
                                       start=(d == 0), stop=(d == 7)))
                r_gu_last = r_gu
                sg = sg_p.next()
                W("scalar", r_gu)
                W("scalar", sg.free)
                sg.ready = op("scalar", sS, lambda e, pg=pg.res, sg=sg.res:
                              e.activation(sg[:], pg[:], ACT.Silu))
                pg.free = sg.ready
                W("vector", sg.ready)
                W("vector", wj.ready)
                W("vector", ac.free)
                op("vector", sV, lambda e, pu=pu.res, sg=sg.res, ac=ac.res, s=s, j=j:
                   e.tensor_mul(ac[:, ts(j, 512)], sg[:], pu[:]))
                r_act = op("vector", sV, lambda e, ac=ac.res, wj=wj.res, s=s, j=j:
                           e.tensor_mul(ac[:, ts(j, 512)], ac[:, ts(j, 512)], wj[:]))
                pu.free = r_act
                sg.free = r_act
                ac.ready = r_act
            wj.free = r_act
        gu.free = r_gu_last
        r_dn = None
        for j in range(NSLOT):
            pd = d8_ps.next()
            W("tensor", pd.free)
            if j == 0:
                W("tensor", dwe.ready)
            for s in range(4):
                W("tensor", act_slots[s].ready)
            for c in range(8):
                if c > 0:
                    pd = d8_ps.next()
                    W("tensor", pd.free)
                for s in range(4):
                    pd.ready = op("tensor", sT,
                                  lambda e, pd=pd.res, dw=dwe.res, s=s, c=c, j=j,
                                  st_=(s == 0), sp=(s == 3):
                                  e.matmul(pd[:], dw[:, s * 1024 + c * 128:
                                                    s * 1024 + c * 128 + 128],
                                           act_e[s][:, ts(j, 512)],
                                           start=st_, stop=sp))
                W("vector", pd.ready)
                pd.free = op("vector", sV, lambda e, pd=pd.res, c=c, j=j:
                             e.scalar_tensor_tensor(
                    x_res[c][:, ts(j, 512)], pd[:], gb_fm[:, c:c + 1],
                    x_res[c][:, ts(j, 512)], ALU.mult, ALU.add))
                r_dn = pd.free
        dwe.free = r_dn
        for s in range(4):
            act_slots[s].free = r_dn
    ctx.flush()
    b7.close()

    b9 = ExitStack()
    b9sb = lambda n, s, d=BF16: b9.enter_context(nc.sbuf_tensor(n, s, d))
    sgu_bf = [b9sb(f"sgu{d}", [128, 2048]) for d in range(8)]
    s_act = [b9sb(f"sac{i}", [128, NTOK]) for i in range(8)]
    sdw_bf = [b9sb(f"sdwb{i}", [128, 1024]) for i in range(2)]
    sg9 = [b9sb(f"sg9{i}", [128, 512]) for i in range(2)]
    gu9 = Pool([b9.enter_context(nc.psum_tensor(f"g9{i}", [128, 512], F32))
                for i in range(4)])
    d9_ps = Pool([b9.enter_context(nc.psum_tensor(f"d9{i}", [128, 512], F32))
                  for i in range(2)])
    sdw_p = Pool(sdw_bf)
    for i, sl in enumerate(sdw_p.slots):
        sl.sem = sd_sems[i]
    sg9_p = Pool(sg9)

    prev_g9 = None
    prev_dn9 = None
    barrier()
    for ih in range(2):
        d_sg = None
        if ih == 1:
            W("gpsimd", prev_g9)
        for d in range(8):
            dma("gpsimd", s_sgu, sgu_bf[d][:, 0:1024],
                sgw[ts(d, 128), ih * 1024:ih * 1024 + 1024])
            d_sg = dma("gpsimd", s_sgu, sgu_bf[d][:, 1024:2048],
                       suw[ts(d, 128), ih * 1024:ih * 1024 + 1024])
        if ih == 1:
            W("vector", prev_dn9)
        s_act_ready = [None] * 8
        r_act9 = None
        for ic in range(8):
            for j in range(NSLOT):
                pg = gu9.next()
                pu = gu9.next()
                W("tensor", pg.free)
                W("tensor", pu.free)
                if ic == 0 and j == 0:
                    W("tensor", d_sg)
                for d in range(8):
                    opq("tensor", lambda e, pg=pg.res, d=d, ic=ic, j=j:
                        e.matmul(pg[:], sgu_bf[d][:, ts(ic, 128)],
                                 h2_fm[d][:, ts(j, 512)], start=(d == 0), stop=(d == 7)))
                for d in range(8):
                    r_g9 = op("tensor", sT,
                              lambda e, pu=pu.res, d=d, ic=ic, j=j:
                              e.matmul(pu[:], sgu_bf[d][:, 1024 + ic * 128:
                                                        1024 + ic * 128 + 128],
                                       h2_fm[d][:, ts(j, 512)],
                                       start=(d == 0), stop=(d == 7)))
                sg = sg9_p.next()
                W("scalar", r_g9)
                W("scalar", sg.free)
                sg.ready = op("scalar", sS, lambda e, pg=pg.res, sg=sg.res:
                              e.activation(sg[:], pg[:], ACT.Silu))
                pg.free = sg.ready
                W("vector", sg.ready)
                r_a = op("vector", sV, lambda e, pu=pu.res, sg=sg.res, ic=ic, j=j:
                         e.tensor_mul(s_act[ic][:, ts(j, 512)], sg[:], pu[:]))
                pu.free = r_a
                sg.free = r_a
                r_act9 = r_a
            s_act_ready[ic] = r_act9
        r_dn9 = None
        for c in range(8):
            sd = sdw_p.next()
            W("gpsimd", sd.free)
            sd.ready = dma("gpsimd", sd.sem, bass.AP(sd.res, 0,
                                                 [[1024, 128], [128, 8], [1, 128]]),
                           bass.AP(sdw.tensor, ih * 1024 * 1024 + c * 128,
                                   [[1024, 128], [128 * 1024, 8], [1, 128]]))
            for j in range(NSLOT):
                pd = d9_ps.next()
                W("tensor", pd.free)
                if j == 0:
                    W("tensor", sd.ready)
                    for ic in range(8):
                        W("tensor", s_act_ready[ic])
                for ic in range(8):
                    pd.ready = op("tensor", sT,
                                  lambda e, pd=pd.res, sd=sd.res, ic=ic, j=j,
                                  st_=(ic == 0), sp=(ic == 7):
                                  e.matmul(pd[:], sd[:, ts(ic, 128)],
                                           s_act[ic][:, ts(j, 512)],
                                           start=st_, stop=sp))
                W("vector", pd.ready)
                pd.free = op("vector", sV, lambda e, pd=pd.res, c=c, j=j:
                             e.scalar_tensor_tensor(
                    x_res[c][:, ts(j, 512)], pd[:], gb_fm[:, c:c + 1],
                    x_res[c][:, ts(j, 512)], ALU.mult, ALU.add))
                r_dn9 = pd.free
            sd.free = r_dn9
        prev_g9 = r_g9
        prev_dn9 = r_dn9
    ctx.flush()
    b9.close()

    b10 = ExitStack()
    yhl = [b10.enter_context(nc.sbuf_tensor(f"yhl{i}", [128, 256], BF16))
           for i in range(2)]
    y_sb = [b10.enter_context(nc.sbuf_tensor(f"ysb{i}", [128, D], F32))
            for i in range(2)]
    trA = Pool([b10.enter_context(nc.psum_tensor(f"tA{i}", [128, 128], F32))
                for i in range(2)])
    yhl_p = Pool(yhl)
    ysb_slots = [Slot(t) for t in y_sb]
    for i, sl in enumerate(ysb_slots):
        sl.sem = ys_sems[i]
    barrier()
    for jo in range(NSLOT):
        for i in range(4):
            ysl = ysb_slots[i % 2]
            r_tile = None
            for dd in (0, 2, 4, 6):
                hbs = [yhl_p.next(), yhl_p.next()]
                for p in (0, 1):
                    d = dd + p
                    W("vector", hbs[p].free)
                    op("vector", sV, lambda e, hb=hbs[p].res, d=d, jo=jo, i=i:
                       e.tensor_copy(hb[:, 0:128],
                                     x_res[d][:, jo * 512 + i * 128:
                                              jo * 512 + i * 128 + 128]))
                for p in (0, 1):
                    d = dd + p
                    hbs[p].ready = op("vector", sV,
                                      lambda e, hb=hbs[p].res, d=d, jo=jo, i=i:
                                      e.tensor_sub(
                        hb[:, 128:256],
                        x_res[d][:, jo * 512 + i * 128:jo * 512 + i * 128 + 128],
                        hb[:, 0:128]))
                for p in (0, 1):
                    d = dd + p
                    hb = hbs[p]
                    tp = trA.next()
                    W("tensor", hb.ready)
                    W("tensor", tp.free)
                    if d == 0:
                        W("tensor", ysl.free)
                    opq("tensor", lambda e, tp=tp.res, hb=hb.res: e.matmul(
                        tp[:], hb[:, 0:128], ident[:], start=True, stop=False))
                    tp.ready = op("tensor", sT, lambda e, tp=tp.res, hb=hb.res:
                                  e.matmul(tp[:], hb[:, 128:256], ident[:],
                                           start=False, stop=True))
                    hb.free = tp.ready
                    W("scalar", tp.ready)
                    tp.free = op("scalar", sS,
                                 lambda e, tp=tp.res, d=d, ysl=ysl.res:
                                 e.copy(ysl[:, ts(d, 128)], tp[:]))
                    r_tile = tp.free
            ysl.ready = r_tile
            W("sync", ysl.ready)
            ysl.free = dma("sync", ysl.sem,
                           y[jo * 512 + i * 128: jo * 512 + i * 128 + 128, :],
                           ysl.res[:])
    opq("sync", lambda e, v0=ys_sems[0].n, v1=ys_sems[1].n:
        (e.wait_ge(ys_sems[0].h, v0), e.wait_ge(ys_sems[1].h, v1)))
    ctx.flush()
    b10.close()
    root.close()
    return nc


_PROG = None


def _get_prog():
    global _PROG
    if _PROG is None:
        _PROG = build_program()
    return _PROG


def kernel(L0_attn_x, L0_attn_w_qkv, L0_attn_w_out, L0_ffn_router_w,
           L0_ffn_router_bias, L0_ffn_gate_w, L0_ffn_up_w, L0_ffn_down_w,
           L0_ffn_s_gate_w, L0_ffn_s_up_w, L0_ffn_s_down_w, L0_c, L0_wmod):
    bf = ml_dtypes.bfloat16
    x = np.ascontiguousarray(np.asarray(L0_attn_x, dtype=np.float32))
    wqkv_t = np.ascontiguousarray(np.asarray(L0_attn_w_qkv, np.float32).T)
    wout_t = np.ascontiguousarray(np.asarray(L0_attn_w_out, np.float32).T)
    wmod = np.ascontiguousarray(np.asarray(L0_wmod, np.float32))
    cmat = np.ascontiguousarray(np.asarray(L0_c, np.float32))
    rbv = np.asarray(L0_ffn_router_bias, np.float32).reshape(1, 1)
    rwm = np.asarray(L0_ffn_router_w, np.float32)
    rw_fm = np.ascontiguousarray(
        rwm.reshape(8, 128, 8).transpose(1, 0, 2).reshape(128, 64))
    gwm = np.ascontiguousarray(np.asarray(L0_ffn_gate_w, np.float32))
    uwm = np.ascontiguousarray(np.asarray(L0_ffn_up_w, np.float32))
    dwm = np.ascontiguousarray(np.asarray(L0_ffn_down_w, np.float32))
    sgwm = np.ascontiguousarray(np.asarray(L0_ffn_s_gate_w, np.float32))
    suwm = np.ascontiguousarray(np.asarray(L0_ffn_s_up_w, np.float32))
    sdwm = np.ascontiguousarray(np.asarray(L0_ffn_s_down_w, np.float32))

    mask_h = {}
    for half in (0, 1):
        m = np.zeros((NSLOT, 128, 4096), dtype=bf)
        for sl in range(NSLOT):
            gq = OWN_TGS[half][sl]
            for mc in range(8):
                kc = 4 * (EXT[sl] - 2) + mc
                keys = kc * 128 + np.arange(128)[:, None]
                qs = gq * 512 + np.arange(512)[None, :]
                m[sl, :, mc * 512:(mc + 1) * 512] = (keys <= qs).astype(bf)
        mask_h[half] = m

    nc = _get_prog()
    in_maps = []
    for core in range(8):
        b, half = core // 2, core % 2
        tgs = OWN_TGS[half]
        x_b = np.ascontiguousarray(x[b])
        x_own = np.ascontiguousarray(
            np.concatenate([x[b, ts(t, 512), :] for t in tgs], axis=0))
        in_maps.append({
            "x_full": x_b, "x_own": x_own, "wqkv_t": wqkv_t, "wout_t": wout_t,
            "wmod": wmod,
            "c_fm_in": np.ascontiguousarray(cmat[b].reshape(8, 128).T),
            "rb": rbv, "rw": rw_fm, "gwt": gwm, "uwt": uwm, "dwt": dwm,
            "sgw": sgwm, "suw": suwm, "sdw": sdwm, "masks": mask_h[half],
        })
    res = run_bass_kernel_spmd(nc, in_maps, core_ids=list(range(8)))
    out = np.empty((B, T, D), dtype=np.float32)
    for core in range(8):
        b, half = core // 2, core % 2
        yv = res.results[core]["y"]
        for sl, tg in enumerate(OWN_TGS[half]):
            out[b, ts(tg, 512), :] = yv[ts(sl, 512), :]
    return out


# version 25
# speedup vs baseline: 1.1868x; 1.0094x over previous
import numpy as np
import ml_dtypes
from contextlib import ExitStack

import concourse.bass as bass
import concourse.mybir as mybir
from concourse.bass_utils import run_bass_kernel_spmd

F32 = mybir.dt.float32
BF16 = mybir.dt.bfloat16
AX = mybir.AxisListType.X
ALU = mybir.AluOpType
ACT = mybir.ActivationFunctionType

B, T, D = 4, 4096, 1024
NTOK = 2048
NSLOT = 4
EXT = [2, 4, 6, 8]
OWN_TGS = {0: [0, 3, 4, 7], 1: [1, 2, 5, 6]}
SHIFT = 20.0
NEXP, IEXP, ISH = 8, 512, 2048
EPS = 1e-5


def ts(i, n):
    return slice(i * n, (i + 1) * n)


class Sem:
    def __init__(self, stk, nc, name):
        self.h = stk.enter_context(nc.semaphore(name))
        self.n = 0


class Ctx:
    ENGINES = ("sync", "gpsimd", "tensor", "vector", "scalar")

    def __init__(self, nc):
        self.nc = nc
        self.ops = {e: [] for e in self.ENGINES}

    def emit(self, eng, fn):
        self.ops[eng].append(fn)

    def wait(self, eng, dep):
        if dep is None:
            return
        sem, val = dep
        if val <= 0:
            return
        assert val <= sem.n, f"wait {val} > emitted {sem.n} on {eng}"
        self.ops[eng].append(lambda e, s=sem.h, v=val: e.wait_ge(s, v))

    def flush(self):
        with self.nc.Block() as blk:
            for name in self.ENGINES:
                lst = self.ops[name]
                if lst:
                    getattr(blk, name)(
                        lambda e, _l=tuple(lst): [f(e) for f in _l])
        self.ops = {e: [] for e in self.ENGINES}


class Slot:
    def __init__(self, res, sem=None):
        self.res = res
        self.sem = sem
        self.free = None
        self.ready = None


class Pool:
    def __init__(self, resources):
        self.slots = [Slot(r) for r in resources]
        self.i = 0

    def next(self):
        s = self.slots[self.i % len(self.slots)]
        self.i += 1
        return s


def build_program():
    nc = bass.Bass("TRN2", target_bir_lowering=False, debug=False,
                   detect_race_conditions=False)
    dri = lambda name, shape, dt=F32: nc.dram_tensor(
        name, shape, dt, kind="ExternalInput").ap()
    x_full = dri("x_full", [T, D])
    x_own = dri("x_own", [NTOK, D])
    wqkv_t = dri("wqkv_t", [D, 3 * D])
    wout_t = dri("wout_t", [D, D])
    wmod = dri("wmod", [D, 6 * D])
    c_fm_in = dri("c_fm_in", [128, 8])
    rb = dri("rb", [1, 1])
    rw = dri("rw", [128, 64])
    gwt = dri("gwt", [NEXP, D, IEXP])
    uwt = dri("uwt", [NEXP, D, IEXP])
    dwt = dri("dwt", [NEXP, IEXP, D])
    sgw = dri("sgw", [D, ISH])
    suw = dri("suw", [D, ISH])
    sdw = dri("sdw", [ISH, D])
    masks = dri("masks", [NSLOT, 128, 4096], BF16)
    y = nc.dram_tensor("y", [NTOK, D], F32, kind="ExternalOutput").ap()
    mod_dram = nc.dram_tensor("mod_scratch", [1, 4 * D], F32, kind="Internal").ap()

    root = ExitStack()
    rsb = lambda n, s, d=F32: root.enter_context(nc.sbuf_tensor(n, s, d))

    K_bf = [rsb(f"K{p}", [128, T], BF16) for p in range(4)]
    V_bf = rsb("Vb", [128, 32 * 520], BF16)
    attnout = [rsb(f"ao{d}", [128, NTOK], BF16) for d in range(8)]
    ident = rsb("ident", [128, 128], BF16)
    ones_bf = rsb("ones", [128, 512], BF16)
    brow = rsb("brow", [1, 3 * D], BF16)
    cbp1_tm = rsb("cbp1", [128, D])
    sbm_tm = rsb("sbm", [128, D], BF16)
    cap1_fm = rsb("cap1", [128, 8])
    ga_fm = rsb("ga", [128, 8])
    gb_fm = rsb("gb", [128, 8])
    neg20 = rsb("neg20", [128, 1])
    epsc = rsb("epsc", [128, 1])
    rbcol = rsb("rbcol", [128, 1])
    wsel_bf = rsb("wsel", [128, 128], BF16)
    sa_bf = rsb("sa_bf", [128, 8], BF16)
    x_res = [K_bf[d][:].bitcast(F32) for d in range(4)] + \
            [V_bf[:].bitcast(F32)[:, ts(i, NTOK)] for i in range(4)]
    h2_fm = attnout

    ctx = Ctx(nc)
    sG = Sem(root, nc, "sG")
    sT = Sem(root, nc, "sT")
    sV = Sem(root, nc, "sV")
    sS = Sem(root, nc, "sS")
    s_gp0 = Sem(root, nc, "s_gp0")
    s_wqkv = Sem(root, nc, "s_wqkv")
    s_wsl = Sem(root, nc, "s_wsl")
    s_sgu = Sem(root, nc, "s_sgu")
    wm_sems = [Sem(root, nc, f"s_wm{i}") for i in range(4)]
    xa_sems = [Sem(root, nc, f"s_xa{i}") for i in range(4)]
    s_msk = Sem(root, nc, "s_msk")
    gu_sems = [Sem(root, nc, f"s_gu{i}") for i in range(2)]
    dw_sems = [Sem(root, nc, f"s_dwe{i}") for i in range(2)]
    dc_sems = [Sem(root, nc, f"s_dc{i}") for i in range(2)]
    sd_sems = [Sem(root, nc, f"s_sd{i}") for i in range(2)]
    ys_sems = [Sem(root, nc, f"s_ys{i}") for i in range(2)]
    s_bc = Sem(root, nc, "s_bc")

    def dma(eng, sem, out, in_, slow=False):
        def f(e, o=out, i=in_, s=sem.h, sl=slow):
            kw = {"allow_slow_non_contiguous": True} if sl else {}
            e.dma_start(o, i, **kw).then_inc(s, 16)
        ctx.emit(eng, f)
        sem.n += 16
        return (sem, sem.n)

    def op(eng, sem, fn):
        def f(e, fn=fn, s=sem.h):
            fn(e).then_inc(s, 1)
        ctx.emit(eng, f)
        sem.n += 1
        return (sem, sem.n)

    def opq(eng, fn):
        ctx.emit(eng, fn)

    W = ctx.wait

    def _recl(e, out_ap, in_ap):
        with nc.allow_low_precision(reason="softmax denom recip to bf16"):
            return e.reciprocal(out_ap, in_ap)

    def barrier():
        snap = [(sT, sT.n), (sV, sV.n), (sS, sS.n)]
        for eng in Ctx.ENGINES:
            for dep in snap:
                W(eng, dep)

    b0 = ExitStack()
    bsb = lambda n, s, d=F32: b0.enter_context(nc.sbuf_tensor(n, s, d))
    bps = lambda n, s: b0.enter_context(nc.psum_tensor(n, s, F32))
    io_col = bsb("io_col", [128, 128])
    io_par = bsb("io_par", [128, 1])
    c_fm = bsb("c_fm", [128, 8])
    sil = bsb("sil", [128, 8])
    sil_hi = bsb("sil_hi", [128, 8], BF16)
    sil_lo = bsb("sil_lo", [128, 8], BF16)
    rep_hi = [bsb(f"rph{d}", [128, 128], BF16) for d in range(8)]
    rep_lo = [bsb(f"rpl{d}", [128, 128], BF16) for d in range(8)]
    wm_st = [bsb(f"wm{i}", [128, 512]) for i in range(4)]
    wm_hl = [bsb(f"wmh{i}", [128, 1024], BF16) for i in range(2)]
    modrow = bsb("modrow", [1, 4 * D])
    sga = bsb("sga", [128, 8])
    sgb = bsb("sgb", [128, 8])
    aba = bsb("aba", [128, 8])
    abb = bsb("abb", [128, 8])
    rb_sb = bsb("rb_sb", [1, 1], BF16)
    m32 = bsb("m32", [32, 128])
    m32h = bsb("m32h", [32, 128], BF16)
    m32l = bsb("m32l", [32, 128], BF16)
    mfm = bsb("mfm", [128, 32])
    mfm_ps = b0.enter_context(nc.psum_tensor("pmfm", [128, 32], F32))
    mod_ps = Pool([bps(f"p0_{i}", [128, 512]) for i in range(2)])
    rbc_ps = bps("prbc", [128, 512])

    d_iota = op("gpsimd", sG, lambda e: e.iota(
        io_col[:], [[1, 128]], channel_multiplier=0,
        allow_small_or_imprecise_dtypes=True))
    d_iota = op("gpsimd", sG, lambda e: e.iota(
        io_par[:], [[0, 1]], channel_multiplier=1,
        allow_small_or_imprecise_dtypes=True))
    W("vector", d_iota)
    r_ident = op("vector", sV, lambda e: e.tensor_scalar(
        ident[:], io_col[:], io_par[:, 0:1], None, ALU.is_equal))
    op("vector", sV, lambda e: e.memset(ones_bf[:], 1.0))
    op("vector", sV, lambda e: e.memset(epsc[:], EPS))
    r_ones = op("vector", sV, lambda e: e.memset(neg20[:], -SHIFT))

    dma("gpsimd", s_gp0, c_fm[:, 0:8], c_fm_in[:, :])
    d_crb = dma("gpsimd", s_gp0, rb_sb[0:1, 0:1], rb[0:1, 0:1])
    W("scalar", d_crb)
    r_sil = op("scalar", sS, lambda e: e.activation(sil[:], c_fm[:], ACT.Silu))
    W("vector", r_sil)
    a = op("vector", sV, lambda e: e.tensor_copy(sil_hi[:], sil[:]))
    W("vector", a)
    for d in range(8):
        r_rep = op("vector", sV, lambda e, d=d: e.tensor_copy(
            rep_hi[d][:], sil_hi[:, d:d + 1].to_broadcast([128, 128])))

    st_slots = Pool(wm_st)
    for i, sl in enumerate(st_slots.slots):
        sl.sem = wm_sems[i]
        sl.q = "sync" if i < 2 else "gpsimd"
    hl_slots = Pool(wm_hl)
    r_mod = {}
    for g in range(12):
        mp = mod_ps.next()
        W("tensor", mp.free)
        for d in range(8):
            k = g * 8 + d
            st = st_slots.next()
            W(st.q, st.free)
            st.ready = dma(st.q, st.sem, st.res[:], wmod[ts(d, 128), ts(g, 512)])
            hl = hl_slots.next()
            W("vector", st.ready)
            W("vector", hl.free)
            hl.ready = op("vector", sV, lambda e, a=hl.res, b=st.res:
                          e.tensor_copy(a[:, 0:512], b[:]))
            st.free = hl.ready
            W("tensor", hl.ready)
            if k == 0:
                W("tensor", r_rep)
            hl.free = op("tensor", sT, lambda e, d=d, a=hl.res, mp=mp.res,
                         st_=(d == 0), sp=(d == 7):
                         e.matmul(mp[:], rep_hi[d][:], a[:, 0:512],
                                  start=st_, stop=sp))
        mp.ready = hl.free
        m, hh = g // 2, g % 2
        W("vector", mp.ready)
        if m == 0:
            ev = op("vector", sV, lambda e, p=mp.res, hh=hh:
                    e.tensor_copy(modrow[0:1, ts(hh, 512)], p[0:1, :]))
        elif m == 1:
            ev = op("vector", sV, lambda e, p=mp.res, hh=hh: e.tensor_copy(
                modrow[0:1, 1024 + hh * 512:1024 + hh * 512 + 512], p[0:1, :]))
        elif m == 2:
            ev = op("vector", sV, lambda e, p=mp.res, hh=hh: e.tensor_copy(
                modrow[0:1, 2048 + hh * 512:2048 + hh * 512 + 512], p[0:1, :]))
        elif m == 3:
            ev = op("vector", sV, lambda e, p=mp.res, hh=hh:
                    e.tensor_copy(sbm_tm[:, ts(hh, 512)], p[:]))
        elif m == 4:
            ev = op("vector", sV, lambda e, p=mp.res, hh=hh: e.tensor_scalar(
                cbp1_tm[:, ts(hh, 512)], p[:], 1.0, None, ALU.add))
        else:
            ev = op("vector", sV, lambda e, p=mp.res, hh=hh: e.tensor_copy(
                modrow[0:1, 3072 + hh * 512:3072 + hh * 512 + 512], p[0:1, :]))
        mp.free = ev
        r_mod[m * 2 + hh] = ev

    W("gpsimd", r_mod[11])
    d_bounce = dma("gpsimd", s_gp0, mod_dram[0:1, :], modrow[0:1, :])
    W("gpsimd", d_bounce)
    d_m32 = dma("gpsimd", s_gp0, m32[:, :],
                bass.AP(mod_dram.tensor, 0, [[128, 32], [1, 128]]))
    W("vector", d_m32)
    r_m32 = op("vector", sV, lambda e: e.tensor_copy(m32h[:], m32[:]))
    W("tensor", r_m32)
    W("tensor", r_ident)
    t_mfm = op("tensor", sT, lambda e: e.matmul(mfm_ps[:], m32h[:],
                                                ident[0:32, 0:32],
                                                start=True, stop=True))
    W("vector", t_mfm)
    r_mfm = op("vector", sV, lambda e: e.tensor_copy(mfm[:], mfm_ps[0:128, 0:32]))
    W("vector", r_mfm)
    r_cap1 = op("vector", sV, lambda e: e.tensor_scalar(
        cap1_fm[:], mfm[:, 8:16], 1.0, None, ALU.add))
    r_sabf = op("vector", sV, lambda e: e.tensor_copy(sa_bf[:], mfm[:, 0:8]))
    op("vector", sV, lambda e: e.tensor_copy(ga_fm[:], mfm[:, 16:24]))
    r_gcp = op("vector", sV, lambda e: e.tensor_copy(gb_fm[:], mfm[:, 24:32]))
    W("scalar", r_gcp)
    op("scalar", sS, lambda e: e.activation(sga[:], ga_fm[:], ACT.Sign))
    op("scalar", sS, lambda e: e.activation(sgb[:], gb_fm[:], ACT.Sign))
    op("scalar", sS, lambda e: e.activation(aba[:], ga_fm[:], ACT.Abs))
    a_gab = op("scalar", sS, lambda e: e.activation(abb[:], gb_fm[:], ACT.Abs))
    W("vector", a_gab)
    op("vector", sV, lambda e: e.tensor_scalar(
        aba[:], aba[:], 0.25, None, ALU.max))
    a = op("vector", sV, lambda e: e.tensor_scalar(
        abb[:], abb[:], 0.25, None, ALU.max))
    W("vector", a)
    op("vector", sV, lambda e: e.tensor_mul(ga_fm[:], sga[:], aba[:]))
    r_gab = op("vector", sV, lambda e: e.tensor_mul(gb_fm[:], sgb[:], abb[:]))

    W("tensor", d_crb)
    W("tensor", r_ones)
    t_rbc = op("tensor", sT, lambda e: e.matmul(
        rbc_ps[0:128, 0:1], ones_bf[0:1, 0:128], rb_sb[0:1, 0:1],
        start=True, stop=True))
    W("vector", t_rbc)
    op("vector", sV, lambda e: e.tensor_copy(rbcol[:], rbc_ps[0:128, 0:1]))

    ctx.flush()
    b0.close()
    barrier()

    sa_stk = ExitStack()
    asb = lambda n, s, d=F32: sa_stk.enter_context(nc.sbuf_tensor(n, s, d))
    Q_bf = [asb(f"Q{p}", [128, NTOK], BF16) for p in range(4)]
    wsl = [asb(f"wsl{d}", [128, 1536], BF16) for d in range(8)]
    xa = [asb(f"xa{i}", [128, D]) for i in range(4)]
    xnb = [asb(f"xn{i}", [128, D], BF16) for i in range(2)]
    xfm = [[asb(f"xf{s}_{d}", [128, 512], BF16) for d in range(8)]
           for s in range(2)]
    st8 = asb("st8", [128, 32])
    P_sb = [asb(f"P{i}", [128, 1024], BF16) for i in range(4)]
    msk_sb = asb("msk", [128, 4096], BF16)
    avb_st = [asb(f"avst{i}", [64, 512], BF16) for i in range(2)]
    bc2 = [asb(f"bc2_{i}", [64, 512], BF16) for i in range(2)]
    rec1 = asb("rec1", [1, 512], BF16)

    xofm = [Q_bf[p][:].bitcast(F32) for p in range(4)]

    def ln_group(tiles):
        n = len(tiles)
        for k, (xt, xnt, st, dep, nfree) in enumerate(tiles):
            W("vector", dep)
            op("vector", sV, lambda e, xt=xt, st=st:
               e.reduce_sum(st[:, 0:1], xt[:], AX))
        a = None
        for k, (xt, xnt, st, dep, nfree) in enumerate(tiles):
            W("scalar", dep)
            W("scalar", nfree)
            a = op("scalar", sS, lambda e, xt=xt, xnt=xnt, st=st: e.activation(
                xnt[:], xt[:], ACT.Square, accum_out=st[:, 1:2]))
        W("vector", a)
        for k, (xt, xnt, st, dep, nfree) in enumerate(tiles):
            op("vector", sV, lambda e, st=st: e.tensor_scalar(
                st[:, 5:6], st[:, 1:2], 1.0 / D, EPS, ALU.mult, ALU.add))
        a = None
        for k, (xt, xnt, st, dep, nfree) in enumerate(tiles):
            a = op("vector", sV, lambda e, st=st: e.tensor_scalar(
                st[:, 2:3], st[:, 0:1], 1.0 / D, None, ALU.mult))
        if n < 4:
            W("vector", a)
        for k, (xt, xnt, st, dep, nfree) in enumerate(tiles):
            a = op("vector", sV, lambda e, st=st: e.tensor_mul(
                st[:, 3:4], st[:, 2:3], st[:, 2:3]))
        if n < 4:
            W("vector", a)
        for k, (xt, xnt, st, dep, nfree) in enumerate(tiles):
            a = op("vector", sV, lambda e, st=st: e.scalar_tensor_tensor(
                st[:, 4:5], st[:, 3:4], -1.0, st[:, 5:6], ALU.mult, ALU.add))
        W("scalar", a)
        for k, (xt, xnt, st, dep, nfree) in enumerate(tiles):
            a = op("scalar", sS, lambda e, st=st: e.activation(
                st[:, 6:7], st[:, 4:5], ACT.Sqrt))
        W("vector", a)
        for k, (xt, xnt, st, dep, nfree) in enumerate(tiles):
            a = op("vector", sV, lambda e, st=st: e.reciprocal(
                st[:, 7:8], st[:, 6:7]))
        if n < 4:
            W("vector", a)
        fin = [None] * n
        for k, (xt, xnt, st, dep, nfree) in enumerate(tiles):
            W("vector", nfree)
            fin[k] = op("vector", sV, lambda e, xt=xt, xnt=xnt, st=st:
                        e.scalar_tensor_tensor(
                xnt[:], xt[:], st[:, 2:3],
                st[:, 7:8].to_broadcast([128, D]), ALU.subtract, ALU.mult))
        return fin

    def ln_tile(xt, xnt, stats, dep_x):
        return ln_group([(xt, xnt, stats, dep_x, None)])[0]

    xa_p = Pool(xa)
    for i, sl in enumerate(xa_p.slots):
        sl.sem = xa_sems[i]
    xn_p = Pool(xnb + [msk_sb[:, ts(i, 1024)] for i in range(4)])
    for hg in range(2):
        barrier()
        pa = ExitStack()
        tr_ps = Pool([pa.enter_context(nc.psum_tensor(f"tr{hg}_{i}", [128, 128], F32))
                      for i in range(2)])
        kv_ps = Pool([pa.enter_context(nc.psum_tensor(f"kv{hg}_{i}", [128, 512], F32))
                      for i in range(4)])

        d_w = None
        for d in range(8):
            dma("gpsimd", s_wsl, wsl[d][:, 0:512],
                wqkv_t[ts(d, 128), hg * 512:hg * 512 + 512])
            dma("gpsimd", s_wsl, wsl[d][:, 512:1024],
                wqkv_t[ts(d, 128), 1024 + hg * 512:1024 + hg * 512 + 512])
            d_w = dma("gpsimd", s_wsl, wsl[d][:, 1024:1536],
                      wqkv_t[ts(d, 128), 2048 + hg * 512:2048 + hg * 512 + 512])
        br_ps = Pool([pa.enter_context(nc.psum_tensor(f"br{hg}_{i}", [1, 512],
                                                      F32)) for i in range(2)])
        W("tensor", d_w)
        W("tensor", r_sabf)
        r_brow = None
        t_brow = None
        for part in range(3):
            bp = br_ps.next()
            W("tensor", bp.free)
            for d in range(8):
                t_brow = op("tensor", sT,
                            lambda e, d=d, part=part, bp=bp.res:
                            e.matmul(bp[:], sa_bf[:, d:d + 1],
                                     wsl[d][:, ts(part, 512)],
                                     start=(d == 0), stop=(d == 7)))
            bp.ready = t_brow
            W("vector", bp.ready)
            dstb = [hg * 512, 1024 + hg * 512, 2048 + hg * 512][part]
            if part == 0:
                ev = op("vector", sV, lambda e, bp=bp.res, dstb=dstb:
                        e.tensor_scalar(brow[0:1, dstb:dstb + 512],
                                        bp[0:1, :], 0.125, None, ALU.mult))
            else:
                ev = op("vector", sV, lambda e, bp=bp.res, dstb=dstb:
                        e.tensor_copy(brow[0:1, dstb:dstb + 512], bp[0:1, :]))
            bp.free = ev
            r_brow = ev
        W("vector", t_brow)
        W("vector", d_w)
        W("vector", r_cap1)
        for d in range(8):
            op("vector", sV, lambda e, d=d: e.tensor_mul(
                wsl[d][:], wsl[d][:],
                cap1_fm[:, d:d + 1].to_broadcast([128, 1536])))
            r_wsl = op("vector", sV, lambda e, d=d: e.tensor_scalar(
                wsl[d][:, 0:512], wsl[d][:, 0:512], 0.125, None, ALU.mult))
        for kc in range(32):
            r_vones = op("vector", sV, lambda e, kc=kc: e.memset(
                bass.AP(V_bf, 520 * kc + 64, [[32 * 520, 128], [65, 8], [1, 1]]),
                1.0))

        xfm_ready = [None, None]
        xfm_free = [None, None]

        def emit_kv(s2, qpass, grp):
            W("tensor", xfm_ready[s2])
            r_cons = None
            if not qpass:
                tg = grp
                for p in range(4):
                    kp = kv_ps.next()
                    W("tensor", kp.free)
                    for d in range(8):
                        opq("tensor", lambda e, kp=kp.res, d=d, p=p, s2=s2:
                            e.matmul(kp[:], wsl[d][:, 512 + p * 128:512 + p * 128 + 128],
                                     xfm[s2][d][:], start=(d == 0), stop=False))
                    kp.ready = op("tensor", sT, lambda e, kp=kp.res, p=p, hg=hg:
                                  e.matmul(kp[:],
                                           brow[0:1, 1024 + hg * 512 + p * 128:
                                                1024 + hg * 512 + p * 128 + 128],
                                           ones_bf[0:1, :], start=False, stop=True))
                    W("vector", kp.ready)
                    kp.free = op("vector", sV, lambda e, kp=kp.res, p=p, tg=tg:
                                 e.tensor_copy(K_bf[p][:, ts(tg, 512)], kp[:]))
                    r_cons = kp.free
                for kk in range(4):
                    kc = tg * 4 + kk
                    vp = kv_ps.next()
                    W("tensor", vp.free)
                    for d in range(8):
                        opq("tensor", lambda e, vp=vp.res, d=d, kk=kk, s2=s2:
                            e.matmul(vp[:], xfm[s2][d][:, ts(kk, 128)],
                                     wsl[d][:, 1024:1536], start=(d == 0), stop=False))
                    vp.ready = op("tensor", sT, lambda e, vp=vp.res, hg=hg:
                                  e.matmul(vp[:], ones_bf[0:1, 0:128],
                                           brow[0:1, 2048 + hg * 512:
                                                2048 + hg * 512 + 512],
                                           start=False, stop=True))
                    W("vector", vp.ready)
                    vp.free = op("vector", sV, lambda e, vp=vp.res, kc=kc:
                                 e.tensor_copy(
                        bass.AP(V_bf, 520 * kc, [[32 * 520, 128], [65, 8], [1, 64]]),
                        vp[:]))
                    r_cons = vp.free
            else:
                jo = grp - 8
                for p in range(4):
                    qp = kv_ps.next()
                    W("tensor", qp.free)
                    for d in range(8):
                        opq("tensor", lambda e, qp=qp.res, d=d, p=p, s2=s2:
                            e.matmul(qp[:], wsl[d][:, p * 128:p * 128 + 128],
                                     xfm[s2][d][:], start=(d == 0), stop=False))
                    qp.ready = op("tensor", sT, lambda e, qp=qp.res, p=p, hg=hg:
                                  e.matmul(qp[:],
                                           brow[0:1, hg * 512 + p * 128:
                                                hg * 512 + p * 128 + 128],
                                           ones_bf[0:1, :], start=False, stop=True))
                    W("vector", qp.ready)
                    qp.free = op("vector", sV, lambda e, qp=qp.res, p=p, jo=jo:
                                 e.tensor_copy(Q_bf[p][:, ts(jo, 512)], qp[:]))
                    r_cons = qp.free
            xfm_free[s2] = r_cons

        pend = None
        for grp in range(12):
            qpass = grp >= 8
            s2 = grp % 2
            group, xss, nss = [], [], []
            for i in range(4):
                it = grp * 4 + i
                src = (x_own[ts(it - 32, 128), :] if qpass
                       else x_full[ts(it, 128), :])
                xs = xa_p.next()
                W("sync", xs.free)
                xs.ready = dma("sync", xs.sem, xs.res[:], src)
                ns = xn_p.next()
                xss.append(xs)
                nss.append(ns)
                group.append((xs.res, ns.res, st8[:, 8 * i:8 * i + 8],
                              xs.ready, ns.free))
            fins = ln_group(group)
            for i in range(4):
                xss[i].free = fins[i]
            if grp == 0:
                W("tensor", r_wsl)
                W("tensor", r_vones)
                W("tensor", r_brow)
                W("tensor", r_ident)
            W("tensor", xfm_free[s2])
            for i in range(4):
                ns = nss[i]
                W("tensor", fins[i])
                for d in range(8):
                    tp = tr_ps.next()
                    W("tensor", tp.free)
                    tp.ready = op("tensor", sT,
                                  lambda e, tp=tp.res, x=ns.res, d=d:
                                  e.matmul(tp[:], x[:, ts(d, 128)], ident[:],
                                           start=True, stop=True))
                    W("scalar", tp.ready)
                    tp.free = op("scalar", sS,
                                 lambda e, tp=tp.res, d=d, s2=s2, i=i:
                                 e.copy(xfm[s2][d][:, ts(i, 128)], tp[:]))
                    r_evac = tp.free
                ns.free = r_evac
            xfm_ready[s2] = r_evac
            if pend is not None:
                emit_kv(*pend)
            pend = (s2, qpass, grp)
        emit_kv(*pend)
        ctx.flush()
        pa.close()
        barrier()

        pa2 = ExitStack()
        sc_ps = Pool([pa2.enter_context(nc.psum_tensor(f"sc{hg}_{i}", [128, 1024], F32))
                      for i in range(3)])
        av_ps = Pool([pa2.enter_context(nc.psum_tensor(f"av{hg}_{i}", [65, 512], F32))
                      for i in range(2)])
        P_p = Pool(P_sb)
        msk_slot = Slot(msk_sb)

        for j in range(NSLOT):
            W("sync", msk_slot.free)
            msk_slot.ready = dma("sync", s_msk, msk_sb[:], masks[j, :, :])
            r_mask_last = None
            npair = 2 * EXT[j]
            for hp in range(4):
                hls = (2 * hp, 2 * hp + 1)
                avs, scs, pbs = {}, {}, {}
                for hl in hls:
                    avs[hl] = av_ps.next()
                def emit_scs(pc):
                    kc0, kc1 = 2 * pc, 2 * pc + 1
                    out = {}
                    for hl in hls:
                        p, par = hl // 2, hl % 2
                        pr = slice(64 * par, 64 * par + 64)
                        sc = sc_ps.next()
                        out[hl] = sc
                        W("tensor", sc.free)
                        opq("tensor",
                            lambda e, sc=sc.res, p=p, pr=pr, kc0=kc0, j=j:
                            e.matmul(sc[:, 0:512], K_bf[p][pr, ts(kc0, 128)],
                                     Q_bf[p][pr, ts(j, 512)],
                                     start=True, stop=True))
                        sc.ready = op("tensor", sT,
                                      lambda e, sc=sc.res, p=p, pr=pr,
                                      kc1=kc1, j=j:
                                      e.matmul(sc[:, 512:1024],
                                               K_bf[p][pr, ts(kc1, 128)],
                                               Q_bf[p][pr, ts(j, 512)],
                                               start=True, stop=True))
                    return out

                scs_next = emit_scs(0)
                for pc in range(npair):
                    kc0, kc1 = 2 * pc, 2 * pc + 1
                    scs = scs_next
                    for hl in hls:
                        sc = scs[hl]
                        pb = P_p.next()
                        pbs[hl] = pb
                        W("scalar", sc.ready)
                        W("scalar", pb.free)
                        pb.ready = op("scalar", sS,
                                      lambda e, sc=sc.res, pb=pb.res:
                                      e.activation(pb[:], sc[:], ACT.Exp,
                                                   bias=neg20[:, 0:1]))
                        sc.free = pb.ready
                    scs_next = emit_scs(pc + 1) if pc + 1 < npair else None
                    masked = pc >= npair - 4
                    if masked:
                        mc = pc - (npair - 4)
                        for hl in hls:
                            pb = pbs[hl]
                            W("vector", pb.ready)
                            if hp == 0 and hl == hls[0] and mc == 0:
                                W("vector", msk_slot.ready)
                            pb.ready = op("vector", sV,
                                          lambda e, pb=pb.res, mc=mc:
                                          e.tensor_mul(pb[:], pb[:],
                                                       msk_sb[:, ts(mc, 1024)]))
                            r_mask_last = pb.ready
                    for hl in hls:
                        av, pb = avs[hl], pbs[hl]
                        if pc == 0:
                            W("tensor", av.free)
                        W("tensor", pb.ready)
                        opq("tensor",
                            lambda e, av=av.res, hl=hl, kc0=kc0, pb=pb.res,
                            st_=(pc == 0):
                            e.matmul(av[:], bass.AP(V_bf, 520 * kc0 + 65 * hl,
                                                    [[32 * 520, 128], [1, 65]]),
                                     pb[:, 0:512], start=st_, stop=False))
                        pb.free = op("tensor", sT,
                                     lambda e, av=av.res, hl=hl, kc1=kc1,
                                     pb=pb.res, sp=(pc == npair - 1):
                                     e.matmul(av[:],
                                              bass.AP(V_bf, 520 * kc1 + 65 * hl,
                                                      [[32 * 520, 128], [1, 65]]),
                                              pb[:, 512:1024],
                                              start=False, stop=sp))
                        av.ready = pb.free
                avb = [avb_st[0], avb_st[1]]
                for q, hl in enumerate(hls):
                    av = avs[hl]
                    W("vector", av.ready)
                    op("vector", sV, lambda e, av=av.res, b=avb[q]:
                       e.tensor_copy(b[:], av[0:64, :]))
                    r_st = op("vector", sV, lambda e, av=av.res: _recl(
                        e, rec1[0:1, :], av[64:65, :]))
                    av.free = r_st
                    W("gpsimd", r_st)
                    d_bc = dma("gpsimd", s_bc, bc2[q][:],
                               bass.AP(rec1, 0, [[1, 1], [0, 64], [1, 512]]))
                    W("vector", d_bc)
                    hd = (hg * 8 + hl) // 2
                    dst = slice(64 * ((hg * 8 + hl) % 2),
                                64 * ((hg * 8 + hl) % 2) + 64)
                    op("vector", sV,
                       lambda e, b=avb[q], hd=hd, dst=dst, j=j, q=q:
                       e.tensor_mul(attnout[hd][dst, ts(j, 512)],
                                    b[:], bc2[q][:]))
            msk_slot.free = r_mask_last
        ctx.flush()
        pa2.close()

    pb5 = ExitStack()
    tr5 = Pool([pb5.enter_context(nc.psum_tensor(f"t5{i}", [128, 128], F32))
                for i in range(2)])
    pr5 = Pool([pb5.enter_context(nc.psum_tensor(f"p5{i}", [128, 512], F32))
                for i in range(2)])
    xhl5_p = Pool([msk_sb[:, 0:2 * D], msk_sb[:, 2 * D:4 * D]])
    wout_bf = [wsl[d][:, 0:1024] for d in range(8)]
    barrier()
    d_wo = None
    for d in range(8):
        d_wo = dma("gpsimd", s_wsl, wout_bf[d], wout_t[ts(d, 128), :])

    for jo in range(NSLOT):
        r_xo = None
        for i in range(4):
            it = jo * 4 + i
            xs = xa_p.next()
            W("sync", xs.free)
            xs.ready = dma("sync", xs.sem, xs.res[:], x_own[ts(it, 128), :])
            W("vector", xs.ready)
            hb = xhl5_p.next()
            W("vector", hb.free)
            op("vector", sV, lambda e, x=xs.res, h=hb.res:
               e.tensor_copy(h[:, 0:D], x[:]))
            r_hl = op("vector", sV, lambda e, x=xs.res, h=hb.res:
                      e.tensor_sub(h[:, D:2 * D], x[:], h[:, 0:D]))
            xs.free = r_hl
            W("tensor", r_hl)
            if it == 0:
                W("tensor", d_wo)
            for d in range(8):
                tp = tr5.next()
                W("tensor", tp.free)
                opq("tensor", lambda e, tp=tp.res, d=d, h=hb.res: e.matmul(
                    tp[:], h[:, ts(d, 128)], ident[:], start=True, stop=False))
                tp.ready = op("tensor", sT, lambda e, tp=tp.res, d=d, h=hb.res:
                              e.matmul(tp[:], h[:, D + d * 128:D + d * 128 + 128],
                                       ident[:], start=False, stop=True))
                hb.free = tp.ready
                W("scalar", tp.ready)
                tp.free = op("scalar", sS, lambda e, tp=tp.res, d=d, i=i: e.copy(
                    xofm[d // 2][:, (d % 2) * 512 + i * 128:
                                 (d % 2) * 512 + i * 128 + 128], tp[:]))
                r_xo = tp.free
        W("tensor", r_xo)
        for c in range(8):
            pp = pr5.next()
            W("tensor", pp.free)
            for d in range(8):
                pp.ready = op("tensor", sT, lambda e, pp=pp.res, c=c, d=d, jo=jo:
                              e.matmul(pp[:], wout_bf[d][:, ts(c, 128)],
                                       attnout[d][:, ts(jo, 512)],
                                       start=(d == 0), stop=(d == 7)))
            W("vector", pp.ready)
            pp.free = op("vector", sV, lambda e, pp=pp.res, c=c, jo=jo:
                         e.scalar_tensor_tensor(
                x_res[c][:, ts(jo, 512)], pp[:], ga_fm[:, c:c + 1],
                xofm[c // 2][:, (c % 2) * 512:(c % 2) * 512 + 512],
                ALU.mult, ALU.add))
    ctx.flush()
    pb5.close()
    sa_stk.close()

    b6 = ExitStack()
    b6sb = lambda n, s, d=F32: b6.enter_context(nc.sbuf_tensor(n, s, d))
    x2tm = [b6sb(f"x2_{i}", [128, D]) for i in range(2)]
    h2tm = [b6sb(f"h2_{i}", [128, D], BF16) for i in range(2)]
    hi6 = [b6sb(f"hi6_{p}", [128, 128], BF16) for p in range(2)]
    lo6 = [b6sb(f"lo6_{p}", [128, 128], BF16) for p in range(2)]
    st6 = b6sb("st6", [128, 16])
    rw_bf = b6sb("rw_bf", [128, 64], BF16)
    sm8 = b6sb("sm8", [128, 192])
    tr6 = Pool([b6.enter_context(nc.psum_tensor(f"t6{i}", [128, 128], F32))
                for i in range(2)])
    lg6 = Pool([b6.enter_context(nc.psum_tensor(f"l6{i}", [128, 8], F32))
                for i in range(4)])

    barrier()
    d_rw = dma("gpsimd", s_gp0, rw_bf[:, 0:64], rw[:, :])

    x2_p = Pool(x2tm)
    h2_p = Pool(h2tm)
    r_h2last = None
    for it in range(16):
        jo, i = divmod(it, 4)
        x2 = x2_p.next()
        h2s = h2_p.next()
        r_tx2 = [None, None]
        for dd in (0, 2, 4, 6):
            r_hl = [None, None]
            for p in (0, 1):
                d = dd + p
                W("vector", r_tx2[p])
                op("vector", sV, lambda e, d=d, jo=jo, i=i, p=p: e.tensor_copy(
                    hi6[p][:],
                    x_res[d][:, jo * 512 + i * 128:jo * 512 + i * 128 + 128]))
            for p in (0, 1):
                d = dd + p
                r_hl[p] = op("vector", sV, lambda e, d=d, jo=jo, i=i, p=p:
                             e.tensor_sub(
                    lo6[p][:],
                    x_res[d][:, jo * 512 + i * 128:jo * 512 + i * 128 + 128],
                    hi6[p][:]))
            for p in (0, 1):
                d = dd + p
                tp = tr6.next()
                W("tensor", r_hl[p])
                W("tensor", tp.free)
                W("tensor", x2.free)
                opq("tensor", lambda e, tp=tp.res, p=p: e.matmul(
                    tp[:], hi6[p][:], ident[:], start=True, stop=False))
                tp.ready = op("tensor", sT, lambda e, tp=tp.res, p=p: e.matmul(
                    tp[:], lo6[p][:], ident[:], start=False, stop=True))
                r_tx2[p] = tp.ready
                W("scalar", tp.ready)
                tp.free = op("scalar", sS, lambda e, tp=tp.res, d=d, x2=x2.res:
                             e.copy(x2[:, ts(d, 128)], tp[:]))
                r_x2 = tp.free
        x2.ready = r_x2
        W("vector", h2s.free)
        r_ln = ln_tile(x2.res, h2s.res, st6, x2.ready)
        op("vector", sV, lambda e, h=h2s.res: e.tensor_mul(h[:], h[:], cbp1_tm[:]))
        r_h2 = op("vector", sV, lambda e, h=h2s.res:
                  e.tensor_add(h[:], h[:], sbm_tm[:]))
        x2.free = r_h2
        W("tensor", r_h2)
        for d in range(8):
            tp = tr6.next()
            W("tensor", tp.free)
            tp.ready = op("tensor", sT, lambda e, tp=tp.res, d=d, h=h2s.res:
                          e.matmul(tp[:], h[:, ts(d, 128)], ident[:],
                                   start=True, stop=True))
            W("scalar", tp.ready)
            tp.free = op("scalar", sS, lambda e, tp=tp.res, d=d, it=it:
                         e.copy(h2_fm[d][:, ts(it, 128)], tp[:]))
            r_h2last = tp.free
        h2s.free = r_h2last

    W("tensor", r_h2last)
    W("tensor", d_rw)
    for tkp in range(4):
        tks = tuple(4 * tkp + q for q in range(4))
        sm = [sm8[:, ts(q, 48)] for q in range(4)]
        lgs = []
        for tk in tks:
            lg = lg6.next()
            W("tensor", lg.free)
            for d in range(8):
                lg.ready = op("tensor", sT, lambda e, lg=lg.res, d=d, tk=tk:
                              e.matmul(lg[:], h2_fm[d][:, ts(tk, 128)],
                                       rw_bf[:, ts(d, 8)],
                                       start=(d == 0), stop=(d == 7)))
            lgs.append(lg)
        for p, lg in enumerate(lgs):
            W("vector", lg.ready)
            a = op("vector", sV, lambda e, lg=lg.res, s=sm[p]:
                   e.reduce_max(s[:, 0:1], lg[:], AX, negate=True))
            W("scalar", a)
            a = op("scalar", sS, lambda e, lg=lg.res, s=sm[p]: e.activation(
                s[:, 8:16], lg[:], ACT.Exp, bias=s[:, 0:1]))
            lg.free = a
            W("vector", a)

        def pair(fn):
            r = None
            for p in range(4):
                r = op("vector", sV, lambda e, s=sm[p], p=p: fn(e, s, p))
            W("vector", r)
            return r
        pair(lambda e, s, p: e.reduce_sum(s[:, 1:2], s[:, 8:16], AX))
        pair(lambda e, s, p: e.reciprocal(s[:, 2:3], s[:, 1:2]))
        pair(lambda e, s, p: e.tensor_mul(
            s[:, 8:16], s[:, 8:16], s[:, 2:3].to_broadcast([128, 8])))
        pair(lambda e, s, p: e.tensor_add(
            s[:, 16:24], s[:, 8:16], rbcol[:, 0:1].to_broadcast([128, 8])))
        pair(lambda e, s, p: e.max(out=s[:, 24:32], in_=s[:, 16:24]))
        pair(lambda e, s, p: e.tensor_tensor(
            s[:, 32:40], s[:, 16:24],
            s[:, 27:28].to_broadcast([128, 8]), ALU.is_ge))
        pair(lambda e, s, p: e.tensor_mul(
            s[:, 32:40], s[:, 32:40], s[:, 8:16]))
        pair(lambda e, s, p: e.reduce_sum(s[:, 3:4], s[:, 32:40], AX))
        pair(lambda e, s, p: e.tensor_scalar(
            s[:, 3:4], s[:, 3:4], 1e-20, None, ALU.add))
        pair(lambda e, s, p: e.reciprocal(s[:, 4:5], s[:, 3:4]))
        pair(lambda e, s, p, _tks=tks: e.tensor_mul(
            wsel_bf[:, ts(_tks[p], 8)], s[:, 32:40],
            s[:, 4:5].to_broadcast([128, 8])))
    ctx.flush()
    b6.close()

    b7 = ExitStack()
    b7sb = lambda n, s, d=BF16: b7.enter_context(nc.sbuf_tensor(n, s, d))
    gu_bf = [b7sb(f"gu{i}", [128, 8192]) for i in range(2)]
    dw_bf = [b7sb(f"dwb{i}", [128, 4096]) for i in range(2)]
    act_e = [b7sb(f"acte{s}", [128, NTOK]) for s in range(4)]
    rep7 = b7sb("rep7", [128, 128])
    wsb7 = [b7sb(f"wsb{i}", [128, 512]) for i in range(2)]
    sg7 = [b7sb(f"sg7{i}", [128, 512]) for i in range(2)]
    gu_ps = Pool([b7.enter_context(nc.psum_tensor(f"g7{i}", [128, 512], F32))
                  for i in range(4)])
    wb_ps = Pool([b7.enter_context(nc.psum_tensor(f"w7{i}", [128, 512], F32))
                  for i in range(2)])
    d8_ps = Pool([b7.enter_context(nc.psum_tensor(f"d8{i}", [128, 512], F32))
                  for i in range(2)])
    gu_p = Pool(gu_bf)
    for i, sl in enumerate(gu_p.slots):
        sl.sem = gu_sems[i]
    dw_p = Pool(dw_bf)
    for i, sl in enumerate(dw_p.slots):
        sl.sem = dw_sems[i]
    wsb_p = Pool(wsb7)
    sg_p = Pool(sg7)
    act_slots = [Slot(a) for a in act_e]

    barrier()
    for e_i in range(NEXP):
        gu = gu_p.next()
        W("gpsimd", gu.free)
        for d in range(8):
            dma("gpsimd", gu.sem, gu.res[:, ts(d, 512)], gwt[e_i, ts(d, 128), :])
        for d in range(8):
            gu.ready = dma("gpsimd", gu.sem,
                           gu.res[:, 4096 + d * 512:4096 + d * 512 + 512],
                           uwt[e_i, ts(d, 128), :])
        dwe = dw_p.next()
        W("gpsimd", dwe.free)
        for s in range(4):
            dwe.ready = dma("gpsimd", dwe.sem, dwe.res[:, ts(s, 1024)],
                            dwt[e_i, ts(s, 128), :])
        r_gu_last = None
        for j in range(NSLOT):
            wj = wsb_p.next()
            wp = wb_ps.next()
            W("tensor", wp.free)
            r_rep = None
            for q4 in range(4):
                tk = j * 4 + q4
                W("vector", r_rep)
                r_rep0 = op("vector", sV, lambda e, tk=tk, e_i=e_i: e.tensor_copy(
                    rep7[:], wsel_bf[:, tk * 8 + e_i:tk * 8 + e_i + 1]
                    .to_broadcast([128, 128])))
                W("tensor", r_rep0)
                r_rep = op("tensor", sT, lambda e, wp=wp.res, q4=q4: e.matmul(
                    wp[:, ts(q4, 128)], rep7[:], ident[:], start=True, stop=True))
            wp.ready = r_rep
            W("vector", wp.ready)
            W("vector", wj.free)
            wj.ready = op("vector", sV, lambda e, wj=wj.res, wp=wp.res:
                          e.tensor_copy(wj[:], wp[:]))
            wp.free = wj.ready
            if j == 0:
                W("tensor", gu.ready)
            for s in range(4):
                pg = gu_ps.next()
                pu = gu_ps.next()
                ac = act_slots[s]
                W("tensor", pg.free)
                W("tensor", pu.free)
                for d in range(8):
                    opq("tensor", lambda e, pg=pg.res, gu=gu.res, d=d, s=s, j=j:
                        e.matmul(pg[:], gu[:, d * 512 + s * 128:d * 512 + s * 128 + 128],
                                 h2_fm[d][:, ts(j, 512)], start=(d == 0), stop=(d == 7)))
                for d in range(8):
                    r_gu = op("tensor", sT,
                              lambda e, pu=pu.res, gu=gu.res, d=d, s=s, j=j:
                              e.matmul(pu[:],
                                       gu[:, 4096 + d * 512 + s * 128:
                                          4096 + d * 512 + s * 128 + 128],
                                       h2_fm[d][:, ts(j, 512)],
                                       start=(d == 0), stop=(d == 7)))
                r_gu_last = r_gu
                sg = sg_p.next()
                W("scalar", r_gu)
                W("scalar", sg.free)
                sg.ready = op("scalar", sS, lambda e, pg=pg.res, sg=sg.res:
                              e.activation(sg[:], pg[:], ACT.Silu))
                pg.free = sg.ready
                W("vector", sg.ready)
                W("vector", wj.ready)
                W("vector", ac.free)
                op("vector", sV, lambda e, pu=pu.res, sg=sg.res, ac=ac.res, s=s, j=j:
                   e.tensor_mul(ac[:, ts(j, 512)], sg[:], pu[:]))
                r_act = op("vector", sV, lambda e, ac=ac.res, wj=wj.res, s=s, j=j:
                           e.tensor_mul(ac[:, ts(j, 512)], ac[:, ts(j, 512)], wj[:]))
                pu.free = r_act
                sg.free = r_act
                ac.ready = r_act
            wj.free = r_act
        gu.free = r_gu_last
        r_dn = None
        for j in range(NSLOT):
            pd = d8_ps.next()
            W("tensor", pd.free)
            if j == 0:
                W("tensor", dwe.ready)
            for s in range(4):
                W("tensor", act_slots[s].ready)
            for c in range(8):
                if c > 0:
                    pd = d8_ps.next()
                    W("tensor", pd.free)
                for s in range(4):
                    pd.ready = op("tensor", sT,
                                  lambda e, pd=pd.res, dw=dwe.res, s=s, c=c, j=j,
                                  st_=(s == 0), sp=(s == 3):
                                  e.matmul(pd[:], dw[:, s * 1024 + c * 128:
                                                    s * 1024 + c * 128 + 128],
                                           act_e[s][:, ts(j, 512)],
                                           start=st_, stop=sp))
                W("vector", pd.ready)
                pd.free = op("vector", sV, lambda e, pd=pd.res, c=c, j=j:
                             e.scalar_tensor_tensor(
                    x_res[c][:, ts(j, 512)], pd[:], gb_fm[:, c:c + 1],
                    x_res[c][:, ts(j, 512)], ALU.mult, ALU.add))
                r_dn = pd.free
        dwe.free = r_dn
        for s in range(4):
            act_slots[s].free = r_dn
    ctx.flush()
    b7.close()

    b9 = ExitStack()
    b9sb = lambda n, s, d=BF16: b9.enter_context(nc.sbuf_tensor(n, s, d))
    sgu_bf = [b9sb(f"sgu{d}", [128, 2048]) for d in range(8)]
    s_act = [b9sb(f"sac{i}", [128, NTOK]) for i in range(8)]
    sdw_bf = [b9sb(f"sdwb{i}", [128, 1024]) for i in range(2)]
    sg9 = [b9sb(f"sg9{i}", [128, 512]) for i in range(2)]
    gu9 = Pool([b9.enter_context(nc.psum_tensor(f"g9{i}", [128, 512], F32))
                for i in range(4)])
    d9_ps = Pool([b9.enter_context(nc.psum_tensor(f"d9{i}", [128, 512], F32))
                  for i in range(2)])
    sdw_p = Pool(sdw_bf)
    for i, sl in enumerate(sdw_p.slots):
        sl.sem = sd_sems[i]
    sg9_p = Pool(sg9)

    prev_g9 = None
    prev_dn9 = None
    barrier()
    for ih in range(2):
        d_sg = None
        if ih == 1:
            W("gpsimd", prev_g9)
        for d in range(8):
            dma("gpsimd", s_sgu, sgu_bf[d][:, 0:1024],
                sgw[ts(d, 128), ih * 1024:ih * 1024 + 1024])
            d_sg = dma("gpsimd", s_sgu, sgu_bf[d][:, 1024:2048],
                       suw[ts(d, 128), ih * 1024:ih * 1024 + 1024])
        if ih == 1:
            W("vector", prev_dn9)
        s_act_ready = [None] * 8
        r_act9 = None
        for ic in range(8):
            for j in range(NSLOT):
                pg = gu9.next()
                pu = gu9.next()
                W("tensor", pg.free)
                W("tensor", pu.free)
                if ic == 0 and j == 0:
                    W("tensor", d_sg)
                for d in range(8):
                    opq("tensor", lambda e, pg=pg.res, d=d, ic=ic, j=j:
                        e.matmul(pg[:], sgu_bf[d][:, ts(ic, 128)],
                                 h2_fm[d][:, ts(j, 512)], start=(d == 0), stop=(d == 7)))
                for d in range(8):
                    r_g9 = op("tensor", sT,
                              lambda e, pu=pu.res, d=d, ic=ic, j=j:
                              e.matmul(pu[:], sgu_bf[d][:, 1024 + ic * 128:
                                                        1024 + ic * 128 + 128],
                                       h2_fm[d][:, ts(j, 512)],
                                       start=(d == 0), stop=(d == 7)))
                sg = sg9_p.next()
                W("scalar", r_g9)
                W("scalar", sg.free)
                sg.ready = op("scalar", sS, lambda e, pg=pg.res, sg=sg.res:
                              e.activation(sg[:], pg[:], ACT.Silu))
                pg.free = sg.ready
                W("vector", sg.ready)
                r_a = op("vector", sV, lambda e, pu=pu.res, sg=sg.res, ic=ic, j=j:
                         e.tensor_mul(s_act[ic][:, ts(j, 512)], sg[:], pu[:]))
                pu.free = r_a
                sg.free = r_a
                r_act9 = r_a
            s_act_ready[ic] = r_act9
        r_dn9 = None
        for c in range(8):
            sd = sdw_p.next()
            W("gpsimd", sd.free)
            sd.ready = dma("gpsimd", sd.sem, bass.AP(sd.res, 0,
                                                 [[1024, 128], [128, 8], [1, 128]]),
                           bass.AP(sdw.tensor, ih * 1024 * 1024 + c * 128,
                                   [[1024, 128], [128 * 1024, 8], [1, 128]]))
            for j in range(NSLOT):
                pd = d9_ps.next()
                W("tensor", pd.free)
                if j == 0:
                    W("tensor", sd.ready)
                    for ic in range(8):
                        W("tensor", s_act_ready[ic])
                for ic in range(8):
                    pd.ready = op("tensor", sT,
                                  lambda e, pd=pd.res, sd=sd.res, ic=ic, j=j,
                                  st_=(ic == 0), sp=(ic == 7):
                                  e.matmul(pd[:], sd[:, ts(ic, 128)],
                                           s_act[ic][:, ts(j, 512)],
                                           start=st_, stop=sp))
                W("vector", pd.ready)
                pd.free = op("vector", sV, lambda e, pd=pd.res, c=c, j=j:
                             e.scalar_tensor_tensor(
                    x_res[c][:, ts(j, 512)], pd[:], gb_fm[:, c:c + 1],
                    x_res[c][:, ts(j, 512)], ALU.mult, ALU.add))
                r_dn9 = pd.free
            sd.free = r_dn9
        prev_g9 = r_g9
        prev_dn9 = r_dn9
    ctx.flush()
    b9.close()

    b10 = ExitStack()
    yhl = [b10.enter_context(nc.sbuf_tensor(f"yhl{i}", [128, 256], BF16))
           for i in range(2)]
    y_sb = [b10.enter_context(nc.sbuf_tensor(f"ysb{i}", [128, D], F32))
            for i in range(2)]
    trA = Pool([b10.enter_context(nc.psum_tensor(f"tA{i}", [128, 128], F32))
                for i in range(2)])
    yhl_p = Pool(yhl)
    ysb_slots = [Slot(t) for t in y_sb]
    for i, sl in enumerate(ysb_slots):
        sl.sem = ys_sems[i]
    barrier()
    for jo in range(NSLOT):
        for i in range(4):
            ysl = ysb_slots[i % 2]
            r_tile = None
            for dd in (0, 2, 4, 6):
                hbs = [yhl_p.next(), yhl_p.next()]
                for p in (0, 1):
                    d = dd + p
                    W("vector", hbs[p].free)
                    op("vector", sV, lambda e, hb=hbs[p].res, d=d, jo=jo, i=i:
                       e.tensor_copy(hb[:, 0:128],
                                     x_res[d][:, jo * 512 + i * 128:
                                              jo * 512 + i * 128 + 128]))
                for p in (0, 1):
                    d = dd + p
                    hbs[p].ready = op("vector", sV,
                                      lambda e, hb=hbs[p].res, d=d, jo=jo, i=i:
                                      e.tensor_sub(
                        hb[:, 128:256],
                        x_res[d][:, jo * 512 + i * 128:jo * 512 + i * 128 + 128],
                        hb[:, 0:128]))
                for p in (0, 1):
                    d = dd + p
                    hb = hbs[p]
                    tp = trA.next()
                    W("tensor", hb.ready)
                    W("tensor", tp.free)
                    if d == 0:
                        W("tensor", ysl.free)
                    opq("tensor", lambda e, tp=tp.res, hb=hb.res: e.matmul(
                        tp[:], hb[:, 0:128], ident[:], start=True, stop=False))
                    tp.ready = op("tensor", sT, lambda e, tp=tp.res, hb=hb.res:
                                  e.matmul(tp[:], hb[:, 128:256], ident[:],
                                           start=False, stop=True))
                    hb.free = tp.ready
                    W("scalar", tp.ready)
                    tp.free = op("scalar", sS,
                                 lambda e, tp=tp.res, d=d, ysl=ysl.res:
                                 e.copy(ysl[:, ts(d, 128)], tp[:]))
                    r_tile = tp.free
            ysl.ready = r_tile
            W("sync", ysl.ready)
            ysl.free = dma("sync", ysl.sem,
                           y[jo * 512 + i * 128: jo * 512 + i * 128 + 128, :],
                           ysl.res[:])
    opq("sync", lambda e, v0=ys_sems[0].n, v1=ys_sems[1].n:
        (e.wait_ge(ys_sems[0].h, v0), e.wait_ge(ys_sems[1].h, v1)))
    ctx.flush()
    b10.close()
    root.close()
    return nc


_PROG = None


def _get_prog():
    global _PROG
    if _PROG is None:
        _PROG = build_program()
    return _PROG


def kernel(L0_attn_x, L0_attn_w_qkv, L0_attn_w_out, L0_ffn_router_w,
           L0_ffn_router_bias, L0_ffn_gate_w, L0_ffn_up_w, L0_ffn_down_w,
           L0_ffn_s_gate_w, L0_ffn_s_up_w, L0_ffn_s_down_w, L0_c, L0_wmod):
    bf = ml_dtypes.bfloat16
    x = np.ascontiguousarray(np.asarray(L0_attn_x, dtype=np.float32))
    wqkv_t = np.ascontiguousarray(np.asarray(L0_attn_w_qkv, np.float32).T)
    wout_t = np.ascontiguousarray(np.asarray(L0_attn_w_out, np.float32).T)
    wmod = np.ascontiguousarray(np.asarray(L0_wmod, np.float32))
    cmat = np.ascontiguousarray(np.asarray(L0_c, np.float32))
    rbv = np.asarray(L0_ffn_router_bias, np.float32).reshape(1, 1)
    rwm = np.asarray(L0_ffn_router_w, np.float32)
    rw_fm = np.ascontiguousarray(
        rwm.reshape(8, 128, 8).transpose(1, 0, 2).reshape(128, 64))
    gwm = np.ascontiguousarray(np.asarray(L0_ffn_gate_w, np.float32))
    uwm = np.ascontiguousarray(np.asarray(L0_ffn_up_w, np.float32))
    dwm = np.ascontiguousarray(np.asarray(L0_ffn_down_w, np.float32))
    sgwm = np.ascontiguousarray(np.asarray(L0_ffn_s_gate_w, np.float32))
    suwm = np.ascontiguousarray(np.asarray(L0_ffn_s_up_w, np.float32))
    sdwm = np.ascontiguousarray(np.asarray(L0_ffn_s_down_w, np.float32))

    mask_h = {}
    for half in (0, 1):
        m = np.zeros((NSLOT, 128, 4096), dtype=bf)
        for sl in range(NSLOT):
            gq = OWN_TGS[half][sl]
            for mc in range(8):
                kc = 4 * (EXT[sl] - 2) + mc
                keys = kc * 128 + np.arange(128)[:, None]
                qs = gq * 512 + np.arange(512)[None, :]
                m[sl, :, mc * 512:(mc + 1) * 512] = (keys <= qs).astype(bf)
        mask_h[half] = m

    nc = _get_prog()
    in_maps = []
    for core in range(8):
        b, half = core // 2, core % 2
        tgs = OWN_TGS[half]
        x_b = np.ascontiguousarray(x[b])
        x_own = np.ascontiguousarray(
            np.concatenate([x[b, ts(t, 512), :] for t in tgs], axis=0))
        in_maps.append({
            "x_full": x_b, "x_own": x_own, "wqkv_t": wqkv_t, "wout_t": wout_t,
            "wmod": wmod,
            "c_fm_in": np.ascontiguousarray(cmat[b].reshape(8, 128).T),
            "rb": rbv, "rw": rw_fm, "gwt": gwm, "uwt": uwm, "dwt": dwm,
            "sgw": sgwm, "suw": suwm, "sdw": sdwm, "masks": mask_h[half],
        })
    res = run_bass_kernel_spmd(nc, in_maps, core_ids=list(range(8)))
    out = np.empty((B, T, D), dtype=np.float32)
    for core in range(8):
        b, half = core // 2, core % 2
        yv = res.results[core]["y"]
        for sl, tg in enumerate(OWN_TGS[half]):
            out[b, ts(tg, 512), :] = yv[ts(sl, 512), :]
    return out


# version 26
# speedup vs baseline: 1.1869x; 1.0001x over previous
import numpy as np
import ml_dtypes
from contextlib import ExitStack

import concourse.bass as bass
import concourse.mybir as mybir
from concourse.bass_utils import run_bass_kernel_spmd

F32 = mybir.dt.float32
BF16 = mybir.dt.bfloat16
AX = mybir.AxisListType.X
ALU = mybir.AluOpType
ACT = mybir.ActivationFunctionType

B, T, D = 4, 4096, 1024
NTOK = 2048
NSLOT = 4
EXT = [2, 4, 6, 8]
OWN_TGS = {0: [0, 3, 4, 7], 1: [1, 2, 5, 6]}
SHIFT = 20.0
NEXP, IEXP, ISH = 8, 512, 2048
EPS = 1e-5


def ts(i, n):
    return slice(i * n, (i + 1) * n)


class Sem:
    def __init__(self, stk, nc, name):
        self.h = stk.enter_context(nc.semaphore(name))
        self.n = 0


class Ctx:
    ENGINES = ("sync", "gpsimd", "tensor", "vector", "scalar")

    def __init__(self, nc):
        self.nc = nc
        self.ops = {e: [] for e in self.ENGINES}

    def emit(self, eng, fn):
        self.ops[eng].append(fn)

    def wait(self, eng, dep):
        if dep is None:
            return
        sem, val = dep
        if val <= 0:
            return
        assert val <= sem.n, f"wait {val} > emitted {sem.n} on {eng}"
        self.ops[eng].append(lambda e, s=sem.h, v=val: e.wait_ge(s, v))

    def flush(self):
        with self.nc.Block() as blk:
            for name in self.ENGINES:
                lst = self.ops[name]
                if lst:
                    getattr(blk, name)(
                        lambda e, _l=tuple(lst): [f(e) for f in _l])
        self.ops = {e: [] for e in self.ENGINES}


class Slot:
    def __init__(self, res, sem=None):
        self.res = res
        self.sem = sem
        self.free = None
        self.ready = None


class Pool:
    def __init__(self, resources):
        self.slots = [Slot(r) for r in resources]
        self.i = 0

    def next(self):
        s = self.slots[self.i % len(self.slots)]
        self.i += 1
        return s


def build_program():
    nc = bass.Bass("TRN2", target_bir_lowering=False, debug=False,
                   detect_race_conditions=False)
    dri = lambda name, shape, dt=F32: nc.dram_tensor(
        name, shape, dt, kind="ExternalInput").ap()
    x_full = dri("x_full", [T, D])
    x_own = dri("x_own", [NTOK, D])
    wqkv_t = dri("wqkv_t", [D, 3 * D])
    wout_t = dri("wout_t", [D, D])
    wmod = dri("wmod", [D, 6 * D])
    c_fm_in = dri("c_fm_in", [128, 8])
    rb = dri("rb", [1, 1])
    rw = dri("rw", [128, 64])
    gwt = dri("gwt", [NEXP, D, IEXP])
    uwt = dri("uwt", [NEXP, D, IEXP])
    dwt = dri("dwt", [NEXP, IEXP, D])
    sgw = dri("sgw", [D, ISH])
    suw = dri("suw", [D, ISH])
    sdw = dri("sdw", [ISH, D])
    masks = dri("masks", [NSLOT, 128, 4096], BF16)
    y = nc.dram_tensor("y", [NTOK, D], F32, kind="ExternalOutput").ap()
    mod_dram = nc.dram_tensor("mod_scratch", [1, 4 * D], F32, kind="Internal").ap()

    root = ExitStack()
    rsb = lambda n, s, d=F32: root.enter_context(nc.sbuf_tensor(n, s, d))

    K_bf = [rsb(f"K{p}", [128, T], BF16) for p in range(4)]
    V_bf = rsb("Vb", [128, 32 * 520], BF16)
    attnout = [rsb(f"ao{d}", [128, NTOK], BF16) for d in range(8)]
    ident = rsb("ident", [128, 128], BF16)
    ones_bf = rsb("ones", [128, 512], BF16)
    brow = rsb("brow", [1, 3 * D], BF16)
    cbp1_tm = rsb("cbp1", [128, D])
    sbm_tm = rsb("sbm", [128, D], BF16)
    cap1_fm = rsb("cap1", [128, 8])
    ga_fm = rsb("ga", [128, 8])
    gb_fm = rsb("gb", [128, 8])
    neg20 = rsb("neg20", [128, 1])
    epsc = rsb("epsc", [128, 1])
    rbcol = rsb("rbcol", [128, 1])
    wsel_bf = rsb("wsel", [128, 128], BF16)
    sa_bf = rsb("sa_bf", [128, 8], BF16)
    x_res = [K_bf[d][:].bitcast(F32) for d in range(4)] + \
            [V_bf[:].bitcast(F32)[:, ts(i, NTOK)] for i in range(4)]
    h2_fm = attnout

    ctx = Ctx(nc)
    sG = Sem(root, nc, "sG")
    sT = Sem(root, nc, "sT")
    sV = Sem(root, nc, "sV")
    sS = Sem(root, nc, "sS")
    s_gp0 = Sem(root, nc, "s_gp0")
    s_wqkv = Sem(root, nc, "s_wqkv")
    s_wsl = Sem(root, nc, "s_wsl")
    s_sgu = Sem(root, nc, "s_sgu")
    wm_sems = [Sem(root, nc, f"s_wm{i}") for i in range(4)]
    xa_sems = [Sem(root, nc, f"s_xa{i}") for i in range(4)]
    s_msk = Sem(root, nc, "s_msk")
    gu_sems = [Sem(root, nc, f"s_gu{i}") for i in range(2)]
    dw_sems = [Sem(root, nc, f"s_dwe{i}") for i in range(2)]
    dc_sems = [Sem(root, nc, f"s_dc{i}") for i in range(2)]
    sd_sems = [Sem(root, nc, f"s_sd{i}") for i in range(2)]
    ys_sems = [Sem(root, nc, f"s_ys{i}") for i in range(2)]
    s_bc = Sem(root, nc, "s_bc")

    def dma(eng, sem, out, in_, slow=False):
        def f(e, o=out, i=in_, s=sem.h, sl=slow):
            kw = {"allow_slow_non_contiguous": True} if sl else {}
            e.dma_start(o, i, **kw).then_inc(s, 16)
        ctx.emit(eng, f)
        sem.n += 16
        return (sem, sem.n)

    def op(eng, sem, fn):
        def f(e, fn=fn, s=sem.h):
            fn(e).then_inc(s, 1)
        ctx.emit(eng, f)
        sem.n += 1
        return (sem, sem.n)

    def opq(eng, fn):
        ctx.emit(eng, fn)

    W = ctx.wait

    def _recl(e, out_ap, in_ap):
        with nc.allow_low_precision(reason="softmax denom recip to bf16"):
            return e.reciprocal(out_ap, in_ap)

    def barrier():
        snap = [(sT, sT.n), (sV, sV.n), (sS, sS.n)]
        for eng in Ctx.ENGINES:
            for dep in snap:
                W(eng, dep)

    b0 = ExitStack()
    bsb = lambda n, s, d=F32: b0.enter_context(nc.sbuf_tensor(n, s, d))
    bps = lambda n, s: b0.enter_context(nc.psum_tensor(n, s, F32))
    io_col = bsb("io_col", [128, 128])
    io_par = bsb("io_par", [128, 1])
    c_fm = bsb("c_fm", [128, 8])
    sil = bsb("sil", [128, 8])
    sil_hi = bsb("sil_hi", [128, 8], BF16)
    sil_lo = bsb("sil_lo", [128, 8], BF16)
    rep_hi = [bsb(f"rph{d}", [128, 128], BF16) for d in range(8)]
    rep_lo = [bsb(f"rpl{d}", [128, 128], BF16) for d in range(8)]
    wm_st = [bsb(f"wm{i}", [128, 512]) for i in range(4)]
    wm_hl = [bsb(f"wmh{i}", [128, 1024], BF16) for i in range(2)]
    modrow = bsb("modrow", [1, 4 * D])
    sga = bsb("sga", [128, 8])
    sgb = bsb("sgb", [128, 8])
    aba = bsb("aba", [128, 8])
    abb = bsb("abb", [128, 8])
    rb_sb = bsb("rb_sb", [1, 1], BF16)
    m32 = bsb("m32", [32, 128])
    m32h = bsb("m32h", [32, 128], BF16)
    m32l = bsb("m32l", [32, 128], BF16)
    mfm = bsb("mfm", [128, 32])
    mfm_ps = b0.enter_context(nc.psum_tensor("pmfm", [128, 32], F32))
    mod_ps = Pool([bps(f"p0_{i}", [128, 512]) for i in range(2)])
    rbc_ps = bps("prbc", [128, 512])

    d_iota = op("gpsimd", sG, lambda e: e.iota(
        io_col[:], [[1, 128]], channel_multiplier=0,
        allow_small_or_imprecise_dtypes=True))
    d_iota = op("gpsimd", sG, lambda e: e.iota(
        io_par[:], [[0, 1]], channel_multiplier=1,
        allow_small_or_imprecise_dtypes=True))
    W("vector", d_iota)
    r_ident = op("vector", sV, lambda e: e.tensor_scalar(
        ident[:], io_col[:], io_par[:, 0:1], None, ALU.is_equal))
    op("vector", sV, lambda e: e.memset(ones_bf[:], 1.0))
    op("vector", sV, lambda e: e.memset(epsc[:], EPS))
    r_ones = op("vector", sV, lambda e: e.memset(neg20[:], -SHIFT))

    dma("gpsimd", s_gp0, c_fm[:, 0:8], c_fm_in[:, :])
    d_crb = dma("gpsimd", s_gp0, rb_sb[0:1, 0:1], rb[0:1, 0:1])
    W("scalar", d_crb)
    r_sil = op("scalar", sS, lambda e: e.activation(sil[:], c_fm[:], ACT.Silu))
    W("vector", r_sil)
    a = op("vector", sV, lambda e: e.tensor_copy(sil_hi[:], sil[:]))
    W("vector", a)
    for d in range(8):
        r_rep = op("vector", sV, lambda e, d=d: e.tensor_copy(
            rep_hi[d][:], sil_hi[:, d:d + 1].to_broadcast([128, 128])))

    st_slots = Pool(wm_st)
    for i, sl in enumerate(st_slots.slots):
        sl.sem = wm_sems[i]
        sl.q = "sync" if i < 2 else "gpsimd"
    hl_slots = Pool(wm_hl)
    r_mod = {}
    for g in range(12):
        mp = mod_ps.next()
        W("tensor", mp.free)
        for d in range(8):
            k = g * 8 + d
            st = st_slots.next()
            W(st.q, st.free)
            st.ready = dma(st.q, st.sem, st.res[:], wmod[ts(d, 128), ts(g, 512)])
            hl = hl_slots.next()
            W("vector", st.ready)
            W("vector", hl.free)
            hl.ready = op("vector", sV, lambda e, a=hl.res, b=st.res:
                          e.tensor_copy(a[:, 0:512], b[:]))
            st.free = hl.ready
            W("tensor", hl.ready)
            if k == 0:
                W("tensor", r_rep)
            hl.free = op("tensor", sT, lambda e, d=d, a=hl.res, mp=mp.res,
                         st_=(d == 0), sp=(d == 7):
                         e.matmul(mp[:], rep_hi[d][:], a[:, 0:512],
                                  start=st_, stop=sp))
        mp.ready = hl.free
        m, hh = g // 2, g % 2
        W("vector", mp.ready)
        if m == 0:
            ev = op("vector", sV, lambda e, p=mp.res, hh=hh:
                    e.tensor_copy(modrow[0:1, ts(hh, 512)], p[0:1, :]))
        elif m == 1:
            ev = op("vector", sV, lambda e, p=mp.res, hh=hh: e.tensor_copy(
                modrow[0:1, 1024 + hh * 512:1024 + hh * 512 + 512], p[0:1, :]))
        elif m == 2:
            ev = op("vector", sV, lambda e, p=mp.res, hh=hh: e.tensor_copy(
                modrow[0:1, 2048 + hh * 512:2048 + hh * 512 + 512], p[0:1, :]))
        elif m == 3:
            ev = op("vector", sV, lambda e, p=mp.res, hh=hh:
                    e.tensor_copy(sbm_tm[:, ts(hh, 512)], p[:]))
        elif m == 4:
            ev = op("vector", sV, lambda e, p=mp.res, hh=hh: e.tensor_scalar(
                cbp1_tm[:, ts(hh, 512)], p[:], 1.0, None, ALU.add))
        else:
            ev = op("vector", sV, lambda e, p=mp.res, hh=hh: e.tensor_copy(
                modrow[0:1, 3072 + hh * 512:3072 + hh * 512 + 512], p[0:1, :]))
        mp.free = ev
        r_mod[m * 2 + hh] = ev

    W("gpsimd", r_mod[11])
    d_bounce = dma("gpsimd", s_gp0, mod_dram[0:1, :], modrow[0:1, :])
    W("gpsimd", d_bounce)
    d_m32 = dma("gpsimd", s_gp0, m32[:, :],
                bass.AP(mod_dram.tensor, 0, [[128, 32], [1, 128]]))
    W("vector", d_m32)
    r_m32 = op("vector", sV, lambda e: e.tensor_copy(m32h[:], m32[:]))
    W("tensor", r_m32)
    W("tensor", r_ident)
    t_mfm = op("tensor", sT, lambda e: e.matmul(mfm_ps[:], m32h[:],
                                                ident[0:32, 0:32],
                                                start=True, stop=True))
    W("vector", t_mfm)
    r_mfm = op("vector", sV, lambda e: e.tensor_copy(mfm[:], mfm_ps[0:128, 0:32]))
    W("vector", r_mfm)
    r_cap1 = op("vector", sV, lambda e: e.tensor_scalar(
        cap1_fm[:], mfm[:, 8:16], 1.0, None, ALU.add))
    r_sabf = op("vector", sV, lambda e: e.tensor_copy(sa_bf[:], mfm[:, 0:8]))
    op("vector", sV, lambda e: e.tensor_copy(ga_fm[:], mfm[:, 16:24]))
    r_gcp = op("vector", sV, lambda e: e.tensor_copy(gb_fm[:], mfm[:, 24:32]))
    W("scalar", r_gcp)
    op("scalar", sS, lambda e: e.activation(sga[:], ga_fm[:], ACT.Sign))
    op("scalar", sS, lambda e: e.activation(sgb[:], gb_fm[:], ACT.Sign))
    op("scalar", sS, lambda e: e.activation(aba[:], ga_fm[:], ACT.Abs))
    a_gab = op("scalar", sS, lambda e: e.activation(abb[:], gb_fm[:], ACT.Abs))
    W("vector", a_gab)
    op("vector", sV, lambda e: e.tensor_scalar(
        aba[:], aba[:], 0.25, None, ALU.max))
    a = op("vector", sV, lambda e: e.tensor_scalar(
        abb[:], abb[:], 0.25, None, ALU.max))
    W("vector", a)
    op("vector", sV, lambda e: e.tensor_mul(ga_fm[:], sga[:], aba[:]))
    r_gab = op("vector", sV, lambda e: e.tensor_mul(gb_fm[:], sgb[:], abb[:]))

    W("tensor", d_crb)
    W("tensor", r_ones)
    t_rbc = op("tensor", sT, lambda e: e.matmul(
        rbc_ps[0:128, 0:1], ones_bf[0:1, 0:128], rb_sb[0:1, 0:1],
        start=True, stop=True))
    W("vector", t_rbc)
    op("vector", sV, lambda e: e.tensor_copy(rbcol[:], rbc_ps[0:128, 0:1]))

    ctx.flush()
    b0.close()
    barrier()

    sa_stk = ExitStack()
    asb = lambda n, s, d=F32: sa_stk.enter_context(nc.sbuf_tensor(n, s, d))
    Q_bf = [asb(f"Q{p}", [128, NTOK], BF16) for p in range(4)]
    wsl = [asb(f"wsl{d}", [128, 1536], BF16) for d in range(8)]
    xa = [asb(f"xa{i}", [128, D]) for i in range(4)]
    xnb = [asb(f"xn{i}", [128, D], BF16) for i in range(2)]
    xfm = [[asb(f"xf{s}_{d}", [128, 512], BF16) for d in range(8)]
           for s in range(2)]
    st8 = asb("st8", [128, 32])
    P_sb = [asb(f"P{i}", [128, 1024], BF16) for i in range(4)]
    msk_sb = asb("msk", [128, 4096], BF16)
    avb_st = [asb(f"avst{i}", [64, 512], BF16) for i in range(2)]
    bc2 = [asb(f"bc2_{i}", [64, 512], BF16) for i in range(2)]
    rec1 = asb("rec1", [1, 512], BF16)

    xofm = [Q_bf[p][:].bitcast(F32) for p in range(4)]

    def ln_group(tiles):
        n = len(tiles)
        for k, (xt, xnt, st, dep, nfree) in enumerate(tiles):
            W("vector", dep)
            op("vector", sV, lambda e, xt=xt, st=st:
               e.reduce_sum(st[:, 0:1], xt[:], AX))
        a = None
        for k, (xt, xnt, st, dep, nfree) in enumerate(tiles):
            W("scalar", dep)
            W("scalar", nfree)
            a = op("scalar", sS, lambda e, xt=xt, xnt=xnt, st=st: e.activation(
                xnt[:], xt[:], ACT.Square, accum_out=st[:, 1:2]))
        W("vector", a)
        for k, (xt, xnt, st, dep, nfree) in enumerate(tiles):
            op("vector", sV, lambda e, st=st: e.tensor_scalar(
                st[:, 5:6], st[:, 1:2], 1.0 / D, EPS, ALU.mult, ALU.add))
        a = None
        for k, (xt, xnt, st, dep, nfree) in enumerate(tiles):
            a = op("vector", sV, lambda e, st=st: e.tensor_scalar(
                st[:, 2:3], st[:, 0:1], 1.0 / D, None, ALU.mult))
        if n < 4:
            W("vector", a)
        for k, (xt, xnt, st, dep, nfree) in enumerate(tiles):
            a = op("vector", sV, lambda e, st=st: e.tensor_mul(
                st[:, 3:4], st[:, 2:3], st[:, 2:3]))
        if n < 4:
            W("vector", a)
        for k, (xt, xnt, st, dep, nfree) in enumerate(tiles):
            a = op("vector", sV, lambda e, st=st: e.scalar_tensor_tensor(
                st[:, 4:5], st[:, 3:4], -1.0, st[:, 5:6], ALU.mult, ALU.add))
        W("scalar", a)
        for k, (xt, xnt, st, dep, nfree) in enumerate(tiles):
            a = op("scalar", sS, lambda e, st=st: e.activation(
                st[:, 6:7], st[:, 4:5], ACT.Sqrt))
        W("vector", a)
        for k, (xt, xnt, st, dep, nfree) in enumerate(tiles):
            a = op("vector", sV, lambda e, st=st: e.reciprocal(
                st[:, 7:8], st[:, 6:7]))
        if n < 4:
            W("vector", a)
        fin = [None] * n
        for k, (xt, xnt, st, dep, nfree) in enumerate(tiles):
            W("vector", nfree)
            fin[k] = op("vector", sV, lambda e, xt=xt, xnt=xnt, st=st:
                        e.scalar_tensor_tensor(
                xnt[:], xt[:], st[:, 2:3],
                st[:, 7:8].to_broadcast([128, D]), ALU.subtract, ALU.mult))
        return fin

    def ln_tile(xt, xnt, stats, dep_x):
        return ln_group([(xt, xnt, stats, dep_x, None)])[0]

    xa_p = Pool(xa)
    for i, sl in enumerate(xa_p.slots):
        sl.sem = xa_sems[i]
        sl.q = "sync" if i % 2 == 0 else "gpsimd"
    xn_p = Pool(xnb + [msk_sb[:, ts(i, 1024)] for i in range(4)])
    for hg in range(2):
        barrier()
        pa = ExitStack()
        tr_ps = Pool([pa.enter_context(nc.psum_tensor(f"tr{hg}_{i}", [128, 128], F32))
                      for i in range(2)])
        kv_ps = Pool([pa.enter_context(nc.psum_tensor(f"kv{hg}_{i}", [128, 512], F32))
                      for i in range(4)])

        d_w = None
        for d in range(8):
            dma("gpsimd", s_wsl, wsl[d][:, 0:512],
                wqkv_t[ts(d, 128), hg * 512:hg * 512 + 512])
            dma("gpsimd", s_wsl, wsl[d][:, 512:1024],
                wqkv_t[ts(d, 128), 1024 + hg * 512:1024 + hg * 512 + 512])
            d_w = dma("gpsimd", s_wsl, wsl[d][:, 1024:1536],
                      wqkv_t[ts(d, 128), 2048 + hg * 512:2048 + hg * 512 + 512])
        br_ps = Pool([pa.enter_context(nc.psum_tensor(f"br{hg}_{i}", [1, 512],
                                                      F32)) for i in range(2)])
        W("tensor", d_w)
        W("tensor", r_sabf)
        r_brow = None
        t_brow = None
        for part in range(3):
            bp = br_ps.next()
            W("tensor", bp.free)
            for d in range(8):
                t_brow = op("tensor", sT,
                            lambda e, d=d, part=part, bp=bp.res:
                            e.matmul(bp[:], sa_bf[:, d:d + 1],
                                     wsl[d][:, ts(part, 512)],
                                     start=(d == 0), stop=(d == 7)))
            bp.ready = t_brow
            W("vector", bp.ready)
            dstb = [hg * 512, 1024 + hg * 512, 2048 + hg * 512][part]
            if part == 0:
                ev = op("vector", sV, lambda e, bp=bp.res, dstb=dstb:
                        e.tensor_scalar(brow[0:1, dstb:dstb + 512],
                                        bp[0:1, :], 0.125, None, ALU.mult))
            else:
                ev = op("vector", sV, lambda e, bp=bp.res, dstb=dstb:
                        e.tensor_copy(brow[0:1, dstb:dstb + 512], bp[0:1, :]))
            bp.free = ev
            r_brow = ev
        W("vector", t_brow)
        W("vector", d_w)
        W("vector", r_cap1)
        for d in range(8):
            op("vector", sV, lambda e, d=d: e.tensor_mul(
                wsl[d][:], wsl[d][:],
                cap1_fm[:, d:d + 1].to_broadcast([128, 1536])))
            r_wsl = op("vector", sV, lambda e, d=d: e.tensor_scalar(
                wsl[d][:, 0:512], wsl[d][:, 0:512], 0.125, None, ALU.mult))
        for kc in range(32):
            r_vones = op("vector", sV, lambda e, kc=kc: e.memset(
                bass.AP(V_bf, 520 * kc + 64, [[32 * 520, 128], [65, 8], [1, 1]]),
                1.0))

        xfm_ready = [None, None]
        xfm_free = [None, None]

        def emit_kv(s2, qpass, grp):
            W("tensor", xfm_ready[s2])
            r_cons = None
            if not qpass:
                tg = grp
                for p in range(4):
                    kp = kv_ps.next()
                    W("tensor", kp.free)
                    for d in range(8):
                        opq("tensor", lambda e, kp=kp.res, d=d, p=p, s2=s2:
                            e.matmul(kp[:], wsl[d][:, 512 + p * 128:512 + p * 128 + 128],
                                     xfm[s2][d][:], start=(d == 0), stop=False))
                    kp.ready = op("tensor", sT, lambda e, kp=kp.res, p=p, hg=hg:
                                  e.matmul(kp[:],
                                           brow[0:1, 1024 + hg * 512 + p * 128:
                                                1024 + hg * 512 + p * 128 + 128],
                                           ones_bf[0:1, :], start=False, stop=True))
                    W("vector", kp.ready)
                    kp.free = op("vector", sV, lambda e, kp=kp.res, p=p, tg=tg:
                                 e.tensor_copy(K_bf[p][:, ts(tg, 512)], kp[:]))
                    r_cons = kp.free
                for kk in range(4):
                    kc = tg * 4 + kk
                    vp = kv_ps.next()
                    W("tensor", vp.free)
                    for d in range(8):
                        opq("tensor", lambda e, vp=vp.res, d=d, kk=kk, s2=s2:
                            e.matmul(vp[:], xfm[s2][d][:, ts(kk, 128)],
                                     wsl[d][:, 1024:1536], start=(d == 0), stop=False))
                    vp.ready = op("tensor", sT, lambda e, vp=vp.res, hg=hg:
                                  e.matmul(vp[:], ones_bf[0:1, 0:128],
                                           brow[0:1, 2048 + hg * 512:
                                                2048 + hg * 512 + 512],
                                           start=False, stop=True))
                    W("vector", vp.ready)
                    vp.free = op("vector", sV, lambda e, vp=vp.res, kc=kc:
                                 e.tensor_copy(
                        bass.AP(V_bf, 520 * kc, [[32 * 520, 128], [65, 8], [1, 64]]),
                        vp[:]))
                    r_cons = vp.free
            else:
                jo = grp - 8
                for p in range(4):
                    qp = kv_ps.next()
                    W("tensor", qp.free)
                    for d in range(8):
                        opq("tensor", lambda e, qp=qp.res, d=d, p=p, s2=s2:
                            e.matmul(qp[:], wsl[d][:, p * 128:p * 128 + 128],
                                     xfm[s2][d][:], start=(d == 0), stop=False))
                    qp.ready = op("tensor", sT, lambda e, qp=qp.res, p=p, hg=hg:
                                  e.matmul(qp[:],
                                           brow[0:1, hg * 512 + p * 128:
                                                hg * 512 + p * 128 + 128],
                                           ones_bf[0:1, :], start=False, stop=True))
                    W("vector", qp.ready)
                    qp.free = op("vector", sV, lambda e, qp=qp.res, p=p, jo=jo:
                                 e.tensor_copy(Q_bf[p][:, ts(jo, 512)], qp[:]))
                    r_cons = qp.free
            xfm_free[s2] = r_cons

        pend = None
        for grp in range(12):
            qpass = grp >= 8
            s2 = grp % 2
            group, xss, nss = [], [], []
            for i in range(4):
                it = grp * 4 + i
                src = (x_own[ts(it - 32, 128), :] if qpass
                       else x_full[ts(it, 128), :])
                xs = xa_p.next()
                W(xs.q, xs.free)
                xs.ready = dma(xs.q, xs.sem, xs.res[:], src)
                ns = xn_p.next()
                xss.append(xs)
                nss.append(ns)
                group.append((xs.res, ns.res, st8[:, 8 * i:8 * i + 8],
                              xs.ready, ns.free))
            fins = ln_group(group)
            for i in range(4):
                xss[i].free = fins[i]
            if grp == 0:
                W("tensor", r_wsl)
                W("tensor", r_vones)
                W("tensor", r_brow)
                W("tensor", r_ident)
            W("tensor", xfm_free[s2])
            for i in range(4):
                ns = nss[i]
                W("tensor", fins[i])
                for d in range(8):
                    tp = tr_ps.next()
                    W("tensor", tp.free)
                    tp.ready = op("tensor", sT,
                                  lambda e, tp=tp.res, x=ns.res, d=d:
                                  e.matmul(tp[:], x[:, ts(d, 128)], ident[:],
                                           start=True, stop=True))
                    W("scalar", tp.ready)
                    tp.free = op("scalar", sS,
                                 lambda e, tp=tp.res, d=d, s2=s2, i=i:
                                 e.copy(xfm[s2][d][:, ts(i, 128)], tp[:]))
                    r_evac = tp.free
                ns.free = r_evac
            xfm_ready[s2] = r_evac
            if pend is not None:
                emit_kv(*pend)
            pend = (s2, qpass, grp)
        emit_kv(*pend)
        ctx.flush()
        pa.close()
        barrier()

        pa2 = ExitStack()
        sc_ps = Pool([pa2.enter_context(nc.psum_tensor(f"sc{hg}_{i}", [128, 1024], F32))
                      for i in range(3)])
        av_ps = Pool([pa2.enter_context(nc.psum_tensor(f"av{hg}_{i}", [65, 512], F32))
                      for i in range(2)])
        P_p = Pool(P_sb)
        msk_slot = Slot(msk_sb)

        for j in range(NSLOT):
            W("sync", msk_slot.free)
            msk_slot.ready = dma("sync", s_msk, msk_sb[:], masks[j, :, :])
            r_mask_last = None
            npair = 2 * EXT[j]
            for hp in range(4):
                hls = (2 * hp, 2 * hp + 1)
                avs, scs, pbs = {}, {}, {}
                for hl in hls:
                    avs[hl] = av_ps.next()
                def emit_scs(pc):
                    kc0, kc1 = 2 * pc, 2 * pc + 1
                    out = {}
                    for hl in hls:
                        p, par = hl // 2, hl % 2
                        pr = slice(64 * par, 64 * par + 64)
                        sc = sc_ps.next()
                        out[hl] = sc
                        W("tensor", sc.free)
                        opq("tensor",
                            lambda e, sc=sc.res, p=p, pr=pr, kc0=kc0, j=j:
                            e.matmul(sc[:, 0:512], K_bf[p][pr, ts(kc0, 128)],
                                     Q_bf[p][pr, ts(j, 512)],
                                     start=True, stop=True))
                        sc.ready = op("tensor", sT,
                                      lambda e, sc=sc.res, p=p, pr=pr,
                                      kc1=kc1, j=j:
                                      e.matmul(sc[:, 512:1024],
                                               K_bf[p][pr, ts(kc1, 128)],
                                               Q_bf[p][pr, ts(j, 512)],
                                               start=True, stop=True))
                    return out

                scs_next = emit_scs(0)
                for pc in range(npair):
                    kc0, kc1 = 2 * pc, 2 * pc + 1
                    scs = scs_next
                    for hl in hls:
                        sc = scs[hl]
                        pb = P_p.next()
                        pbs[hl] = pb
                        W("scalar", sc.ready)
                        W("scalar", pb.free)
                        pb.ready = op("scalar", sS,
                                      lambda e, sc=sc.res, pb=pb.res:
                                      e.activation(pb[:], sc[:], ACT.Exp,
                                                   bias=neg20[:, 0:1]))
                        sc.free = pb.ready
                    scs_next = emit_scs(pc + 1) if pc + 1 < npair else None
                    masked = pc >= npair - 4
                    if masked:
                        mc = pc - (npair - 4)
                        for hl in hls:
                            pb = pbs[hl]
                            W("vector", pb.ready)
                            if hp == 0 and hl == hls[0] and mc == 0:
                                W("vector", msk_slot.ready)
                            pb.ready = op("vector", sV,
                                          lambda e, pb=pb.res, mc=mc:
                                          e.tensor_mul(pb[:], pb[:],
                                                       msk_sb[:, ts(mc, 1024)]))
                            r_mask_last = pb.ready
                    for hl in hls:
                        av, pb = avs[hl], pbs[hl]
                        if pc == 0:
                            W("tensor", av.free)
                        W("tensor", pb.ready)
                        opq("tensor",
                            lambda e, av=av.res, hl=hl, kc0=kc0, pb=pb.res,
                            st_=(pc == 0):
                            e.matmul(av[:], bass.AP(V_bf, 520 * kc0 + 65 * hl,
                                                    [[32 * 520, 128], [1, 65]]),
                                     pb[:, 0:512], start=st_, stop=False))
                        pb.free = op("tensor", sT,
                                     lambda e, av=av.res, hl=hl, kc1=kc1,
                                     pb=pb.res, sp=(pc == npair - 1):
                                     e.matmul(av[:],
                                              bass.AP(V_bf, 520 * kc1 + 65 * hl,
                                                      [[32 * 520, 128], [1, 65]]),
                                              pb[:, 512:1024],
                                              start=False, stop=sp))
                        av.ready = pb.free
                avb = [avb_st[0], avb_st[1]]
                for q, hl in enumerate(hls):
                    av = avs[hl]
                    W("vector", av.ready)
                    op("vector", sV, lambda e, av=av.res, b=avb[q]:
                       e.tensor_copy(b[:], av[0:64, :]))
                    r_st = op("vector", sV, lambda e, av=av.res: _recl(
                        e, rec1[0:1, :], av[64:65, :]))
                    av.free = r_st
                    W("gpsimd", r_st)
                    d_bc = dma("gpsimd", s_bc, bc2[q][:],
                               bass.AP(rec1, 0, [[1, 1], [0, 64], [1, 512]]))
                    W("vector", d_bc)
                    hd = (hg * 8 + hl) // 2
                    dst = slice(64 * ((hg * 8 + hl) % 2),
                                64 * ((hg * 8 + hl) % 2) + 64)
                    op("vector", sV,
                       lambda e, b=avb[q], hd=hd, dst=dst, j=j, q=q:
                       e.tensor_mul(attnout[hd][dst, ts(j, 512)],
                                    b[:], bc2[q][:]))
            msk_slot.free = r_mask_last
        ctx.flush()
        pa2.close()

    pb5 = ExitStack()
    tr5 = Pool([pb5.enter_context(nc.psum_tensor(f"t5{i}", [128, 128], F32))
                for i in range(2)])
    pr5 = Pool([pb5.enter_context(nc.psum_tensor(f"p5{i}", [128, 512], F32))
                for i in range(2)])
    xhl5_p = Pool([msk_sb[:, 0:2 * D], msk_sb[:, 2 * D:4 * D]])
    wout_bf = [wsl[d][:, 0:1024] for d in range(8)]
    barrier()
    d_wo = None
    for d in range(8):
        d_wo = dma("gpsimd", s_wsl, wout_bf[d], wout_t[ts(d, 128), :])

    for jo in range(NSLOT):
        r_xo = None
        for i in range(4):
            it = jo * 4 + i
            xs = xa_p.next()
            W(xs.q, xs.free)
            xs.ready = dma(xs.q, xs.sem, xs.res[:], x_own[ts(it, 128), :])
            W("vector", xs.ready)
            hb = xhl5_p.next()
            W("vector", hb.free)
            op("vector", sV, lambda e, x=xs.res, h=hb.res:
               e.tensor_copy(h[:, 0:D], x[:]))
            r_hl = op("vector", sV, lambda e, x=xs.res, h=hb.res:
                      e.tensor_sub(h[:, D:2 * D], x[:], h[:, 0:D]))
            xs.free = r_hl
            W("tensor", r_hl)
            if it == 0:
                W("tensor", d_wo)
            for d in range(8):
                tp = tr5.next()
                W("tensor", tp.free)
                opq("tensor", lambda e, tp=tp.res, d=d, h=hb.res: e.matmul(
                    tp[:], h[:, ts(d, 128)], ident[:], start=True, stop=False))
                tp.ready = op("tensor", sT, lambda e, tp=tp.res, d=d, h=hb.res:
                              e.matmul(tp[:], h[:, D + d * 128:D + d * 128 + 128],
                                       ident[:], start=False, stop=True))
                hb.free = tp.ready
                W("scalar", tp.ready)
                tp.free = op("scalar", sS, lambda e, tp=tp.res, d=d, i=i: e.copy(
                    xofm[d // 2][:, (d % 2) * 512 + i * 128:
                                 (d % 2) * 512 + i * 128 + 128], tp[:]))
                r_xo = tp.free
        W("tensor", r_xo)
        for c in range(8):
            pp = pr5.next()
            W("tensor", pp.free)
            for d in range(8):
                pp.ready = op("tensor", sT, lambda e, pp=pp.res, c=c, d=d, jo=jo:
                              e.matmul(pp[:], wout_bf[d][:, ts(c, 128)],
                                       attnout[d][:, ts(jo, 512)],
                                       start=(d == 0), stop=(d == 7)))
            W("vector", pp.ready)
            pp.free = op("vector", sV, lambda e, pp=pp.res, c=c, jo=jo:
                         e.scalar_tensor_tensor(
                x_res[c][:, ts(jo, 512)], pp[:], ga_fm[:, c:c + 1],
                xofm[c // 2][:, (c % 2) * 512:(c % 2) * 512 + 512],
                ALU.mult, ALU.add))
    ctx.flush()
    pb5.close()
    sa_stk.close()

    b6 = ExitStack()
    b6sb = lambda n, s, d=F32: b6.enter_context(nc.sbuf_tensor(n, s, d))
    x2tm = [b6sb(f"x2_{i}", [128, D]) for i in range(2)]
    h2tm = [b6sb(f"h2_{i}", [128, D], BF16) for i in range(2)]
    hi6 = [b6sb(f"hi6_{p}", [128, 128], BF16) for p in range(2)]
    lo6 = [b6sb(f"lo6_{p}", [128, 128], BF16) for p in range(2)]
    st6 = b6sb("st6", [128, 16])
    rw_bf = b6sb("rw_bf", [128, 64], BF16)
    sm8 = b6sb("sm8", [128, 192])
    tr6 = Pool([b6.enter_context(nc.psum_tensor(f"t6{i}", [128, 128], F32))
                for i in range(2)])
    lg6 = Pool([b6.enter_context(nc.psum_tensor(f"l6{i}", [128, 8], F32))
                for i in range(4)])

    barrier()
    d_rw = dma("gpsimd", s_gp0, rw_bf[:, 0:64], rw[:, :])

    x2_p = Pool(x2tm)
    h2_p = Pool(h2tm)
    r_h2last = None
    for it in range(16):
        jo, i = divmod(it, 4)
        x2 = x2_p.next()
        h2s = h2_p.next()
        r_tx2 = [None, None]
        for dd in (0, 2, 4, 6):
            r_hl = [None, None]
            for p in (0, 1):
                d = dd + p
                W("vector", r_tx2[p])
                op("vector", sV, lambda e, d=d, jo=jo, i=i, p=p: e.tensor_copy(
                    hi6[p][:],
                    x_res[d][:, jo * 512 + i * 128:jo * 512 + i * 128 + 128]))
            for p in (0, 1):
                d = dd + p
                r_hl[p] = op("vector", sV, lambda e, d=d, jo=jo, i=i, p=p:
                             e.tensor_sub(
                    lo6[p][:],
                    x_res[d][:, jo * 512 + i * 128:jo * 512 + i * 128 + 128],
                    hi6[p][:]))
            for p in (0, 1):
                d = dd + p
                tp = tr6.next()
                W("tensor", r_hl[p])
                W("tensor", tp.free)
                W("tensor", x2.free)
                opq("tensor", lambda e, tp=tp.res, p=p: e.matmul(
                    tp[:], hi6[p][:], ident[:], start=True, stop=False))
                tp.ready = op("tensor", sT, lambda e, tp=tp.res, p=p: e.matmul(
                    tp[:], lo6[p][:], ident[:], start=False, stop=True))
                r_tx2[p] = tp.ready
                W("scalar", tp.ready)
                tp.free = op("scalar", sS, lambda e, tp=tp.res, d=d, x2=x2.res:
                             e.copy(x2[:, ts(d, 128)], tp[:]))
                r_x2 = tp.free
        x2.ready = r_x2
        W("vector", h2s.free)
        r_ln = ln_tile(x2.res, h2s.res, st6, x2.ready)
        op("vector", sV, lambda e, h=h2s.res: e.tensor_mul(h[:], h[:], cbp1_tm[:]))
        r_h2 = op("vector", sV, lambda e, h=h2s.res:
                  e.tensor_add(h[:], h[:], sbm_tm[:]))
        x2.free = r_h2
        W("tensor", r_h2)
        for d in range(8):
            tp = tr6.next()
            W("tensor", tp.free)
            tp.ready = op("tensor", sT, lambda e, tp=tp.res, d=d, h=h2s.res:
                          e.matmul(tp[:], h[:, ts(d, 128)], ident[:],
                                   start=True, stop=True))
            W("scalar", tp.ready)
            tp.free = op("scalar", sS, lambda e, tp=tp.res, d=d, it=it:
                         e.copy(h2_fm[d][:, ts(it, 128)], tp[:]))
            r_h2last = tp.free
        h2s.free = r_h2last

    W("tensor", r_h2last)
    W("tensor", d_rw)
    for tkp in range(4):
        tks = tuple(4 * tkp + q for q in range(4))
        sm = [sm8[:, ts(q, 48)] for q in range(4)]
        lgs = []
        for tk in tks:
            lg = lg6.next()
            W("tensor", lg.free)
            for d in range(8):
                lg.ready = op("tensor", sT, lambda e, lg=lg.res, d=d, tk=tk:
                              e.matmul(lg[:], h2_fm[d][:, ts(tk, 128)],
                                       rw_bf[:, ts(d, 8)],
                                       start=(d == 0), stop=(d == 7)))
            lgs.append(lg)
        for p, lg in enumerate(lgs):
            W("vector", lg.ready)
            a = op("vector", sV, lambda e, lg=lg.res, s=sm[p]:
                   e.reduce_max(s[:, 0:1], lg[:], AX, negate=True))
            W("scalar", a)
            a = op("scalar", sS, lambda e, lg=lg.res, s=sm[p]: e.activation(
                s[:, 8:16], lg[:], ACT.Exp, bias=s[:, 0:1]))
            lg.free = a
            W("vector", a)

        def pair(fn):
            r = None
            for p in range(4):
                r = op("vector", sV, lambda e, s=sm[p], p=p: fn(e, s, p))
            W("vector", r)
            return r
        pair(lambda e, s, p: e.reduce_sum(s[:, 1:2], s[:, 8:16], AX))
        pair(lambda e, s, p: e.reciprocal(s[:, 2:3], s[:, 1:2]))
        pair(lambda e, s, p: e.tensor_mul(
            s[:, 8:16], s[:, 8:16], s[:, 2:3].to_broadcast([128, 8])))
        pair(lambda e, s, p: e.tensor_add(
            s[:, 16:24], s[:, 8:16], rbcol[:, 0:1].to_broadcast([128, 8])))
        pair(lambda e, s, p: e.max(out=s[:, 24:32], in_=s[:, 16:24]))
        pair(lambda e, s, p: e.tensor_tensor(
            s[:, 32:40], s[:, 16:24],
            s[:, 27:28].to_broadcast([128, 8]), ALU.is_ge))
        pair(lambda e, s, p: e.tensor_mul(
            s[:, 32:40], s[:, 32:40], s[:, 8:16]))
        pair(lambda e, s, p: e.reduce_sum(s[:, 3:4], s[:, 32:40], AX))
        pair(lambda e, s, p: e.tensor_scalar(
            s[:, 3:4], s[:, 3:4], 1e-20, None, ALU.add))
        pair(lambda e, s, p: e.reciprocal(s[:, 4:5], s[:, 3:4]))
        pair(lambda e, s, p, _tks=tks: e.tensor_mul(
            wsel_bf[:, ts(_tks[p], 8)], s[:, 32:40],
            s[:, 4:5].to_broadcast([128, 8])))
    ctx.flush()
    b6.close()

    b7 = ExitStack()
    b7sb = lambda n, s, d=BF16: b7.enter_context(nc.sbuf_tensor(n, s, d))
    gu_bf = [b7sb(f"gu{i}", [128, 8192]) for i in range(2)]
    dw_bf = [b7sb(f"dwb{i}", [128, 4096]) for i in range(2)]
    act_e = [b7sb(f"acte{s}", [128, NTOK]) for s in range(4)]
    rep7 = b7sb("rep7", [128, 128])
    wsb7 = [b7sb(f"wsb{i}", [128, 512]) for i in range(2)]
    sg7 = [b7sb(f"sg7{i}", [128, 512]) for i in range(2)]
    gu_ps = Pool([b7.enter_context(nc.psum_tensor(f"g7{i}", [128, 512], F32))
                  for i in range(4)])
    wb_ps = Pool([b7.enter_context(nc.psum_tensor(f"w7{i}", [128, 512], F32))
                  for i in range(2)])
    d8_ps = Pool([b7.enter_context(nc.psum_tensor(f"d8{i}", [128, 512], F32))
                  for i in range(2)])
    gu_p = Pool(gu_bf)
    for i, sl in enumerate(gu_p.slots):
        sl.sem = gu_sems[i]
    dw_p = Pool(dw_bf)
    for i, sl in enumerate(dw_p.slots):
        sl.sem = dw_sems[i]
    wsb_p = Pool(wsb7)
    sg_p = Pool(sg7)
    act_slots = [Slot(a) for a in act_e]

    barrier()
    for e_i in range(NEXP):
        gu = gu_p.next()
        W("gpsimd", gu.free)
        for d in range(8):
            dma("gpsimd", gu.sem, gu.res[:, ts(d, 512)], gwt[e_i, ts(d, 128), :])
        for d in range(8):
            gu.ready = dma("gpsimd", gu.sem,
                           gu.res[:, 4096 + d * 512:4096 + d * 512 + 512],
                           uwt[e_i, ts(d, 128), :])
        dwe = dw_p.next()
        W("gpsimd", dwe.free)
        for s in range(4):
            dwe.ready = dma("gpsimd", dwe.sem, dwe.res[:, ts(s, 1024)],
                            dwt[e_i, ts(s, 128), :])
        r_gu_last = None
        for j in range(NSLOT):
            wj = wsb_p.next()
            wp = wb_ps.next()
            W("tensor", wp.free)
            r_rep = None
            for q4 in range(4):
                tk = j * 4 + q4
                W("vector", r_rep)
                r_rep0 = op("vector", sV, lambda e, tk=tk, e_i=e_i: e.tensor_copy(
                    rep7[:], wsel_bf[:, tk * 8 + e_i:tk * 8 + e_i + 1]
                    .to_broadcast([128, 128])))
                W("tensor", r_rep0)
                r_rep = op("tensor", sT, lambda e, wp=wp.res, q4=q4: e.matmul(
                    wp[:, ts(q4, 128)], rep7[:], ident[:], start=True, stop=True))
            wp.ready = r_rep
            W("vector", wp.ready)
            W("vector", wj.free)
            wj.ready = op("vector", sV, lambda e, wj=wj.res, wp=wp.res:
                          e.tensor_copy(wj[:], wp[:]))
            wp.free = wj.ready
            if j == 0:
                W("tensor", gu.ready)
            for s in range(4):
                pg = gu_ps.next()
                pu = gu_ps.next()
                ac = act_slots[s]
                W("tensor", pg.free)
                W("tensor", pu.free)
                for d in range(8):
                    opq("tensor", lambda e, pg=pg.res, gu=gu.res, d=d, s=s, j=j:
                        e.matmul(pg[:], gu[:, d * 512 + s * 128:d * 512 + s * 128 + 128],
                                 h2_fm[d][:, ts(j, 512)], start=(d == 0), stop=(d == 7)))
                for d in range(8):
                    r_gu = op("tensor", sT,
                              lambda e, pu=pu.res, gu=gu.res, d=d, s=s, j=j:
                              e.matmul(pu[:],
                                       gu[:, 4096 + d * 512 + s * 128:
                                          4096 + d * 512 + s * 128 + 128],
                                       h2_fm[d][:, ts(j, 512)],
                                       start=(d == 0), stop=(d == 7)))
                r_gu_last = r_gu
                sg = sg_p.next()
                W("scalar", r_gu)
                W("scalar", sg.free)
                sg.ready = op("scalar", sS, lambda e, pg=pg.res, sg=sg.res:
                              e.activation(sg[:], pg[:], ACT.Silu))
                pg.free = sg.ready
                W("vector", sg.ready)
                W("vector", wj.ready)
                W("vector", ac.free)
                op("vector", sV, lambda e, pu=pu.res, sg=sg.res, ac=ac.res, s=s, j=j:
                   e.tensor_mul(ac[:, ts(j, 512)], sg[:], pu[:]))
                r_act = op("vector", sV, lambda e, ac=ac.res, wj=wj.res, s=s, j=j:
                           e.tensor_mul(ac[:, ts(j, 512)], ac[:, ts(j, 512)], wj[:]))
                pu.free = r_act
                sg.free = r_act
                ac.ready = r_act
            wj.free = r_act
        gu.free = r_gu_last
        r_dn = None
        for j in range(NSLOT):
            pd = d8_ps.next()
            W("tensor", pd.free)
            if j == 0:
                W("tensor", dwe.ready)
            for s in range(4):
                W("tensor", act_slots[s].ready)
            for c in range(8):
                if c > 0:
                    pd = d8_ps.next()
                    W("tensor", pd.free)
                for s in range(4):
                    pd.ready = op("tensor", sT,
                                  lambda e, pd=pd.res, dw=dwe.res, s=s, c=c, j=j,
                                  st_=(s == 0), sp=(s == 3):
                                  e.matmul(pd[:], dw[:, s * 1024 + c * 128:
                                                    s * 1024 + c * 128 + 128],
                                           act_e[s][:, ts(j, 512)],
                                           start=st_, stop=sp))
                W("vector", pd.ready)
                pd.free = op("vector", sV, lambda e, pd=pd.res, c=c, j=j:
                             e.scalar_tensor_tensor(
                    x_res[c][:, ts(j, 512)], pd[:], gb_fm[:, c:c + 1],
                    x_res[c][:, ts(j, 512)], ALU.mult, ALU.add))
                r_dn = pd.free
        dwe.free = r_dn
        for s in range(4):
            act_slots[s].free = r_dn
    ctx.flush()
    b7.close()

    b9 = ExitStack()
    b9sb = lambda n, s, d=BF16: b9.enter_context(nc.sbuf_tensor(n, s, d))
    sgu_bf = [b9sb(f"sgu{d}", [128, 2048]) for d in range(8)]
    s_act = [b9sb(f"sac{i}", [128, NTOK]) for i in range(8)]
    sdw_bf = [b9sb(f"sdwb{i}", [128, 1024]) for i in range(2)]
    sg9 = [b9sb(f"sg9{i}", [128, 512]) for i in range(2)]
    gu9 = Pool([b9.enter_context(nc.psum_tensor(f"g9{i}", [128, 512], F32))
                for i in range(4)])
    d9_ps = Pool([b9.enter_context(nc.psum_tensor(f"d9{i}", [128, 512], F32))
                  for i in range(2)])
    sdw_p = Pool(sdw_bf)
    for i, sl in enumerate(sdw_p.slots):
        sl.sem = sd_sems[i]
    sg9_p = Pool(sg9)

    prev_g9 = None
    prev_dn9 = None
    barrier()
    for ih in range(2):
        d_sg = None
        if ih == 1:
            W("gpsimd", prev_g9)
        for d in range(8):
            dma("gpsimd", s_sgu, sgu_bf[d][:, 0:1024],
                sgw[ts(d, 128), ih * 1024:ih * 1024 + 1024])
            d_sg = dma("gpsimd", s_sgu, sgu_bf[d][:, 1024:2048],
                       suw[ts(d, 128), ih * 1024:ih * 1024 + 1024])
        if ih == 1:
            W("vector", prev_dn9)
        s_act_ready = [None] * 8
        r_act9 = None
        for ic in range(8):
            for j in range(NSLOT):
                pg = gu9.next()
                pu = gu9.next()
                W("tensor", pg.free)
                W("tensor", pu.free)
                if ic == 0 and j == 0:
                    W("tensor", d_sg)
                for d in range(8):
                    opq("tensor", lambda e, pg=pg.res, d=d, ic=ic, j=j:
                        e.matmul(pg[:], sgu_bf[d][:, ts(ic, 128)],
                                 h2_fm[d][:, ts(j, 512)], start=(d == 0), stop=(d == 7)))
                for d in range(8):
                    r_g9 = op("tensor", sT,
                              lambda e, pu=pu.res, d=d, ic=ic, j=j:
                              e.matmul(pu[:], sgu_bf[d][:, 1024 + ic * 128:
                                                        1024 + ic * 128 + 128],
                                       h2_fm[d][:, ts(j, 512)],
                                       start=(d == 0), stop=(d == 7)))
                sg = sg9_p.next()
                W("scalar", r_g9)
                W("scalar", sg.free)
                sg.ready = op("scalar", sS, lambda e, pg=pg.res, sg=sg.res:
                              e.activation(sg[:], pg[:], ACT.Silu))
                pg.free = sg.ready
                W("vector", sg.ready)
                r_a = op("vector", sV, lambda e, pu=pu.res, sg=sg.res, ic=ic, j=j:
                         e.tensor_mul(s_act[ic][:, ts(j, 512)], sg[:], pu[:]))
                pu.free = r_a
                sg.free = r_a
                r_act9 = r_a
            s_act_ready[ic] = r_act9
        r_dn9 = None
        for c in range(8):
            sd = sdw_p.next()
            W("gpsimd", sd.free)
            sd.ready = dma("gpsimd", sd.sem, bass.AP(sd.res, 0,
                                                 [[1024, 128], [128, 8], [1, 128]]),
                           bass.AP(sdw.tensor, ih * 1024 * 1024 + c * 128,
                                   [[1024, 128], [128 * 1024, 8], [1, 128]]))
            for j in range(NSLOT):
                pd = d9_ps.next()
                W("tensor", pd.free)
                if j == 0:
                    W("tensor", sd.ready)
                    for ic in range(8):
                        W("tensor", s_act_ready[ic])
                for ic in range(8):
                    pd.ready = op("tensor", sT,
                                  lambda e, pd=pd.res, sd=sd.res, ic=ic, j=j,
                                  st_=(ic == 0), sp=(ic == 7):
                                  e.matmul(pd[:], sd[:, ts(ic, 128)],
                                           s_act[ic][:, ts(j, 512)],
                                           start=st_, stop=sp))
                W("vector", pd.ready)
                pd.free = op("vector", sV, lambda e, pd=pd.res, c=c, j=j:
                             e.scalar_tensor_tensor(
                    x_res[c][:, ts(j, 512)], pd[:], gb_fm[:, c:c + 1],
                    x_res[c][:, ts(j, 512)], ALU.mult, ALU.add))
                r_dn9 = pd.free
            sd.free = r_dn9
        prev_g9 = r_g9
        prev_dn9 = r_dn9
    ctx.flush()
    b9.close()

    b10 = ExitStack()
    yhl = [b10.enter_context(nc.sbuf_tensor(f"yhl{i}", [128, 256], BF16))
           for i in range(2)]
    y_sb = [b10.enter_context(nc.sbuf_tensor(f"ysb{i}", [128, D], F32))
            for i in range(2)]
    trA = Pool([b10.enter_context(nc.psum_tensor(f"tA{i}", [128, 128], F32))
                for i in range(2)])
    yhl_p = Pool(yhl)
    ysb_slots = [Slot(t) for t in y_sb]
    for i, sl in enumerate(ysb_slots):
        sl.sem = ys_sems[i]
    barrier()
    for jo in range(NSLOT):
        for i in range(4):
            ysl = ysb_slots[i % 2]
            r_tile = None
            for dd in (0, 2, 4, 6):
                hbs = [yhl_p.next(), yhl_p.next()]
                for p in (0, 1):
                    d = dd + p
                    W("vector", hbs[p].free)
                    op("vector", sV, lambda e, hb=hbs[p].res, d=d, jo=jo, i=i:
                       e.tensor_copy(hb[:, 0:128],
                                     x_res[d][:, jo * 512 + i * 128:
                                              jo * 512 + i * 128 + 128]))
                for p in (0, 1):
                    d = dd + p
                    hbs[p].ready = op("vector", sV,
                                      lambda e, hb=hbs[p].res, d=d, jo=jo, i=i:
                                      e.tensor_sub(
                        hb[:, 128:256],
                        x_res[d][:, jo * 512 + i * 128:jo * 512 + i * 128 + 128],
                        hb[:, 0:128]))
                for p in (0, 1):
                    d = dd + p
                    hb = hbs[p]
                    tp = trA.next()
                    W("tensor", hb.ready)
                    W("tensor", tp.free)
                    if d == 0:
                        W("tensor", ysl.free)
                    opq("tensor", lambda e, tp=tp.res, hb=hb.res: e.matmul(
                        tp[:], hb[:, 0:128], ident[:], start=True, stop=False))
                    tp.ready = op("tensor", sT, lambda e, tp=tp.res, hb=hb.res:
                                  e.matmul(tp[:], hb[:, 128:256], ident[:],
                                           start=False, stop=True))
                    hb.free = tp.ready
                    W("scalar", tp.ready)
                    tp.free = op("scalar", sS,
                                 lambda e, tp=tp.res, d=d, ysl=ysl.res:
                                 e.copy(ysl[:, ts(d, 128)], tp[:]))
                    r_tile = tp.free
            ysl.ready = r_tile
            W("sync", ysl.ready)
            ysl.free = dma("sync", ysl.sem,
                           y[jo * 512 + i * 128: jo * 512 + i * 128 + 128, :],
                           ysl.res[:])
    opq("sync", lambda e, v0=ys_sems[0].n, v1=ys_sems[1].n:
        (e.wait_ge(ys_sems[0].h, v0), e.wait_ge(ys_sems[1].h, v1)))
    ctx.flush()
    b10.close()
    root.close()
    return nc


_PROG = None


def _get_prog():
    global _PROG
    if _PROG is None:
        _PROG = build_program()
    return _PROG


def kernel(L0_attn_x, L0_attn_w_qkv, L0_attn_w_out, L0_ffn_router_w,
           L0_ffn_router_bias, L0_ffn_gate_w, L0_ffn_up_w, L0_ffn_down_w,
           L0_ffn_s_gate_w, L0_ffn_s_up_w, L0_ffn_s_down_w, L0_c, L0_wmod):
    bf = ml_dtypes.bfloat16
    x = np.ascontiguousarray(np.asarray(L0_attn_x, dtype=np.float32))
    wqkv_t = np.ascontiguousarray(np.asarray(L0_attn_w_qkv, np.float32).T)
    wout_t = np.ascontiguousarray(np.asarray(L0_attn_w_out, np.float32).T)
    wmod = np.ascontiguousarray(np.asarray(L0_wmod, np.float32))
    cmat = np.ascontiguousarray(np.asarray(L0_c, np.float32))
    rbv = np.asarray(L0_ffn_router_bias, np.float32).reshape(1, 1)
    rwm = np.asarray(L0_ffn_router_w, np.float32)
    rw_fm = np.ascontiguousarray(
        rwm.reshape(8, 128, 8).transpose(1, 0, 2).reshape(128, 64))
    gwm = np.ascontiguousarray(np.asarray(L0_ffn_gate_w, np.float32))
    uwm = np.ascontiguousarray(np.asarray(L0_ffn_up_w, np.float32))
    dwm = np.ascontiguousarray(np.asarray(L0_ffn_down_w, np.float32))
    sgwm = np.ascontiguousarray(np.asarray(L0_ffn_s_gate_w, np.float32))
    suwm = np.ascontiguousarray(np.asarray(L0_ffn_s_up_w, np.float32))
    sdwm = np.ascontiguousarray(np.asarray(L0_ffn_s_down_w, np.float32))

    mask_h = {}
    for half in (0, 1):
        m = np.zeros((NSLOT, 128, 4096), dtype=bf)
        for sl in range(NSLOT):
            gq = OWN_TGS[half][sl]
            for mc in range(8):
                kc = 4 * (EXT[sl] - 2) + mc
                keys = kc * 128 + np.arange(128)[:, None]
                qs = gq * 512 + np.arange(512)[None, :]
                m[sl, :, mc * 512:(mc + 1) * 512] = (keys <= qs).astype(bf)
        mask_h[half] = m

    nc = _get_prog()
    in_maps = []
    for core in range(8):
        b, half = core // 2, core % 2
        tgs = OWN_TGS[half]
        x_b = np.ascontiguousarray(x[b])
        x_own = np.ascontiguousarray(
            np.concatenate([x[b, ts(t, 512), :] for t in tgs], axis=0))
        in_maps.append({
            "x_full": x_b, "x_own": x_own, "wqkv_t": wqkv_t, "wout_t": wout_t,
            "wmod": wmod,
            "c_fm_in": np.ascontiguousarray(cmat[b].reshape(8, 128).T),
            "rb": rbv, "rw": rw_fm, "gwt": gwm, "uwt": uwm, "dwt": dwm,
            "sgw": sgwm, "suw": suwm, "sdw": sdwm, "masks": mask_h[half],
        })
    res = run_bass_kernel_spmd(nc, in_maps, core_ids=list(range(8)))
    out = np.empty((B, T, D), dtype=np.float32)
    for core in range(8):
        b, half = core // 2, core % 2
        yv = res.results[core]["y"]
        for sl, tg in enumerate(OWN_TGS[half]):
            out[b, ts(tg, 512), :] = yv[ts(sl, 512), :]
    return out
